# version 15
# baseline (speedup 1.0000x reference)
import numpy as np
import contextlib
import ml_dtypes
import concourse.bass as bass
import concourse.mybir as mybir
from concourse.bass_utils import run_bass_kernel_spmd

F32 = mybir.dt.float32
BF16 = mybir.dt.bfloat16
I32 = mybir.dt.int32
OP = mybir.AluOpType
AX = mybir.AxisListType
ACT = mybir.ActivationFunctionType

CMAGIC = float(np.float32(1.5 * 2 ** 23))

S = 4096
D = 1024
TOWN = 2048
NPAIR = 4
NKC = S // 128
NQT = S // 512
FFI = 3072
NIC = FFI // 128

SB_BASE = 16512
OFF_KT = SB_BASE
OFF_QT = SB_BASE + 32768
OFF_VA = SB_BASE + 65536
OFF_XNFM = SB_BASE + 98816
OFF_XQ = OFF_XNFM
OFF_CTXOWN = SB_BASE + 131584
OFF_FIX = SB_BASE + 164352
OFF_STREAM = SB_BASE + 186368


class Bag:
    pass


class Sched:
    def __init__(self):
        self.steps = []
        self.cnt = {}

    def on(self, eng, fn):
        self.steps.append((eng, fn))

    def inc(self, name, k=1):
        self.cnt[name] = self.cnt.get(name, 0) + k
        return self.cnt[name]

    def val(self, name):
        return self.cnt.get(name, 0)


def make_consts():
    masks = np.zeros((128, 4 * 512), dtype=ml_dtypes.bfloat16)
    p = np.arange(128)[:, None]
    q = np.arange(512)[None, :]
    for dv in range(4):
        masks[:, 512 * dv:512 * (dv + 1)] = (q >= 128 * dv + p).astype(ml_dtypes.bfloat16)
    ident = np.eye(128, dtype=np.float32)
    ones64 = np.zeros((128, 2), dtype=ml_dtypes.bfloat16)
    ones64[0:64, 0] = 1.0
    ones64[64:128, 1] = 1.0
    ind32 = np.zeros((32, 1024), dtype=np.float32)
    for cc in range(8):
        for pp in range(128):
            ind32[4 * cc + pp // 32, 128 * cc + pp] = 1.0
    ind96 = np.zeros((96, 3072), dtype=np.float32)
    for ic in range(24):
        for pp in range(128):
            ind96[4 * ic + pp // 32, 128 * ic + pp] = 1.0
    onesf = np.ones((128, 128), dtype=np.float32)
    return masks, ident, ones64, ind32, ind96, onesf


def build(debug=(), phases_upto=99):
    nc = bass.Bass("TRN2", target_bir_lowering=False)
    t = Bag()
    t.debug = set(debug)

    din = {}
    def di(name, shape, dtype=F32):
        din[name] = nc.dram_tensor(name, list(shape), dtype, kind="ExternalInput").ap()
    di("x_seq", (S, D))
    di("x_own", (TOWN, D))
    di("wkT", (D, 512)); di("wqT", (D, 512)); di("wvT", (D, 512))
    di("wprojT", (D, D))
    di("b_row", (1, D))
    di("wupN", (FFI, D)); di("wupT", (D, FFI))
    di("wgateN", (FFI, D)); di("wgateT", (D, FFI))
    di("wdownN", (D, FFI)); di("wdownT", (FFI, D))
    di("masks", (128, 2048), BF16)
    di("ident", (128, 128)); di("ones64", (128, 2), BF16); di("ind32", (32, 1024)); di("ind96", (96, 3072)); di("onesf", (128, 128))
    di("offs", (1, 8), I32)
    t.din = din
    t.out_own = nc.dram_tensor("out_own", [TOWN, D], F32, kind="ExternalOutput").ap()

    t.stash_up = nc.dram_tensor("stash_up", [NIC, 128, TOWN], F32, kind="Internal")
    t.stash_gate = nc.dram_tensor("stash_gate", [NIC, 128, TOWN], F32, kind="Internal")
    t.stash_g2 = nc.dram_tensor("stash_g2", [NIC, 128, TOWN], F32, kind="Internal")
    t.stash_o = nc.dram_tensor("stash_o", [NIC, 128, TOWN], F32, kind="Internal")
    t.stash_wdq = nc.dram_tensor("stash_wdq", [NIC, 128, D], BF16, kind="Internal")
    t.ar_in = [nc.dram_tensor(f"ar_in{i}", [1, 8], F32, kind="Internal") for i in range(4)]
    t.ar_out = [nc.dram_tensor(f"ar_out{i}", [1, 8], F32, kind="Internal") for i in range(4)]
    t.lnc_hbm = nc.dram_tensor("lnc_hbm", [8, 128, 8], F32, kind="Internal")
    t.ag_in = nc.dram_tensor("ag_in", [4, 128, S], BF16, kind="Internal")
    t.ag_out = nc.dram_tensor("ag_out", [8, 4, 128, S], BF16, kind="Internal")

    t.dout = {}
    def do(name, shape, dtype):
        if name in t.debug:
            t.dout[name] = nc.dram_tensor("dbg_" + name, list(shape), dtype, kind="ExternalOutput").ap()
    do("kT", (128, 4 * S), BF16)
    do("qT", (128, 4 * S), BF16)
    do("vA", (128, NKC * 520), BF16)
    do("xq", (128, 8 * TOWN), BF16)
    do("scl", (128, 32), F32)
    do("small", (128, 16), F32)
    do("ctxT", (128, 4 * S), BF16)
    do("ctxown", (128, 8 * TOWN), BF16)
    do("aout", (TOWN, D), F32)
    do("upst", (NIC, 128, TOWN), F32)
    do("gatest", (NIC, 128, TOWN), F32)
    do("g2st", (NIC, 128, TOWN), F32)
    do("ost", (NIC, 128, TOWN), F32)
    do("wdqst", (NIC, 128, D), BF16)

    es = contextlib.ExitStack()
    with es:
        def sb(name, shape, dtype, offset):
            return nc.alloc_sbuf_tensor_at(name, list(shape), dtype, offset=offset)

        t.kT = sb("kT", (128, 4 * S), BF16, OFF_KT)
        t.qT = sb("qT", (128, 4 * S), BF16, OFF_QT)
        t.vA = sb("vA", (128, NKC * 520), BF16, OFF_VA)
        t.xn_fm = sb("xn_fm", (128, 8 * TOWN), BF16, OFF_XNFM)
        t.wkqv = sb("wkqv", (128, 2 * 8 * 512), BF16, OFF_CTXOWN)
        t.wvb = sb("wvb", (128, 8 * 512), BF16, OFF_CTXOWN + 16384)
        t.xq_fm = sb("xq_fm", (128, 8 * TOWN), BF16, OFF_XQ)
        t.shift_sb = sb("shift_sb", (128, 1024), F32, OFF_XQ)
        t.praw32 = sb("praw32", (128, 1024), F32, OFF_XQ + 4096)
        t.pfin2 = sb("pfin2", (128, 2048), BF16, OFF_XQ + 8192)
        t.slotA = sb("slotA", (128, 2048), F32, OFF_XQ + 12288)
        t.slotB = sb("slotB", (128, 2048), F32, OFF_XQ + 20480)
        t.praw32b = sb("praw32b", (128, 1024), F32, OFF_XQ + 28672)
        t.ctx_own = sb("ctx_own", (128, 8 * TOWN), BF16, OFF_CTXOWN)
        t.wprojT = sb("wprojT", (128, 8 * D), BF16, OFF_VA)
        t.aout_a = sb("aout_a", (128, 8 * D), BF16, OFF_KT)
        t.aout_b = sb("aout_b", (128, 8 * D), BF16, OFF_KT + 16384)
        t.sTu = sb("sTu", (32, FFI), F32, OFF_CTXOWN)
        t.rTu = sb("rTu", (32, FFI), F32, OFF_CTXOWN + 12288)
        t.s5 = [sb(f"s5_{i}", (128, 2048), F32, OFF_QT + 8192 * i) for i in range(4)] + \
               [sb(f"s5_{i+4}", (128, 2048), F32, OFF_CTXOWN + 8192 * i) for i in range(4)]
        t.sall = sb("sall", (128, 768), F32, OFF_CTXOWN + 24576)
        t.wqb = sb("wqb", (128, 8 * 512), BF16, OFF_STREAM)
        t.ebuf0 = sb("ebuf0", (128, 1024), F32, OFF_STREAM + 8192)
        t.ebuf1 = sb("ebuf1", (128, 1024), F32, OFF_STREAM + 12288)
        t.oqbuf = sb("oqbuf", (128, NIC * 512), BF16, OFF_QT + 16384)
        t.wdq_s = sb("wdq_s", (128, 2 * D), BF16, OFF_QT + 16384 + 24576)

        fo = OFF_FIX
        t.masks = sb("masks", (128, 2048), BF16, fo); fo += 4096
        t.ident = sb("ident", (128, 128), F32, fo); fo += 512
        t.ones64 = sb("ones64", (128, 2), BF16, fo); fo += 32
        t.ind32 = sb("ind32", (32, 1024), F32, fo); fo += 4096
        t.ind96 = sb("ind96", (96, 3072), F32, OFF_QT)
        t.rTd = sb("rTd", (96, 1024), F32, OFF_QT + 12288)
        t.onesf = sb("onesf", (128, 128), F32, fo); fo += 512
        t.btile = sb("btile", (128, D), F32, fo); fo += 4096
        t.scl = sb("scl", (128, 32), F32, fo); fo += 128
        t.offs = sb("offs", (1, 8), I32, fo); fo += 32
        t.sTd = sb("sTd", (96, D), F32, fo); fo += 4096
        t.norm2 = sb("norm2", (2, 64), F32, fo); fo += 256
        t.small = sb("small", (128, 16), F32, fo); fo += 64
        t.mxcol = sb("mxcol", (128, 48), F32, fo); fo += 192
        t.negct = sb("negct", (128, 64), F32, fo); fo += 256
        assert fo <= OFF_STREAM, fo

        so = OFF_STREAM
        t.xbuf = sb("xbuf", (128, 2 * D), F32, so)
        t.ctxstage = sb("ctxstage", (128, 1024), F32, so)
        so += 8192
        t.xnbuf = sb("xnbuf", (128, D), F32, so); so += 4096
        t.xnbf = sb("xnbf", (128, 2 * D), BF16, so); so += 4096
        t.stats = sb("stats", (128, 32), F32, so); so += 128
        t.lnt = sb("lnt", (128, 32), F32, so); so += 128
        t.praw = sb("praw", (128, 1024), BF16, so)
        t.pfin = sb("pfin", (128, 1024), BF16, so + 2048)
        t.qdt = sb("qdt", (128, D), F32, so)
        t.rnorm_e = sb("rnorm_e", (128, 1024), F32, so)
        so += 4096
        t.rnorm = sb("rnorm", (128, 512), F32, so); so += 2048
        t.sqs = sb("sqs", (128, 1024), BF16, so); so += 2048
        assert so <= 229344, so

        t.pA = es.enter_context(nc.psum_tensor("pA", [128, 2048], F32))
        t.pB = es.enter_context(nc.psum_tensor("pB", [128, 2048], F32))

        sems = {}
        for name in ["ph", "ldx", "ldw", "lda", "ldb", "tra", "trb", "trs",
                     "c1", "c2", "c3", "c4", "c5", "c6", "c7", "c8", "c9",
                     "n1", "n2", "ccs", "ccd", "stw", "str", "sto",
                     "a1", "a2", "a3", "a4", "a5"]:
            sems[name] = es.enter_context(nc.semaphore(name))
        t.sem = sems

        sch = Sched()
        c2cs = {}
        def midhook(idx, part):
            if idx == 0 and part == 0:
                c2cs["a"] = _p5_sclderiv(nc, t, sch)
            if idx <= 11:
                _p5a_chunk(nc, t, sch, 2 * idx + part, c2cs["a"])
            elif 13 <= idx <= 24:
                _p5b_chunk(nc, t, sch, 2 * (idx - 13) + part, c2cs["b"])
        def prehook(idx):
            if idx == 13:
                def v_c15(e):
                    e.drain()
                    e.tensor_scalar(t.scl[:, 15:16], t.scl[:, 8:9], -CMAGIC, None,
                                    OP.mult).then_inc(t.sem["c2"], 1)
                r = _p5_arwin(nc, t, sch, 2, _p5_sc3(t), (2, 8), post_fn=v_c15)
                c2cs["b"] = sch.inc("c2")
                return r
            if idx == 26:
                return _p5_arwin(nc, t, sch, 3, _p5_sc4(t), (2, 10))
            return []
        def gen_p3x(nc, t, sch):
            gen_p3(nc, t, sch, midhook=midhook, prehook=prehook)
        def gen_p45(nc, t, sch):
            c7p5b = sch.val("c7")
            agccd, ccs_ag = _p4_ag_start(nc, t, sch)
            agv = _p4_extract(nc, t, sch, ccs_ag)
            _p4_proj(nc, t, sch, agv)
            c8proj = sch.val("c8")
            _p5_dequant(nc, t, sch, agccd, c8proj)
            _p5_main(nc, t, sch, agccd, c7p5b)
        gens = [gen_p0, gen_p1, gen_p2a, gen_p2b, gen_p3x, gen_p45]
        nph = 0
        for gi, gen in enumerate(gens):
            if gi >= phases_upto:
                break
            gen(nc, t, sch)
            sch.steps.append(("BARRIER", dict(sch.cnt)))
            nph += 1
        gen_dbg(nc, t, sch)
        sch.steps.append(("BARRIER", dict(sch.cnt)))
        nph += 1

        block = es.enter_context(nc.Block())

        SEM_ISSUER = {"lda": "sync", "ldb": "sync", "ldx": "sync", "ldw": "gpsimd",
                      "tra": "scalar", "trb": "scalar", "trs": "sync",
                      "ccd": "gpsimd", "stw": "gpsimd", "str": "sync", "sto": "gpsimd"}

        def runner(eng):
            def body(e):
                k = 0
                for s_eng, fn in sch.steps:
                    if s_eng == "BARRIER":
                        k += 1
                        for sname, issuer in SEM_ISSUER.items():
                            if issuer == eng and fn.get(sname, 0) > 0:
                                e.wait_ge(sems[sname], fn[sname])
                        e.drain()
                        e.nop().then_inc(sems["ph"], 1)
                        e.wait_ge(sems["ph"], 5 * k)
                    elif s_eng == eng:
                        fn(e)
                return None
            return body

        block.sync(runner("sync"))
        block.scalar(runner("scalar"))
        block.vector(runner("vector"))
        block.tensor(runner("tensor"))
        block.gpsimd(runner("gpsimd"))

    return nc, t


def gen_p0(nc, t, sch):
    sem = t.sem
    din = t.din

    ldx0 = sch.val("ldx")
    def s_loads(e):
        e.dma_start(out=t.masks[:, :], in_=din["masks"]).then_inc(sem["ldx"], 16)
        e.dma_start(out=t.ident[:, :], in_=din["ident"]).then_inc(sem["ldx"], 16)
        e.dma_start(out=t.ones64[:, :], in_=din["ones64"]).then_inc(sem["ldx"], 16)
        e.dma_start(out=t.ind32[:, :], in_=din["ind32"]).then_inc(sem["ldx"], 16)
        e.dma_start(out=t.onesf[:, :], in_=din["onesf"]).then_inc(sem["ldx"], 16)
        e.dma_start(out=t.offs[:, :], in_=din["offs"]).then_inc(sem["ldx"], 16)
        e.dma_start(out=t.xbuf[0:1, 0:D], in_=din["b_row"]).then_inc(sem["ldx"], 16)
    for _ in range(7):
        sch.inc("ldx", 16)
    sch.on("sync", s_loads)
    v_ld = sch.val("ldx")

    def g_loads(e):
        e.dma_start(out=t.wkqv[:, 0:8 * 512].rearrange("p (c f) -> p c f", c=8),
                    in_=din["wkT"].rearrange("(c p) f -> p c f", p=128)).then_inc(sem["ldw"], 16)
        e.dma_start(out=t.wkqv[:, 8 * 512:16 * 512].rearrange("p (c f) -> p c f", c=8),
                    in_=din["wqT"].rearrange("(c p) f -> p c f", p=128)).then_inc(sem["ldw"], 16)
        e.dma_start(out=t.wvb[:, :].rearrange("p (c f) -> p c f", c=8),
                    in_=din["wvT"].rearrange("(c p) f -> p c f", p=128)).then_inc(sem["ldw"], 16)
        ones_ap = t.vA[:, :].rearrange("p (c h e) -> p c h e", c=NKC, h=8)[:, :, :, 64:65]
        e.memset(ones_ap, 1.0)
        e.memset(t.scl[:, :], 0.0)
        e.memset(t.small[:, :], 0.0)
        e.drain()
        e.memset(t.scl[:, 12:13], CMAGIC)
    for _ in range(3):
        sch.inc("ldw", 16)
    sch.on("gpsimd", g_loads)

    def t_btile(e):
        e.wait_ge(sem["ldx"], v_ld)
        e.matmul(t.pA[:, 0:512], t.onesf[0:1, :], t.xbuf[0:1, 0:512], start=True, stop=True)
        e.matmul(t.pA[:, 512:1024], t.onesf[0:1, :], t.xbuf[0:1, 512:1024],
                 start=True, stop=True).then_inc(sem["c1"], 1)
    c1v = sch.inc("c1")
    sch.on("tensor", t_btile)

    def v_btile(e):
        e.wait_ge(sem["c1"], c1v)
        e.tensor_copy(t.btile[:, :], t.pA[:, 0:1024])
    sch.on("vector", v_btile)


def gen_ln(nc, t, sch, tiles, x_in_fn, fm_dst_fn, absmax_base=None, qdq=False,
           loader=("sync", ("lda", "ldb")), cache_to=None, stats_cached=False):
    sem = t.sem
    NG = tiles // 8
    for g in range(NG):
        if stats_cached:
            if g == 0:
                def g_push(e):
                    e.dma_start(out=t.lnc_hbm.ap().rearrange("b p c -> p b c"),
                                in_=t.negct[:, 0:64].rearrange("p (b c) -> p b c", c=8)
                                ).then_inc(sem["ccd"], 16)
                sch.inc("ccd", 16)
                sch.on("gpsimd", g_push)
            pushv = sch.val("ccd")
            c3c = sch.val("c3")
            def g_stat(e, g=g, c3c=c3c, pushv=pushv):
                e.wait_ge(sem["ccd"], pushv)
                e.wait_ge(sem["c3"], c3c)
                mnoff = e.value_load(t.offs[0:1, (4 if g == 0 else 5):(5 if g == 0 else 6)])
                nmoff = e.value_load(t.offs[0:1, (6 if g == 0 else 7):(7 if g == 0 else 8)])
                e.dma_start(out=t.lnt[:, 0:8],
                            in_=t.lnc_hbm.ap()[bass.ts(mnoff, 1), :, :]).then_inc(sem["ccd"], 16)
                e.dma_start(out=t.lnt[:, 24:32],
                            in_=t.lnc_hbm.ap()[bass.ts(nmoff, 1), :, :]).then_inc(sem["ccd"], 16)
            ldsv = sch.inc("ccd", 32)
            sch.on("gpsimd", g_stat)
            for ji in range(8):
                j = 8 * g + ji
                c3p = sch.val("c3")
                leng = loader[0]
                lsem = loader[1][j % 2]
                def s_load2(e, j=j, c3p=c3p, lsem=lsem):
                    e.wait_ge(sem["c3"], max(0, c3p - 1))
                    e.dma_start(out=t.xbuf[:, (j % 2) * D:(j % 2 + 1) * D],
                                in_=x_in_fn(e, j)).then_inc(sem[lsem], 16)
                ldv = sch.inc(lsem, 16)
                sch.on(leng, s_load2)
                trv = sch.val(("tra", "trb")[j % 2])
                def v_apply(e, j=j, ji=ji, ldv=ldv, trv=trv, lsem=lsem,
                            tsem=("tra", "trb")[j % 2], ldsv=ldsv):
                    e.wait_ge(sem[lsem], ldv)
                    e.wait_ge(sem["ccd"], ldsv)
                    e.wait_ge(sem[tsem], trv)
                    e.drain()
                    ln = t.lnt
                    e.tensor_scalar(t.xnbuf[:, :], t.xbuf[:, (j % 2) * D:(j % 2 + 1) * D],
                                    ln[:, 24 + ji:25 + ji], ln[:, ji:ji + 1],
                                    OP.mult, OP.add).then_inc(sem["c3"], 1)
                    e.drain()
                    e.tensor_scalar(t.qdt[:, :], t.xnbuf[:, :], t.scl[:, 1:2], CMAGIC,
                                    OP.mult, OP.add)
                    e.drain()
                    e.tensor_scalar(t.xnbf[:, (j % 2) * D:(j % 2 + 1) * D], t.qdt[:, :],
                                    CMAGIC, t.scl[:, 0:1], OP.subtract, OP.mult).then_inc(sem["c2"], 1)
                sch.inc("c3")
                c2v = sch.inc("c2")
                sch.on("vector", v_apply)
                tsem = ("tra", "trb")[j % 2]
                def a_tr(e, j=j, c2v=c2v, tsem=tsem):
                    e.wait_ge(sem["c2"], c2v)
                    for c in range(8):
                        e.dma_start(out=fm_dst_fn(j, c),
                                    in_=t.xnbf[:, (j % 2) * D + 128 * c:(j % 2) * D + 128 * (c + 1)],
                                    transpose=True).then_inc(sem[tsem], 16)
                sch.inc(tsem, 16 * 8)
                sch.on("scalar", a_tr)
            continue
        for ji in range(8):
            j = 8 * g + ji
            c1p = sch.val("c1")
            c9p = sch.val("c9")
            c3p = sch.val("c3")
            leng = loader[0]
            lsem = loader[1][j % 2]
            def s_load(e, j=j, c1p=c1p, c9p=c9p, c3p=c3p, lsem=lsem):
                e.wait_ge(sem["c1"], max(0, c1p - 1))
                e.wait_ge(sem["c9"], max(0, c9p - 1))
                e.wait_ge(sem["c3"], c3p)
                e.dma_start(out=t.xbuf[:, (j % 2) * D:(j % 2 + 1) * D],
                            in_=x_in_fn(e, j)).then_inc(sem[lsem], 16)
            ldv = sch.inc(lsem, 16)
            sch.on(leng, s_load)
            c1g = sch.val("c1")
            def a_sq(e, j=j, ldv=ldv, lsem=lsem, c1g=c1g):
                e.wait_ge(sem[lsem], ldv)
                e.wait_ge(sem["c1"], c1g)
                sc = 2 * (j % 16)
                e.activation(t.qdt[:, :], t.xbuf[:, (j % 2) * D:(j % 2 + 1) * D], ACT.Square,
                             accum_out=t.stats[:, sc + 1:sc + 2]).then_inc(sem["c1"], 1)
            sch.inc("c1")
            sch.on("scalar", a_sq)
            def v_sum(e, j=j, ldv=ldv, lsem=lsem):
                e.wait_ge(sem[lsem], ldv)
                sc = 2 * (j % 16)
                e.reduce_sum(t.stats[:, sc:sc + 1],
                             t.xbuf[:, (j % 2) * D:(j % 2 + 1) * D], AX.X).then_inc(sem["c9"], 1)
            sch.inc("c9")
            sch.on("vector", v_sum)
        c1v = sch.val("c1")
        def v_chain(e, g=g, c1v=c1v):
            e.wait_ge(sem["c1"], c1v)
            e.drain()
            base = 16 * (g % 2)
            st = t.stats[:, base:base + 16].rearrange("p (j two) -> p j two", two=2)
            ln = t.lnt
            e.tensor_scalar(ln[:, 0:8], st[:, :, 0], 1.0 / D, None, OP.mult)
            e.tensor_scalar(ln[:, 8:16], st[:, :, 1], 1.0 / D, None, OP.mult)
            e.drain()
            e.tensor_tensor(ln[:, 16:24], ln[:, 0:8], ln[:, 0:8], OP.mult)
            e.drain()
            e.scalar_tensor_tensor(ln[:, 24:32], ln[:, 8:16], 1e-5, ln[:, 16:24],
                                   OP.add, OP.subtract)
            e.drain()
            e.reciprocal(ln[:, 16:24], ln[:, 24:32]).then_inc(sem["c4"], 1)
        c4v = sch.inc("c4")
        sch.on("vector", v_chain)
        def a_sqrt(e, c4v=c4v):
            e.wait_ge(sem["c4"], c4v)
            e.activation(t.lnt[:, 24:32], t.lnt[:, 16:24], ACT.Sqrt).then_inc(sem["c5"], 1)
        c5v = sch.inc("c5")
        sch.on("scalar", a_sqrt)
        def v_nmr(e, g=g, c5v=c5v, cache_to=cache_to):
            e.wait_ge(sem["c5"], c5v)
            ln = t.lnt
            e.tensor_tensor(ln[:, 16:24], ln[:, 0:8], ln[:, 24:32], OP.mult)
            e.drain()
            e.tensor_scalar(ln[:, 0:8], ln[:, 16:24], -1.0, None, OP.mult)
            e.drain()
            if cache_to is not None:
                cb = cache_to + 8 * g
                e.tensor_copy(t.negct[:, cb:cb + 8], ln[:, 0:8])
                e.tensor_copy(t.negct[:, 32 + cb:32 + cb + 8], ln[:, 24:32])
                e.drain()
        sch.on("vector", v_nmr)
        for ji in range(8):
            j = 8 * g + ji
            c3p = sch.val("c3")
            leng = loader[0]
            lsem = loader[1][j % 2]
            c1p2 = sch.val("c1")
            c9p2 = sch.val("c9")
            def s_load2(e, j=j, c3p=c3p, lsem=lsem, c1p2=c1p2, c9p2=c9p2):
                e.wait_ge(sem["c3"], max(0, c3p - 1))
                e.wait_ge(sem["c1"], c1p2)
                e.wait_ge(sem["c9"], c9p2)
                e.dma_start(out=t.xbuf[:, (j % 2) * D:(j % 2 + 1) * D],
                            in_=x_in_fn(e, j)).then_inc(sem[lsem], 16)
            ldv = sch.inc(lsem, 16)
            sch.on(leng, s_load2)
            trv = sch.val(("tra", "trb")[j % 2])
            def v_apply(e, j=j, ji=ji, ldv=ldv, trv=trv, lsem=lsem, tsem=("tra", "trb")[j % 2]):
                e.wait_ge(sem[lsem], ldv)
                e.wait_ge(sem[tsem], trv)
                e.drain()
                ln = t.lnt
                e.tensor_scalar(t.xnbuf[:, :], t.xbuf[:, (j % 2) * D:(j % 2 + 1) * D],
                                ln[:, 24 + ji:25 + ji], ln[:, ji:ji + 1],
                                OP.mult, OP.add).then_inc(sem["c3"], 1)
                e.drain()
                if not qdq:
                    e.tensor_copy(t.xnbf[:, (j % 2) * D:(j % 2 + 1) * D],
                                  t.xnbuf[:, :]).then_inc(sem["c2"], 1)
                else:
                    e.tensor_scalar(t.qdt[:, :], t.xnbuf[:, :], t.scl[:, 1:2], CMAGIC,
                                    OP.mult, OP.add)
                    e.drain()
                    e.tensor_scalar(t.xnbf[:, (j % 2) * D:(j % 2 + 1) * D], t.qdt[:, :],
                                    CMAGIC, t.scl[:, 0:1], OP.subtract, OP.mult).then_inc(sem["c2"], 1)
                if absmax_base is not None:
                    e.drain()
                    e.tensor_reduce(t.mxcol[:, absmax_base + j:absmax_base + j + 1],
                                    t.xnbuf[:, :], AX.X, OP.max, apply_absolute_value=True)
            sch.inc("c3")
            c2v = sch.inc("c2")
            sch.on("vector", v_apply)
            tsem = ("tra", "trb")[j % 2]
            def a_tr(e, j=j, c2v=c2v, tsem=tsem):
                e.wait_ge(sem["c2"], c2v)
                for c in range(8):
                    e.dma_start(out=fm_dst_fn(j, c),
                                in_=t.xnbf[:, (j % 2) * D + 128 * c:(j % 2) * D + 128 * (c + 1)],
                                transpose=True).then_inc(sem[tsem], 16)
            sch.inc(tsem, 16 * 8)
            sch.on("scalar", a_tr)


def gen_p1(nc, t, sch):
    sem = t.sem
    din = t.din
    pA, pB = t.pA, t.pB

    def x_in(e, j):
        return din["x_seq"][128 * j:128 * (j + 1), :]

    def fm_dst(j, c):
        return t.xn_fm[:, TOWN * c + 128 * j: TOWN * c + 128 * (j + 1)]

    def kqv_half(th):
        trva = sch.val("tra")
        trvb = sch.val("trb")
        ldwv = sch.val("ldw")
        for wi in range(2):
            for fc in range(4):
                j = wi * 4 + fc
                evv = sch.val("c6")
                def t_mm(e, wi=wi, fc=fc, th=th, trva=trva, trvb=trvb, ldwv=ldwv, evv=evv):
                    e.wait_ge(sem["ldw"], ldwv)
                    e.wait_ge(sem["tra"], trva)
                    e.wait_ge(sem["trb"], trvb)
                    e.wait_ge(sem["c6"], evv)
                    wbase = (wi * 8 + 0) * 512
                    for cc in range(8):
                        lhs = t.wkqv[:, wi * 4096 + 512 * cc + 128 * fc:
                                        wi * 4096 + 512 * cc + 128 * (fc + 1)]
                        e.ldweights(lhs)
                        for tq in range(4):
                            mm = e.matmul(pA[:, 512 * tq:512 * (tq + 1)], lhs,
                                          t.xn_fm[:, TOWN * cc + 512 * tq:
                                                     TOWN * cc + 512 * (tq + 1)],
                                          start=(cc == 0), stop=(cc == 7))
                    mm.then_inc(sem["c7"], 1)
                c7v = sch.inc("c7")
                sch.on("tensor", t_mm)
                def v_evac(e, wi=wi, fc=fc, th=th, c7v=c7v):
                    e.wait_ge(sem["c7"], c7v)
                    dst = (t.kT if wi == 0 else t.qT)
                    e.tensor_copy(dst[:, S * fc + 2048 * th:S * fc + 2048 * (th + 1)],
                                  pA[:, 0:2048]).then_inc(sem["c6"], 1)
                sch.inc("c6")
                sch.on("vector", v_evac)

    def v_half(th):
        trva = sch.val("tra")
        trvb = sch.val("trb")
        for tc in range(16 * th, 16 * (th + 1)):
            evv = sch.val("c9")
            slot = tc % 2
            def t_vmm(e, tc=tc, slot=slot, trva=trva, trvb=trvb, evv=evv):
                e.wait_ge(sem["tra"], trva)
                e.wait_ge(sem["trb"], trvb)
                e.wait_ge(sem["c9"], max(0, evv - 1))
                for cc in range(8):
                    tl = tc % 16
                    lhs = t.xn_fm[:, TOWN * cc + 128 * tl:TOWN * cc + 128 * (tl + 1)]
                    e.ldweights(lhs)
                    mm = e.matmul(pB[:, 512 * slot:512 * (slot + 1)], lhs, t.wvb[:, 512 * cc:512 * (cc + 1)],
                                  start=(cc == 0), stop=(cc == 7))
                mm.then_inc(sem["c8"], 1)
            c8v = sch.inc("c8")
            sch.on("tensor", t_vmm)
            def v_vevac(e, tc=tc, slot=slot, c8v=c8v):
                e.wait_ge(sem["c8"], c8v)
                src = pB[:, 512 * slot:512 * (slot + 1)].rearrange("p (h d) -> p h d", h=8)
                dst = t.vA[:, 520 * tc:520 * (tc + 1)].rearrange("p (h e) -> p h e", h=8)[:, :, 0:64]
                e.tensor_copy(dst, src).then_inc(sem["c9"], 1)
            sch.inc("c9")
            sch.on("vector", v_vevac)

    gen_ln(nc, t, sch, 16, x_in, fm_dst, absmax_base=0, cache_to=0)
    kqv_half(0)
    v_half(0)
    c7v, c8v = sch.val("c7"), sch.val("c8")
    def a_waitreuse(e, c7v=c7v, c8v=c8v):
        e.wait_ge(sem["c7"], c7v)
        e.wait_ge(sem["c8"], c8v)
    sch.on("scalar", a_waitreuse)
    def x_in2(e, j):
        return din["x_seq"][128 * (j + 16):128 * (j + 17), :]
    gen_ln(nc, t, sch, 16, x_in2, fm_dst, absmax_base=16, cache_to=16)
    kqv_half(1)
    v_half(1)

    vfin = sch.val("c3")
    n2p = sch.val("n2")
    def v_negb(e, vfin=vfin):
        e.wait_ge(sem["c3"], vfin)
        e.drain()
        e.tensor_reduce(t.small[:, 6:7], t.mxcol[:, 0:32], AX.X, OP.max).then_inc(sem["n1"], 1)
    n1v3 = sch.inc("n1")
    sch.on("vector", v_negb)
    def t_ntr3(e, n1v3=n1v3, n2p=n2p):
        e.wait_ge(sem["n1"], n1v3)
        e.wait_ge(sem["n2"], n2p)
        e.transpose(pB[0:1, 8:136], t.small[:, 6:7], t.ident[:, :]).then_inc(sem["n2"], 1)
    n2v3 = sch.inc("n2")
    sch.on("tensor", t_ntr3)
    def v_gmax(e, n2v3=n2v3):
        e.wait_ge(sem["n2"], n2v3)
        e.tensor_reduce(t.small[0:1, 7:8], pB[0:1, 8:136], AX.X, OP.max).then_inc(sem["n1"], 1)
    sch.inc("n1")
    sch.on("vector", v_gmax)


def gen_p2a(nc, t, sch):
    sem = t.sem
    din = t.din
    pB = t.pB

    ccd1 = sch.inc("ccd", 16)
    ccs1 = sch.inc("ccs", 1)
    ccd2 = sch.inc("ccd", 16)
    arv = sch.inc("c1")
    def g_ar(e, ccd1=ccd1, ccs1=ccs1, ccd2=ccd2):
        e.dma_start(out=t.ar_in[0].ap()[0:1, 0:8], in_=t.small[0:1, 0:8]).then_inc(sem["ccd"], 16)
        e.wait_ge(sem["ccd"], ccd1)
        e.collective_compute("AllReduce", OP.max, replica_groups=[list(range(8))],
                             ins=[t.ar_in[0].ap()], outs=[t.ar_out[0].ap()]).then_inc(sem["ccs"], 1)
        e.wait_ge(sem["ccs"], ccs1)
        e.dma_start(out=t.small[0:1, 8:9], in_=t.ar_out[0].ap()[0:1, 7:8]).then_inc(sem["ccd"], 16)
        e.wait_ge(sem["ccd"], ccd2)
        e.nop().then_inc(sem["c1"], 1)
    sch.on("gpsimd", g_ar)

    def v_scales(e, arv=arv):
        e.wait_ge(sem["c1"], arv)
        e.tensor_scalar(t.small[0:1, 9:10], t.small[0:1, 8:9], 1.0 / 32767.0, 1e-12, OP.mult, OP.max)
        e.drain()
        e.reciprocal(t.small[0:1, 10:11], t.small[0:1, 9:10]).then_inc(sem["c2"], 1)
    c2v = sch.inc("c2")
    sch.on("vector", v_scales)
    def t_bcast(e, c2v=c2v):
        e.wait_ge(sem["c2"], c2v)
        e.matmul(pB[:, 0:2], t.onesf[0:1, :], t.small[0:1, 9:11], start=True, stop=True).then_inc(sem["c1"], 1)
    c1v = sch.inc("c1")
    sch.on("tensor", t_bcast)
    def v_sc2(e, c1v=c1v):
        e.wait_ge(sem["c1"], c1v)
        e.tensor_copy(t.scl[:, 0:2], pB[:, 0:2]).then_inc(sem["c2"], 1)
    scv = sch.inc("c2")
    sch.on("vector", v_sc2)

    def x_in(e, j):
        return din["x_own"][128 * j:128 * (j + 1), :]

    def fm_dst(j, c):
        return t.xq_fm[:, TOWN * c + 128 * j: TOWN * c + 128 * (j + 1)]

    def v_wait(e, scv=scv):
        e.wait_ge(sem["c2"], scv)
    sch.on("vector", v_wait)
    gen_ln(nc, t, sch, 16, x_in, fm_dst, absmax_base=None, qdq=True, stats_cached=True)


def _gen_wscales(nc, t, sch, wname, nchunks, blocks_per_chunk, dst_sT, halfload):
    sem = t.sem
    din = t.din
    for i in range(nchunks):
        nb = blocks_per_chunk
        rounds = 2 if halfload else 1
        for h in range(rounds):
            cw = 1536 if halfload else nb * 32
            c6p = sch.val("c6")
            c7p = sch.val("c7")
            c8p = sch.val("c8")
            def s_ld(e, i=i, h=h, cw=cw, c6p=c6p, c7p=c7p, c8p=c8p, wname=wname):
                e.wait_ge(sem["c6"], c6p)
                e.wait_ge(sem["c7"], c7p)
                e.wait_ge(sem["c8"], c8p)
                e.dma_start(out=t.xbuf[:, 0:cw],
                            in_=din[wname][128 * i:128 * (i + 1), 1536 * h:1536 * h + cw]
                            ).then_inc(sem["lda"], 16)
            ldv = sch.inc("lda", 16)
            sch.on("sync", s_ld)
            nbh = nb // rounds
            def v_red(e, i=i, h=h, cw=cw, ldv=ldv, nb=nb, nbh=nbh):
                e.wait_ge(sem["lda"], ldv)
                wv = t.xbuf[:, 0:cw].rearrange("p (b e) -> p b e", e=32)
                e.tensor_reduce(t.qdt[:, 0:nbh], wv, AX.X, OP.max, apply_absolute_value=True)
                e.drain()
                e.tensor_scalar(t.sall[:, nb * i + nbh * h:nb * i + nbh * (h + 1)], t.qdt[:, 0:nbh],
                                1.0 / 7.0, 1e-12, OP.mult, OP.max).then_inc(sem["c6"], 1)
            sch.inc("c6")
            sch.on("vector", v_red)
    for i in range(nchunks):
        c6v = sch.val("c6")
        n2p = sch.val("n2")
        nb = blocks_per_chunk
        def t_tr(e, i=i, c6v=c6v, n2p=n2p, nb=nb):
            e.wait_ge(sem["c6"], c6v)
            e.wait_ge(sem["n2"], n2p)
            e.transpose(t.pB[0:nb, 0:128], t.sall[:, nb * i:nb * (i + 1)],
                        t.ident[:, :]).then_inc(sem["n1"], 1)
        n1v = sch.inc("n1")
        sch.on("tensor", t_tr)
        def v_ev(e, i=i, n1v=n1v, nb=nb, dst_sT=dst_sT):
            e.wait_ge(sem["n1"], n1v)
            e.tensor_copy(dst_sT[0:nb, 128 * i:128 * (i + 1)], t.pB[0:nb, 0:128]).then_inc(sem["n2"], 1)
        sch.inc("n2")
        sch.on("vector", v_ev)


def _gen_updown_mm(nc, t, sch, wTname, stash, maxcol0):
    sem = t.sem
    din = t.din
    pA, pB = t.pA, t.pB
    for isl in range(6):
        for cc in range(8):
            c7p = sch.val("c7")
            c6p = sch.val("c6")
            def s_ld(e, isl=isl, cc=cc, c7p=c7p, c6p=c6p, wTname=wTname):
                e.wait_ge(sem["c7"], max(0, c7p - 1))
                e.wait_ge(sem["c6"], c6p)
                e.dma_start(out=t.qdt[:, 512 * (cc % 2):512 * (cc % 2 + 1)],
                            in_=din[wTname][128 * cc:128 * (cc + 1), 512 * isl:512 * (isl + 1)]
                            ).then_inc(sem[("lda", "ldb")[cc % 2]], 16)
            ldv = sch.inc(("lda", "ldb")[cc % 2], 16)
            sch.on("sync", s_ld)
            n2p = sch.val("n2")
            c6r = sch.val("c6")
            def t_bc(e, isl=isl, cc=cc, n2p=n2p, c6r=c6r):
                e.wait_ge(sem["n2"], n2p)
                e.wait_ge(sem["c6"], c6r)
                e.matmul(pB[:, 1024:1536], t.ind32[:, 128 * cc:128 * (cc + 1)],
                         t.rTu[:, 512 * isl:512 * (isl + 1)],
                         start=True, stop=True)
                e.matmul(pB[:, 1536:2048], t.ind32[:, 128 * cc:128 * (cc + 1)],
                         t.sTu[:, 512 * isl:512 * (isl + 1)],
                         start=True, stop=True).then_inc(sem["n1"], 1)
            n1v = sch.inc("n1")
            sch.on("tensor", t_bc)
            mm8 = sch.val("c8")
            def v_dq(e, isl=isl, cc=cc, ldv=ldv, n1v=n1v, mm8=mm8):
                e.wait_ge(sem[("lda", "ldb")[cc % 2]], ldv)
                e.wait_ge(sem["n1"], n1v)
                e.wait_ge(sem["c8"], mm8)
                src = t.qdt[:, 512 * (cc % 2):512 * (cc % 2 + 1)]
                e.tensor_tensor(t.rnorm[:, :], src, pB[:, 1024:1536], OP.mult)
                e.drain()
                e.tensor_scalar(t.rnorm[:, :], t.rnorm[:, :], CMAGIC, CMAGIC, OP.add, OP.subtract)
                e.drain()
                e.tensor_tensor(t.wqb[:, 512 * cc:512 * (cc + 1)], t.rnorm[:, :], pB[:, 1536:2048],
                                OP.mult).then_inc(sem["n2"], 1)
                e.drain()
                e.nop().then_inc(sem["c7"], 1)
            sch.inc("n2")
            sch.inc("c7")
            sch.on("vector", v_dq)
        dqv = sch.val("n2")
        for il in range(4):
            ic = 4 * isl + il
            evp = sch.val("c5")
            def t_mm(e, il=il, dqv=dqv, evp=evp):
                e.wait_ge(sem["n2"], dqv)
                e.wait_ge(sem["c5"], evp)
                for cc in range(8):
                    lhs = t.wqb[:, 512 * cc + 128 * il:512 * cc + 128 * (il + 1)]
                    e.ldweights(lhs)
                    for tq in range(4):
                        mm = e.matmul(pA[:, 512 * tq:512 * (tq + 1)], lhs,
                                      t.xq_fm[:, TOWN * cc + 512 * tq:TOWN * cc + 512 * (tq + 1)],
                                      start=(cc == 0), stop=(cc == 7))
                mm.then_inc(sem["c8"], 1)
            c8v = sch.inc("c8")
            sch.on("tensor", t_mm)
            stp = sch.val("stw")
            def v_evac(e, ic=ic, c8v=c8v, stp=stp, maxcol0=maxcol0):
                e.wait_ge(sem["c8"], c8v)
                e.wait_ge(sem["stw"], stp)
                e.drain()
                e.tensor_copy(t.ebuf0[:, :], pA[:, 0:1024]).then_inc(sem["c5"], 1)
                e.tensor_copy(t.ebuf1[:, :], pA[:, 1024:2048]).then_inc(sem["c5"], 1)
                e.drain()
                e.tensor_reduce(t.mxcol[:, maxcol0 + 2 * (ic % 12):maxcol0 + 2 * (ic % 12) + 1],
                                t.ebuf0[:, :], AX.X, OP.max, apply_absolute_value=True)
                e.drain()
                e.tensor_reduce(t.mxcol[:, maxcol0 + 2 * (ic % 12) + 1:maxcol0 + 2 * (ic % 12) + 2],
                                t.ebuf1[:, :], AX.X, OP.max, apply_absolute_value=True).then_inc(sem["c6"], 1)
            sch.inc("c5", 2)
            c6v = sch.inc("c6")
            sch.on("vector", v_evac)
            if ic % 12 == 11:
                def v_mrg(e, ic=ic, c6v=c6v, maxcol0=maxcol0):
                    e.wait_ge(sem["c6"], c6v)
                    e.drain()
                    e.tensor_reduce(t.mxcol[:, maxcol0 + 24 + (ic // 12):maxcol0 + 25 + (ic // 12)],
                                    t.mxcol[:, maxcol0:maxcol0 + 24], AX.X, OP.max).then_inc(sem["c6"], 1)
                sch.inc("c6")
                sch.on("vector", v_mrg)
            def g_st(e, ic=ic, c6v=c6v, stash=stash):
                e.wait_ge(sem["c6"], c6v)
                e.dma_start(out=stash.ap()[ic, :, 0:1024], in_=t.ebuf0[:, :]).then_inc(sem["stw"], 16)
                e.dma_start(out=stash.ap()[ic, :, 1024:2048], in_=t.ebuf1[:, :]).then_inc(sem["stw"], 16)
            sch.inc("stw", 32)
            sch.on("gpsimd", g_st)


def gen_p2b(nc, t, sch):
    sem = t.sem
    pB = t.pB
    def gen_recip():
        c6v = sch.val("c6")
        n2v = sch.val("n2")
        def v_rc(e, c6v=c6v, n2v=n2v):
            e.wait_ge(sem["c6"], c6v)
            e.wait_ge(sem["n2"], n2v)
            e.reciprocal(t.rTu[:, :], t.sTu[:, :]).then_inc(sem["c6"], 1)
        sch.inc("c6")
        sch.on("vector", v_rc)

    _gen_wscales(nc, t, sch, "wupN", NIC, 32, t.sTu, False)
    gen_recip()
    _gen_updown_mm(nc, t, sch, "wupT", t.stash_up, 0)
    _gen_finmax(nc, t, sch, 0, 11)
    _gen_wscales(nc, t, sch, "wgateN", NIC, 32, t.sTu, False)
    gen_recip()
    _gen_updown_mm(nc, t, sch, "wgateT", t.stash_gate, 0)
    _gen_finmax(nc, t, sch, 0, 12)
    _gen_wscales(nc, t, sch, "wdownN", 8, 96, t.sTd, True)
    ccd1 = sch.inc("ccd", 16)
    ccs1 = sch.inc("ccs", 1)
    ccd2 = sch.inc("ccd", 16)
    c6f = sch.val("c6")
    arv = sch.inc("c1")
    def g_ar(e, ccd1=ccd1, ccs1=ccs1, ccd2=ccd2, c6f=c6f):
        e.wait_ge(sem["c6"], c6f)
        e.dma_start(out=t.ar_in[1].ap()[0:1, 0:8], in_=t.small[0:1, 8:16]).then_inc(sem["ccd"], 16)
        e.wait_ge(sem["ccd"], ccd1)
        e.collective_compute("AllReduce", OP.max, replica_groups=[list(range(8))],
                             ins=[t.ar_in[1].ap()], outs=[t.ar_out[1].ap()]).then_inc(sem["ccs"], 1)
        e.wait_ge(sem["ccs"], ccs1)
        e.dma_start(out=t.small[0:1, 11:13], in_=t.ar_out[1].ap()[0:1, 3:5]).then_inc(sem["ccd"], 16)
        e.wait_ge(sem["ccd"], ccd2)
        e.nop().then_inc(sem["c1"], 1)
    sch.on("gpsimd", g_ar)
    def v_s2(e, arv=arv):
        e.wait_ge(sem["c1"], arv)
        e.tensor_scalar(t.small[0:1, 0:2], t.small[0:1, 11:13], 1.0 / 32767.0, 1e-12, OP.mult, OP.max)
        e.drain()
        e.reciprocal(t.small[0:1, 2:4], t.small[0:1, 0:2]).then_inc(sem["c2"], 1)
    c2v = sch.inc("c2")
    sch.on("vector", v_s2)
    def t_bc2(e, c2v=c2v):
        e.wait_ge(sem["c2"], c2v)
        e.matmul(pB[:, 0:4], t.onesf[0:1, :], t.small[0:1, 0:4], start=True, stop=True).then_inc(sem["c1"], 1)
    c1v = sch.inc("c1")
    sch.on("tensor", t_bc2)
    def v_bc2(e, c1v=c1v):
        e.wait_ge(sem["c1"], c1v)
        e.tensor_copy(t.scl[:, 4:8], pB[:, 0:4]).then_inc(sem["c2"], 1)
    sch.inc("c2")
    sch.on("vector", v_bc2)


def _gen_finmax(nc, t, sch, maxcol0, dstcol):
    sem = t.sem
    pB = t.pB
    c6v = sch.val("c6")
    def v_f(e, c6v=c6v, maxcol0=maxcol0):
        e.wait_ge(sem["c6"], c6v)
        e.tensor_reduce(t.small[:, 14:15], t.mxcol[:, maxcol0 + 24:maxcol0 + 26],
                        AX.X, OP.max).then_inc(sem["n1"], 1)
    n1v = sch.inc("n1")
    sch.on("vector", v_f)
    n2p = sch.val("n2")
    def t_f(e, n1v=n1v, n2p=n2p):
        e.wait_ge(sem["n1"], n1v)
        e.wait_ge(sem["n2"], n2p)
        e.transpose(pB[0:1, 8:136], t.small[:, 14:15], t.ident[:, :]).then_inc(sem["n2"], 1)
    n2v = sch.inc("n2")
    sch.on("tensor", t_f)
    def v_f2(e, n2v=n2v, dstcol=dstcol):
        e.wait_ge(sem["n2"], n2v)
        e.tensor_reduce(t.small[0:1, dstcol:dstcol + 1], pB[0:1, 8:136], AX.X, OP.max).then_inc(sem["c6"], 1)
    sch.inc("c6")
    sch.on("vector", v_f2)

def gen_p3(nc, t, sch, midhook=None, prehook=None):
    sem = t.sem
    pA, pB = t.pA, t.pB
    for p_ in range(NPAIR):
        for sl in range(8):
            n2p = sch.val("n2")
            def v_ksq(e, p_=p_, sl=sl, n2p=n2p):
                e.wait_ge(sem["n2"], n2p)
                e.drain()
                kslc = t.kT[:, S * p_ + 512 * sl:S * p_ + 512 * (sl + 1)]
                e.tensor_tensor(t.sqs[:, 0:512], kslc, kslc, OP.mult).then_inc(sem["n1"], 1)
            n1v = sch.inc("n1")
            sch.on("vector", v_ksq)
            n2c = sch.val("n2")
            def t_kn(e, n1v=n1v, n2c=n2c):
                e.wait_ge(sem["n1"], n1v)
                e.wait_ge(sem["n2"], n2c)
                e.matmul(pB[0:1, 0:512], t.ones64[:, 0:1], t.sqs[:, 0:512], start=True, stop=True)
                e.matmul(pB[64:65, 0:512], t.ones64[:, 1:2], t.sqs[:, 0:512],
                         start=True, stop=True, tile_position=(0, 64)).then_inc(sem["n2"], 1)
            n2v = sch.inc("n2")
            sch.on("tensor", t_kn)
            def v_kmx(e, p_=p_, sl=sl, n2v=n2v):
                e.wait_ge(sem["n2"], n2v)
                e.drain()
                e.tensor_reduce(t.lnt[0:1, 0:1], pB[0:1, 0:512], AX.X, OP.max)
                e.tensor_reduce(t.lnt[64:65, 0:1], pB[64:65, 0:512], AX.X, OP.max)
                e.drain()
                if sl == 0:
                    e.tensor_copy(t.lnt[0:1, 4 + p_:5 + p_], t.lnt[0:1, 0:1])
                    e.tensor_copy(t.lnt[64:65, 4 + p_:5 + p_], t.lnt[64:65, 0:1])
                else:
                    e.tensor_tensor(t.lnt[0:1, 8:9], t.lnt[0:1, 0:1], t.lnt[0:1, 4 + p_:5 + p_], OP.max)
                    e.tensor_tensor(t.lnt[64:65, 8:9], t.lnt[64:65, 0:1], t.lnt[64:65, 4 + p_:5 + p_], OP.max)
                    e.drain()
                    e.tensor_copy(t.lnt[0:1, 4 + p_:5 + p_], t.lnt[0:1, 8:9])
                    e.tensor_copy(t.lnt[64:65, 4 + p_:5 + p_], t.lnt[64:65, 8:9])
                e.drain()
                e.nop().then_inc(sem["n2"], 1)
            sch.inc("n2")
            sch.on("vector", v_kmx)
    n2f = sch.val("n2")
    def a_ksqrt(e, n2f=n2f):
        e.wait_ge(sem["n2"], n2f)
        e.activation(t.lnt[0:1, 12:16], t.lnt[0:1, 4:8], ACT.Sqrt)
        e.activation(t.lnt[64:65, 12:16], t.lnt[64:65, 4:8], ACT.Sqrt).then_inc(sem["a2"], 1)
    a2k = sch.inc("a2")
    sch.on("scalar", a_ksqrt)
    def v_kneg(e, a2k=a2k):
        e.wait_ge(sem["a2"], a2k)
        e.tensor_scalar(t.lnt[0:1, 12:16], t.lnt[0:1, 12:16], -1.0, None, OP.mult)
        e.tensor_scalar(t.lnt[64:65, 12:16], t.lnt[64:65, 12:16], -1.0, None, OP.mult)
        e.drain()
        e.nop().then_inc(sem["n2"], 1)
    sch.inc("n2")
    sch.on("vector", v_kneg)
    prol_n2 = sch.val("n2")

    pend = {}

    def emit_qsq_qn(idx, a2cps, n2sqs):
        p_, qt = idx // NQT, idx % NQT
        def v_qsq(e, p_=p_, qt=qt, n2sqs=n2sqs, prol_n2=prol_n2):
            e.wait_ge(sem["n2"], max(prol_n2, n2sqs))
            e.drain()
            qslc = t.qT[:, S * p_ + 512 * qt:S * p_ + 512 * (qt + 1)]
            e.tensor_tensor(t.sqs[:, 0:512], qslc, qslc, OP.mult).then_inc(sem["n1"], 1)
        n1q = sch.inc("n1")
        sch.on("vector", v_qsq)
        def t_qn(e, n1q=n1q, a2cps=a2cps):
            e.wait_ge(sem["n1"], n1q)
            e.wait_ge(sem["a2"], a2cps)
            e.matmul(pB[0:1, 1024:1536], t.ones64[:, 0:1], t.sqs[:, 0:512],
                     start=True, stop=True)
            e.matmul(pB[64:65, 1024:1536], t.ones64[:, 1:2], t.sqs[:, 0:512],
                     start=True, stop=True, tile_position=(0, 64)).then_inc(sem["n2"], 1)
        n2qv = sch.inc("n2")
        sch.on("tensor", t_qn)
        pend[idx] = dict(n2sqs=n2qv)

    def emit_ncb_shift(idx, a2cps, shiftprev):
        p_ = idx // NQT
        n2qv = pend[idx]["n2sqs"]
        def a_qn(e, n2qv=n2qv, shiftprev=shiftprev):
            e.wait_ge(sem["n2"], max(n2qv, shiftprev))
            e.activation(t.rnorm[0:1, :], pB[0:1, 1024:1536], ACT.Sqrt)
            e.activation(t.rnorm[64:65, :], pB[64:65, 1024:1536], ACT.Sqrt).then_inc(sem["a2"], 1)
        a2qv = sch.inc("a2")
        sch.on("scalar", a_qn)
        def v_ncb(e, p_=p_, a2qv=a2qv):
            e.wait_ge(sem["a2"], a2qv)
            e.drain()
            e.tensor_scalar(t.rnorm[0:1, :], t.rnorm[0:1, :],
                            t.lnt[0:1, 12 + p_:13 + p_], 200.0, OP.mult, OP.add)
            e.tensor_scalar(t.rnorm[64:65, :], t.rnorm[64:65, :],
                            t.lnt[64:65, 12 + p_:13 + p_], 200.0, OP.mult, OP.add)
            e.drain()
            e.nop().then_inc(sem["n2"], 1)
        ncbv = sch.inc("n2")
        sch.on("vector", v_ncb)
        def t_shift(e, ncbv=ncbv, a2cps=a2cps):
            e.wait_ge(sem["n2"], ncbv)
            e.wait_ge(sem["a2"], a2cps)
            e.matmul(pB[:, 1024:1536], t.onesf[0:1, 0:128], t.rnorm[0:1, :],
                     start=True, stop=True, tile_position=(0, 0))
            e.matmul(pB[:, 1536:2048], t.onesf[64:65, 0:128], t.rnorm[64:65, :],
                     start=True, stop=True, tile_position=(64, 0)).then_inc(sem["n2"], 1)
        shiftv = sch.inc("n2")
        sch.on("tensor", t_shift)
        pend[idx]["shiftv"] = shiftv

    def emit_acps(idx):
        shiftv = pend[idx]["shiftv"]
        a5l = sch.val("a5")
        def a_cps(e, shiftv=shiftv, a5l=a5l):
            e.wait_ge(sem["n2"], shiftv)
            e.wait_ge(sem["a5"], a5l)
            e.activation(t.shift_sb[:, :], pB[:, 1024:2048], ACT.Identity).then_inc(sem["a2"], 1)
        pend[idx]["cpsv"] = sch.inc("a2")
        sch.on("scalar", a_cps)

    emit_qsq_qn(0, sch.val("a2"), 0)
    emit_ncb_shift(0, 0, 0)
    emit_acps(0)

    for p_ in range(NPAIR):
        for qt in range(NQT):
            KC = 4 * (qt + 1)
            idx3 = 8 * p_ + qt
            extra_sc = prehook(idx3) if prehook else []
            cpsv = pend[idx3]["cpsv"]
            n2sqs = pend[idx3]["n2sqs"]
            shiftv = pend[idx3]["shiftv"]

            a4st = sch.val("a4")
            a5st = sch.val("a5")
            a2ev = sch.val("a2")
            cxv = {}
            prev_cx = None

            def emit_cx(kc, wv, wsem, p_=p_, KC=KC, a2ev=a2ev):
                def t_cx(e, p_=p_, kc=kc, wv=wv, wsem=wsem, KC=KC, a2ev=a2ev):
                    e.wait_ge(sem[wsem], wv)
                    if kc == 0:
                        e.wait_ge(sem["a2"], a2ev)
                    first, last = (kc == 0), (kc == KC - 1)
                    src = t.pfin2[:, 1024 * (kc % 2):1024 * (kc % 2) + 1024]
                    e.matmul(pA[0:65, 1024:1536],
                             t.vA[:, 520 * kc + 130 * p_:520 * kc + 130 * p_ + 65],
                             src[:, 0:512], start=first, stop=last)
                    e.matmul(pA[0:65, 1536:2048],
                             t.vA[:, 520 * kc + 130 * p_ + 65:520 * kc + 130 * (p_ + 1)],
                             src[:, 512:1024], start=first, stop=last).then_inc(sem["a1"], 1)
                cxv[kc] = sch.inc("a1")
                sch.on("tensor", t_cx)

            for kc in range(KC):
                ps_sc = pA if kc % 2 == 0 else pB
                def t_sc(e, p_=p_, qt=qt, kc=kc, ps_sc=ps_sc,
                         bw=(sch.val("a5") - 1 if kc >= 2 else a5st),
                         ex=(extra_sc if kc == 1 else [])):
                    for smn, vv in ex:
                        e.wait_ge(sem[smn], vv)
                    e.wait_ge(sem["a5"], bw)
                    e.matmul(ps_sc[:, 0:512],
                             t.kT[0:64, S * p_ + 128 * kc:S * p_ + 128 * (kc + 1)],
                             t.qT[0:64, S * p_ + 512 * qt:S * p_ + 512 * (qt + 1)],
                             start=True, stop=True, tile_position=(0, 0))
                    e.matmul(ps_sc[:, 512:1024],
                             t.kT[64:128, S * p_ + 128 * kc:S * p_ + 128 * (kc + 1)],
                             t.qT[64:128, S * p_ + 512 * qt:S * p_ + 512 * (qt + 1)],
                             start=True, stop=True, tile_position=(64, 0)).then_inc(sem["a1"], 1)
                a1v = sch.inc("a1")
                sch.on("tensor", t_sc)
                if prev_cx is not None:
                    emit_cx(*prev_cx)
                a2p = sch.val("a2")
                prw = t.praw32 if kc % 2 == 0 else t.praw32b
                def v_add(e, kc=kc, a1v=a1v, a2p=a2p, ps_sc=ps_sc, prw=prw):
                    e.wait_ge(sem["a1"], a1v)
                    e.wait_ge(sem["a2"], a2p - 1 if kc >= 2 else a2p)
                    e.scalar_tensor_tensor(prw[:, :], ps_sc[:, 0:1024], 1.0,
                                           t.shift_sb[:, :], OP.mult,
                                           OP.add).then_inc(sem["a5"], 1)
                a5v = sch.inc("a5")
                sch.on("vector", v_add)
                pfw = cxv.get(kc - 2, None)
                def a_exp(e, kc=kc, a5v=a5v, pfw=pfw, a4st=a4st, prw=prw):
                    e.wait_ge(sem["a5"], a5v)
                    if pfw is not None:
                        e.wait_ge(sem["a1"], pfw)
                    else:
                        e.wait_ge(sem["a4"], a4st)
                    e.activation(t.pfin2[:, 1024 * (kc % 2):1024 * (kc % 2) + 1024],
                                 prw[:, :], ACT.Exp, scale=0.125).then_inc(sem["a2"], 1)
                a2v = sch.inc("a2")
                sch.on("scalar", a_exp)
                dg = kc - 4 * qt
                if dg >= 0:
                    def v_msk(e, qt=qt, kc=kc, dg=dg, a2v=a2v):
                        e.wait_ge(sem["a2"], a2v)
                        e.drain()
                        mv = t.masks[:, 512 * dg:512 * (dg + 1)]
                        dst = t.pfin2[:, 1024 * (kc % 2):1024 * (kc % 2) + 1024]
                        e.tensor_tensor(dst[:, 0:512], dst[:, 0:512], mv, OP.mult)
                        e.tensor_tensor(dst[:, 512:1024], dst[:, 512:1024], mv,
                                        OP.mult).then_inc(sem["a3"], 1)
                    a3v = sch.inc("a3")
                    sch.on("vector", v_msk)
                    prev_cx = (kc, a3v, "a3")
                else:
                    prev_cx = (kc, a2v, "a2")
                if kc == 1 and idx3 + 1 < 32:
                    emit_qsq_qn(idx3 + 1, cpsv, n2sqs)
                if kc == 2:
                    if idx3 + 1 < 32:
                        emit_ncb_shift(idx3 + 1, cpsv, shiftv)
                    if midhook:
                        midhook(idx3, 0)
                if midhook and kc == min(KC - 1, 10):
                    midhook(idx3, 1)
            emit_cx(*prev_cx)
            if idx3 + 1 < 32:
                emit_acps(idx3 + 1)
            a1e = sch.val("a1")
            a4p = sch.val("a4")
            def a_st(e, a1e=a1e, a4p=a4p):
                e.wait_ge(sem["a1"], a1e)
                e.wait_ge(sem["a4"], a4p)
                e.copy(t.ctxstage[0:65, :], pA[0:65, 1024:2048]).then_inc(sem["a2"], 1)
            astv = sch.inc("a2")
            sch.on("scalar", a_st)
            def v_rc(e, astv=astv):
                e.wait_ge(sem["a2"], astv)
                e.drain()
                e.reciprocal(t.rnorm_e[64:65, 0:1024], t.ctxstage[64:65, :]).then_inc(sem["a4"], 1)
            vrcv = sch.inc("a4")
            sch.on("vector", v_rc)
            def t_rb(e, vrcv=vrcv):
                e.wait_ge(sem["a4"], vrcv)
                e.matmul(pA[0:64, 1024:1536], t.onesf[64:65, 0:64], t.rnorm_e[64:65, 0:512],
                         start=True, stop=True, tile_position=(64, 0))
                e.matmul(pA[0:64, 1536:2048], t.onesf[64:65, 0:64], t.rnorm_e[64:65, 512:1024],
                         start=True, stop=True, tile_position=(64, 0)).then_inc(sem["a1"], 1)
            a1r = sch.inc("a1")
            sch.on("tensor", t_rb)
            def a_ev(e, a1r=a1r, a4p=a4p):
                e.wait_ge(sem["a1"], a1r)
                e.wait_ge(sem["a4"], a4p)
                e.copy(t.rnorm_e[0:64, 0:1024], pA[0:64, 1024:2048]).then_inc(sem["a2"], 1)
            a2r = sch.inc("a2")
            sch.on("scalar", a_ev)
            def v_nm0(e, p_=p_, qt=qt, a2r=a2r):
                e.wait_ge(sem["a2"], a2r)
                e.drain()
                e.tensor_tensor(t.qT[0:64, S * p_ + 512 * qt:S * p_ + 512 * (qt + 1)],
                                t.ctxstage[0:64, 0:512], t.rnorm_e[0:64, 0:512],
                                OP.mult).then_inc(sem["a4"], 1)
            sch.inc("a4")
            sch.on("vector", v_nm0)
            def v_nm1(e, p_=p_, qt=qt, a2r=a2r):
                e.wait_ge(sem["a2"], a2r)
                e.tensor_tensor(t.pfin2[0:64, 0:512], t.ctxstage[0:64, 512:1024],
                                t.rnorm_e[0:64, 512:1024], OP.mult)
                e.drain()
                e.stream_shuffle(t.qT[64:128, S * p_ + 512 * qt:S * p_ + 512 * (qt + 1)],
                                 t.pfin2[0:64, 0:512], list(range(32))).then_inc(sem["a4"], 1)
            sch.inc("a4")
            sch.on("vector", v_nm1)

def _p4_ag_start(nc, t, sch):
    sem = t.sem
    ccd0 = sch.val("ccd")
    def g_ag1(e, ccd0=ccd0):
        for p_ in range(4):
            e.dma_start(out=t.ag_in.ap()[p_, :, :], in_=t.qT[:, S * p_:S * (p_ + 1)]).then_inc(sem["ccd"], 16)
        e.wait_ge(sem["ccd"], ccd0 + 64)
        e.collective_compute("AllGather", OP.bypass, replica_groups=[list(range(8))],
                             ins=[t.ag_in.ap()], outs=[t.ag_out.ap()]).then_inc(sem["ccs"], 1)
    sch.inc("ccd", 64)
    ccs_ag = sch.inc("ccs", 1)
    sch.on("gpsimd", g_ag1)
    return ccd0 + 64, ccs_ag


def _p4_extract(nc, t, sch, ccs_ag):
    sem = t.sem
    din = t.din
    c7g = sch.val("c7")
    stwg = sch.val("stw")
    ccdx = sch.val("ccd")
    def g_ag2(e, ccs_ag=ccs_ag, c7g=c7g, stwg=stwg, ccdx=ccdx):
        e.wait_ge(sem["ccs"], ccs_ag)
        e.wait_ge(sem["c7"], c7g)
        e.wait_ge(sem["stw"], stwg)
        my = e.value_load(t.offs[0:1, 1:2])
        pr = e.value_load(t.offs[0:1, 2:3])
        gv = e.value_load(t.offs[0:1, 3:4])
        for p_ in range(4):
            e.dma_start(out=t.ctx_own[:, TOWN * p_:TOWN * (p_ + 1)],
                        in_=t.ag_out.ap()[bass.ts(my, 1), p_, :, bass.ts(gv, TOWN)]).then_inc(sem["ccd"], 16)
        for p_ in range(4):
            e.dma_start(out=t.ctx_own[:, TOWN * (4 + p_):TOWN * (5 + p_)],
                        in_=t.ag_out.ap()[bass.ts(pr, 1), p_, :, bass.ts(gv, TOWN)]).then_inc(sem["ccd"], 16)
        e.dma_start(out=t.wprojT[:, :].rearrange("p (c f) -> p c f", c=8),
                    in_=din["wprojT"].rearrange("(c p) f -> p c f", p=128)).then_inc(sem["ldw"], 16)
        e.wait_ge(sem["ccd"], ccdx + 128)
        e.nop().then_inc(sem["c4"], 1)
    sch.inc("ccd", 128)
    sch.inc("ldw", 16)
    agv = sch.inc("c4")
    sch.on("gpsimd", g_ag2)
    return agv


def _p4_proj(nc, t, sch, agv):
    sem = t.sem
    pA, pB = t.pA, t.pB
    ldwv = sch.val("ldw")
    for tt in range(16):
        par = tt % 2
        evp = sch.val("c5")
        def t_pj(e, tt=tt, par=par, evp=evp, agv=agv, ldwv=ldwv):
            e.wait_ge(sem["c4"], agv)
            e.wait_ge(sem["ldw"], ldwv)
            e.wait_ge(sem["c5"], evp)
            for fc in range(8):
                lhs = t.ctx_own[:, TOWN * fc + 128 * tt:TOWN * fc + 128 * (tt + 1)]
                e.ldweights(lhs)
                for hf in range(2):
                    mm = e.matmul(pA[:, 1024 * par + 512 * hf:1024 * par + 512 * (hf + 1)], lhs,
                                  t.wprojT[:, 1024 * fc + 512 * hf:1024 * fc + 512 * (hf + 1)],
                                  start=(fc == 0), stop=(fc == 7))
            mm.then_inc(sem["c8"], 1)
        c8v = sch.inc("c8")
        sch.on("tensor", t_pj)
        def v_pj(e, tt=tt, par=par, c8v=c8v):
            e.wait_ge(sem["c8"], c8v)
            dst = (t.aout_a if tt < 8 else t.aout_b)[:, D * (tt % 8):D * (tt % 8 + 1)]
            e.scalar_tensor_tensor(dst, pA[:, 1024 * par:1024 * (par + 1)], 1.0,
                                   t.btile[:, :], OP.mult, OP.add).then_inc(sem["c5"], 1)
        sch.inc("c5")
        sch.on("vector", v_pj)

def _p5_finmax(nc, t, sch, dstcol, a2g=None):
    sem = t.sem
    pB = t.pB
    c6v = sch.val("c6")
    def v_f(e, c6v=c6v):
        e.wait_ge(sem["c6"], c6v)
        e.drain()
        e.tensor_reduce(t.small[:, 14:15], t.mxcol[:, 0:24], AX.X, OP.max).then_inc(sem["n1"], 1)
    n1v = sch.inc("n1")
    sch.on("vector", v_f)
    n2p = sch.val("n2")
    def t_f(e, n1v=n1v, n2p=n2p, a2g=a2g):
        e.wait_ge(sem["n1"], n1v)
        e.wait_ge(sem["n2"], n2p)
        if a2g is not None:
            e.wait_ge(sem["a2"], a2g)
        e.transpose(pB[0:1, 8:136], t.small[:, 14:15], t.ident[:, :]).then_inc(sem["n2"], 1)
    n2v = sch.inc("n2")
    sch.on("tensor", t_f)
    def v_f2(e, n2v=n2v, dstcol=dstcol):
        e.wait_ge(sem["n2"], n2v)
        e.tensor_reduce(t.small[0:1, dstcol:dstcol + 1], pB[0:1, 8:136], AX.X, OP.max).then_inc(sem["c6"], 1)
    sch.inc("c6")
    sch.on("vector", v_f2)


def _p5_ar(nc, t, sch, idx, srccols, scale_fn, bccols):
    sem = t.sem
    pB = t.pB
    ccd1 = sch.inc("ccd", 16)
    ccs1 = sch.inc("ccs", 1)
    ccd2 = sch.inc("ccd", 16)
    c6f = sch.val("c6")
    arv = sch.inc("c1")
    def g_ar(e, idx=idx, ccd1=ccd1, ccs1=ccs1, ccd2=ccd2, c6f=c6f):
        e.wait_ge(sem["c6"], c6f)
        e.dma_start(out=t.ar_in[idx].ap()[0:1, 0:8], in_=t.small[0:1, 8:16]).then_inc(sem["ccd"], 16)
        e.wait_ge(sem["ccd"], ccd1)
        e.collective_compute("AllReduce", OP.max, replica_groups=[list(range(8))],
                             ins=[t.ar_in[idx].ap()], outs=[t.ar_out[idx].ap()]).then_inc(sem["ccs"], 1)
        e.wait_ge(sem["ccs"], ccs1)
        e.dma_start(out=t.small[0:1, 8:16], in_=t.ar_out[idx].ap()[0:1, 0:8]).then_inc(sem["ccd"], 16)
        e.wait_ge(sem["ccd"], ccd2)
        e.nop().then_inc(sem["c1"], 1)
    sch.on("gpsimd", g_ar)
    def v_s(e, arv=arv, scale_fn=scale_fn):
        e.wait_ge(sem["c1"], arv)
        e.drain()
        scale_fn(e)
    c2v = sch.inc("c2")
    sch.on("vector", v_s)
    n, d0 = bccols
    def t_bc(e, c2v=c2v, n=n):
        e.wait_ge(sem["c2"], c2v)
        e.matmul(pB[:, 0:n], t.onesf[0:1, :], t.small[0:1, 0:n], start=True, stop=True).then_inc(sem["c1"], 1)
    c1v = sch.inc("c1")
    sch.on("tensor", t_bc)
    def v_bc(e, c1v=c1v, n=n, d0=d0):
        e.wait_ge(sem["c1"], c1v)
        e.tensor_copy(t.scl[:, d0:d0 + n], pB[:, 0:n]).then_inc(sem["c2"], 1)
    sch.inc("c2")
    sch.on("vector", v_bc)


def _p5_sc3(t):
    CM = CMAGIC
    def sc3(e):
        e.tensor_scalar(t.small[0:1, 0:1], t.small[0:1, 13:14], float(2.0 ** -16 / 32767.0),
                        1e-12, OP.mult, OP.max)
        e.drain()
        e.reciprocal(t.small[0:1, 2:3], t.small[0:1, 0:1])
        e.drain()
        e.tensor_scalar(t.small[0:1, 1:2], t.small[0:1, 2:3], float(2.0 ** -16), None,
                        OP.mult).then_inc(t.sem["c2"], 1)
    return sc3


def _p5_sc4(t):
    def sc4(e):
        e.tensor_scalar(t.small[0:1, 0:1], t.small[0:1, 13:14], 1.0 / 32767.0, 1e-12,
                        OP.mult, OP.max)
        e.drain()
        e.reciprocal(t.small[0:1, 1:2], t.small[0:1, 0:1]).then_inc(t.sem["c2"], 1)
    return sc4


def _p5_sclderiv(nc, t, sch):
    sem = t.sem
    def v_c(e):
        e.drain()
        e.tensor_scalar(t.scl[:, 13:14], t.scl[:, 5:6], -CMAGIC, None, OP.mult)
        e.drain()
        e.tensor_scalar(t.scl[:, 14:15], t.scl[:, 4:5], -CMAGIC, None, OP.mult)
        e.drain()
        e.nop().then_inc(sem["c2"], 1)
    v = sch.inc("c2")
    sch.on("vector", v_c)
    return v


def _p5a_chunk(nc, t, sch, ic, c2c):
    sem = t.sem
    s5 = t.s5
    lsem = ("str", "trs")[ic % 2]
    c7p = sch.val("c7")
    stv = sch.val("stw")
    def s_ld(e, ic=ic, c7p=c7p, stv=stv, lsem=lsem):
        e.wait_ge(sem["c7"], max(0, c7p - 1))
        if ic >= 2:
            e.wait_ge(sem["stw"], stv - 16)
        e.dma_start(out=s5[4 + ic % 2][:, :], in_=t.stash_gate.ap()[ic, :, :]).then_inc(sem[lsem], 16)
    ldv = sch.inc(lsem, 16)
    sch.on("sync", s_ld)
    c7w = sch.val("c7")
    c9w = sch.val("c9")
    def a_blk(e, ic=ic, ldv=ldv, lsem=lsem, c7w=c7w, c9w=c9w, c2c=c2c):
        e.wait_ge(sem[lsem], ldv)
        e.wait_ge(sem["c7"], c7w)
        e.wait_ge(sem["c9"], c9w)
        e.wait_ge(sem["c2"], c2c)
        e.activation(s5[6][:, :], s5[4 + ic % 2][:, :], ACT.Identity,
                     bias=t.scl[:, 12:13], scale=t.scl[:, 7:8])
        e.drain()
        e.activation(s5[7][:, :], s5[6][:, :], ACT.Identity,
                     bias=t.scl[:, 13:14], scale=t.scl[:, 5:6])
        e.drain()
        e.activation(t.slotA[:, :], s5[7][:, :], ACT.Sigmoid)
        e.drain()
        e.activation(t.slotB[:, :], t.slotA[:, :], ACT.Identity,
                     bias=t.scl[:, 12:13], scale=65536.0).then_inc(sem["c1"], 1)
    c1v = sch.inc("c1")
    sch.on("scalar", a_blk)
    def v_blk(e, ic=ic, c1v=c1v):
        e.wait_ge(sem["c1"], c1v)
        e.drain()
        e.scalar_tensor_tensor(t.slotA[:, :], t.slotB[:, :], CMAGIC + 65535.0,
                               t.scl[:, 12:13].broadcast_to((128, 2048)),
                               OP.min, OP.subtract).then_inc(sem["c9"], 1)
        e.drain()
        e.tensor_tensor(s5[4 + ic % 2][:, :], s5[7][:, :], t.slotA[:, :],
                        OP.mult).then_inc(sem["c7"], 1)
    sch.inc("c9")
    c7v = sch.inc("c7")
    sch.on("vector", v_blk)
    def v_2r(e, ic=ic, c7v=c7v):
        e.wait_ge(sem["c7"], c7v)
        e.drain()
        e.tensor_reduce(t.mxcol[:, ic:ic + 1], s5[4 + ic % 2][:, :], AX.X, OP.max,
                        apply_absolute_value=True).then_inc(sem["c6"], 1)
    c6v = sch.inc("c6")
    sch.on("vector", v_2r)
    stq = sch.val("stw")
    def g_st(e, ic=ic, c6v=c6v, stq=stq):
        e.wait_ge(sem["c6"], c6v)
        e.wait_ge(sem["stw"], stq)
        e.dma_start(out=t.stash_g2.ap()[ic, :, :], in_=s5[4 + ic % 2][:, :]).then_inc(sem["stw"], 16)
    sch.inc("stw", 16)
    sch.on("gpsimd", g_st)


def _p5b_chunk(nc, t, sch, ic, c2c):
    sem = t.sem
    s5 = t.s5
    upb = (t.slotA, t.slotB)[ic % 2]
    lsem = ("str", "trs")[ic % 2]
    c7p = sch.val("c7")
    stv = sch.val("stw")
    lag = 1 if ic >= 2 else 0
    def s_ld(e, ic=ic, c7p=c7p, stv=stv, lsem=lsem, lag=lag, upb=upb):
        e.wait_ge(sem["c7"], max(0, c7p - lag))
        e.wait_ge(sem["stw"], max(0, stv - 16))
        e.dma_start(out=s5[4 + ic % 2][:, :], in_=t.stash_g2.ap()[ic, :, :]).then_inc(sem[lsem], 16)
        e.dma_start(out=upb[:, :], in_=t.stash_up.ap()[ic, :, :]).then_inc(sem[lsem], 16)
    ldv = sch.inc(lsem, 32)
    sch.on("sync", s_ld)
    c7w = sch.val("c7")
    def a_blk(e, ic=ic, ldv=ldv, lsem=lsem, c7w=c7w, c2c=c2c, upb=upb):
        e.wait_ge(sem[lsem], ldv)
        e.wait_ge(sem["c7"], c7w)
        e.wait_ge(sem["c2"], c2c)
        e.activation(s5[6][:, :], s5[4 + ic % 2][:, :], ACT.Identity,
                     bias=t.scl[:, 12:13], scale=t.scl[:, 9:10])
        e.drain()
        e.activation(s5[7][:, :], s5[6][:, :], ACT.Identity,
                     bias=t.scl[:, 15:16], scale=t.scl[:, 8:9])
        e.drain()
        e.activation(s5[6][:, :], upb[:, :], ACT.Identity,
                     bias=t.scl[:, 12:13], scale=t.scl[:, 6:7])
        e.drain()
        e.activation(upb[:, :], s5[6][:, :], ACT.Identity,
                     bias=t.scl[:, 14:15], scale=t.scl[:, 4:5]).then_inc(sem["c1"], 1)
    c1v = sch.inc("c1")
    sch.on("scalar", a_blk)
    def v_blk(e, ic=ic, c1v=c1v, upb=upb):
        e.wait_ge(sem["c1"], c1v)
        e.drain()
        e.tensor_tensor(s5[4 + ic % 2][:, :], s5[7][:, :], upb[:, :],
                        OP.mult).then_inc(sem["c7"], 1)
    c7v = sch.inc("c7")
    sch.on("vector", v_blk)
    def v_2r(e, ic=ic, c7v=c7v):
        e.wait_ge(sem["c7"], c7v)
        e.drain()
        e.tensor_reduce(t.mxcol[:, ic:ic + 1], s5[4 + ic % 2][:, :], AX.X, OP.max,
                        apply_absolute_value=True).then_inc(sem["c6"], 1)
    c6v = sch.inc("c6")
    sch.on("vector", v_2r)
    stq = sch.val("stw")
    def g_st(e, ic=ic, c6v=c6v, stq=stq):
        e.wait_ge(sem["c6"], c6v)
        e.wait_ge(sem["stw"], stq)
        e.dma_start(out=t.stash_o.ap()[ic, :, :], in_=s5[4 + ic % 2][:, :]).then_inc(sem["stw"], 16)
    sch.inc("stw", 16)
    sch.on("gpsimd", g_st)


def _p5_arwin(nc, t, sch, idx, scale_fn, bccols, post_fn=None):
    sem = t.sem
    a2g = sch.val("a2")
    _p5_finmax(nc, t, sch, 13, a2g)
    _p5_ar(nc, t, sch, idx, None, scale_fn, bccols)
    if post_fn is not None:
        sch.on("vector", post_fn)
    return [("c2", sch.val("c2"))]


def _p5_dequant(nc, t, sch, agccd, c8proj):
    sem = t.sem
    din = t.din
    pA, pB = t.pA, t.pB
    s5 = t.s5
    ldp = sch.val("ldx")
    stwp = sch.val("stw")
    c7b = sch.val("c7")
    def s_ind(e, ldp=ldp, stwp=stwp, c7b=c7b, agccd=agccd):
        e.wait_ge(sem["c7"], c7b)
        e.wait_ge(sem["stw"], stwp)
        e.wait_ge(sem["ccd"], agccd)
        e.dma_start(out=t.ind96[:, :], in_=din["ind96"]).then_inc(sem["ldx"], 16)
    ldxv = sch.inc("ldx", 16)
    sch.on("sync", s_ind)
    n2p0 = sch.val("n2")
    def v_rtd(e, n2p0=n2p0):
        e.drain()
        e.reciprocal(t.rTd[:, :], t.sTd[:, :]).then_inc(sem["c6"], 1)
    c6rtd = sch.inc("c6")
    sch.on("vector", v_rtd)
    for ic in range(NIC):
        c7p = sch.val("c7")
        def s_ld(e, ic=ic, c7p=c7p, c8proj=c8proj):
            e.wait_ge(sem["c7"], max(0, c7p - 1))
            if ic < 2:
                e.wait_ge(sem["c8"], c8proj)
            e.dma_start(out=s5[4 + ic % 2][:, 0:1024], in_=din["wdownT"][128 * ic:128 * (ic + 1), :]
                        ).then_inc(sem[("str", "trs")[ic % 2]], 16)
        ldv = sch.inc(("str", "trs")[ic % 2], 16)
        sch.on("sync", s_ld)
        n2p = sch.val("n2")
        def t_bc(e, ic=ic, n2p=n2p, ldxv=ldxv, c6rtd=c6rtd):
            e.wait_ge(sem["n2"], n2p)
            e.wait_ge(sem["ldx"], ldxv)
            e.wait_ge(sem["c6"], c6rtd)
            e.matmul(pB[:, 0:512], t.ind96[:, 128 * ic:128 * (ic + 1)], t.rTd[:, 0:512],
                     start=True, stop=True)
            e.matmul(pB[:, 512:1024], t.ind96[:, 128 * ic:128 * (ic + 1)], t.rTd[:, 512:1024],
                     start=True, stop=True)
            e.matmul(pB[:, 1024:1536], t.ind96[:, 128 * ic:128 * (ic + 1)], t.sTd[:, 0:512],
                     start=True, stop=True)
            e.matmul(pB[:, 1536:2048], t.ind96[:, 128 * ic:128 * (ic + 1)], t.sTd[:, 512:1024],
                     start=True, stop=True).then_inc(sem["n1"], 1)
        n1v = sch.inc("n1")
        sch.on("tensor", t_bc)
        stp = sch.val("stw")
        def v_dq(e, ic=ic, ldv=ldv, n1v=n1v, stp=stp):
            e.wait_ge(sem[("str", "trs")[ic % 2]], ldv)
            e.wait_ge(sem["n1"], n1v)
            e.wait_ge(sem["stw"], stp)
            e.drain()
            e.tensor_tensor(s5[6][:, 0:1024], s5[4 + ic % 2][:, 0:1024], pB[:, 0:1024], OP.mult)
            e.drain()
            e.tensor_scalar(s5[6][:, 0:1024], s5[6][:, 0:1024], CMAGIC, CMAGIC, OP.add, OP.subtract)
            e.drain()
            e.tensor_tensor(t.wdq_s[:, 1024 * (ic % 2):1024 * (ic % 2 + 1)], s5[6][:, 0:1024],
                            pB[:, 1024:2048], OP.mult).then_inc(sem["n2"], 1)
            e.drain()
            e.nop().then_inc(sem["c7"], 1)
        sch.inc("n2")
        sch.inc("c7")
        sch.on("vector", v_dq)
        def g_st(e, ic=ic):
            pass
        n2v2 = sch.val("n2")
        def g_st2(e, ic=ic, n2v2=n2v2):
            e.wait_ge(sem["n2"], n2v2)
            e.dma_start(out=t.stash_wdq.ap()[ic, :, :],
                        in_=t.wdq_s[:, 1024 * (ic % 2):1024 * (ic % 2 + 1)]).then_inc(sem["stw"], 16)
        sch.inc("stw", 16)
        sch.on("gpsimd", g_st2)


def _p5_main(nc, t, sch, agccd, c7p5b):
    sem = t.sem
    din = t.din
    pA, pB = t.pA, t.pB
    s5 = t.s5
    stwdq = sch.val("stw")
    oqv_map = {}
    aoq_map = {}
    dn_map = {}

    def emit_oq(tqv, ic):
        lsem = ("lda", "ldb")[ic % 2]
        a1p = aoq_map.get((tqv, ic - 2) if ic >= 2 else (tqv - 1, ic + 22), 0)
        def s_lo(e, tqv=tqv, ic=ic, lsem=lsem, a1p=a1p):
            e.wait_ge(sem["c1"], a1p)
            if tqv == 0 and ic < 2:
                e.wait_ge(sem["c7"], c7p5b)
            e.dma_start(out=t.slotA[:, 512 * (ic % 2):512 * (ic % 2 + 1)],
                        in_=t.stash_o.ap()[ic, :, 512 * tqv:512 * (tqv + 1)]
                        ).then_inc(sem[lsem], 16)
        ldv = sch.inc(lsem, 16)
        sch.on("sync", s_lo)
        c2q = oqv_map.get((tqv, ic - 2) if ic >= 2 else (tqv - 1, ic + 22), 0)
        def a_oq(e, tqv=tqv, ic=ic, ldv=ldv, lsem=lsem, c2q=c2q):
            e.wait_ge(sem[lsem], ldv)
            e.wait_ge(sem["c2"], c2q)
            e.activation(t.slotB[:, 512 * (ic % 2):512 * (ic % 2 + 1)],
                         t.slotA[:, 512 * (ic % 2):512 * (ic % 2 + 1)], ACT.Identity,
                         bias=t.scl[:, 12:13], scale=t.scl[:, 11:12]).then_inc(sem["c1"], 1)
        aoq_map[(tqv, ic)] = sch.inc("c1")
        sch.on("scalar", a_oq)
        dnw = dn_map.get((tqv - 1, ic), 0)
        def v_oq(e, tqv=tqv, ic=ic, dnw=dnw, c1v=aoq_map[(tqv, ic)], agccd=agccd):
            e.wait_ge(sem["c1"], c1v)
            e.wait_ge(sem["c8"], dnw)
            if tqv == 0 and ic == 0:
                e.wait_ge(sem["ccd"], agccd)
            e.drain()
            e.tensor_scalar(t.oqbuf[:, 512 * ic:512 * (ic + 1)],
                            t.slotB[:, 512 * (ic % 2):512 * (ic % 2 + 1)], CMAGIC,
                            t.scl[:, 10:11], OP.subtract, OP.mult).then_inc(sem["c2"], 1)
        oqv_map[(tqv, ic)] = sch.inc("c2")
        sch.on("vector", v_oq)

    for ic in range(NIC):
        emit_oq(0, ic)

    for tqv in range(4):
        evp = sch.val("c5")
        c5q = sch.val("c5")
        for ic in range(NIC):
            c8p = sch.val("c8")
            def s_ldw(e, ic=ic, tqv=tqv, c8p=c8p, stwdq=stwdq, c5q=c5q):
                e.wait_ge(sem["c8"], max(0, c8p - 1))
                if ic < 2:
                    if tqv == 0:
                        e.wait_ge(sem["stw"], stwdq)
                    else:
                        e.wait_ge(sem["c5"], c5q)
                e.dma_start(out=t.wdq_s[:, 1024 * (ic % 2):1024 * (ic % 2 + 1)],
                            in_=t.stash_wdq.ap()[ic, :, :]).then_inc(sem[("str", "trs")[ic % 2]], 16)
            ldv = sch.inc(("str", "trs")[ic % 2], 16)
            sch.on("sync", s_ldw)
            def t_dn(e, ic=ic, ldv=ldv, oqw=oqv_map[(tqv, ic)], evp=evp):
                e.wait_ge(sem[("str", "trs")[ic % 2]], ldv)
                e.wait_ge(sem["c2"], oqw)
                if ic == 0:
                    e.wait_ge(sem["c5"], evp)
                mm = None
                for ts_ in range(4):
                    lhs = t.oqbuf[:, 512 * ic + 128 * ts_:512 * ic + 128 * (ts_ + 1)]
                    e.ldweights(lhs)
                    ps = (pA if ts_ < 2 else pB)
                    for hf in range(2):
                        mm = e.matmul(ps[:, 1024 * (ts_ % 2) + 512 * hf:1024 * (ts_ % 2) + 512 * (hf + 1)],
                                      lhs, t.wdq_s[:, 1024 * (ic % 2) + 512 * hf:1024 * (ic % 2) + 512 * (hf + 1)],
                                      start=(ic == 0), stop=(ic == NIC - 1))
                mm.then_inc(sem["c8"], 1)
            dn_map[(tqv, ic)] = sch.inc("c8")
            sch.on("tensor", t_dn)
            if tqv < 3:
                emit_oq(tqv + 1, ic)
        dnv = sch.val("c8")
        for ts_ in range(4):
            tt = 4 * tqv + ts_
            stop_ = sch.val("sto")
            c5g = sch.val("c5")
            def s_lx(e, tt=tt, stop_=stop_, dnv=dnv, c5g=c5g):
                e.wait_ge(sem["sto"], max(0, stop_ - 32))
                e.wait_ge(sem["c8"], dnv)
                e.wait_ge(sem["c5"], c5g)
                e.dma_start(out=s5[7][:, 0:1024], in_=din["x_own"][128 * tt:128 * (tt + 1), :]
                            ).then_inc(sem["str"], 16)
            ldv = sch.inc("str", 16)
            sch.on("sync", s_lx)
            stov = sch.val("sto")
            def v_ev(e, ts_=ts_, tt=tt, ldv=ldv, dnv=dnv, stov=stov):
                e.wait_ge(sem["c8"], dnv)
                e.wait_ge(sem["str"], ldv)
                e.wait_ge(sem["sto"], stov)
                e.drain()
                ps = (pA if ts_ < 2 else pB)
                src = ps[:, 1024 * (ts_ % 2):1024 * (ts_ % 2 + 1)]
                at = (t.aout_a if tt < 8 else t.aout_b)[:, D * (tt % 8):D * (tt % 8 + 1)]
                e.scalar_tensor_tensor(s5[6][:, 0:1024], src, 1.0, at, OP.mult, OP.add)
                e.drain()
                e.tensor_tensor(s5[6][:, 1024:2048], s5[6][:, 0:1024], s5[7][:, 0:1024],
                                OP.add).then_inc(sem["c5"], 1)
            c5v = sch.inc("c5")
            sch.on("vector", v_ev)
            def g_out(e, tt=tt, c5v=c5v):
                e.wait_ge(sem["c5"], c5v)
                e.dma_start(out=t.out_own[128 * tt:128 * (tt + 1), :],
                            in_=s5[6][:, 1024:2048]).then_inc(sem["sto"], 16)
            sch.inc("sto", 16)
            sch.on("gpsimd", g_out)
    stof = sch.val("sto")
    def g_fin(e, stof=stof):
        e.wait_ge(sem["sto"], stof)
    sch.on("gpsimd", g_fin)


def gen_dbg(nc, t, sch):
    sem = t.sem
    items = []
    if "kT" in t.debug: items.append((t.dout["kT"], t.kT[:, :]))
    if "qT" in t.debug: items.append((t.dout["qT"], t.qT[:, :]))
    if "vA" in t.debug: items.append((t.dout["vA"], t.vA[:, :]))
    if "xq" in t.debug: items.append((t.dout["xq"], t.xq_fm[:, :]))
    if "scl" in t.debug: items.append((t.dout["scl"], t.scl[:, :]))
    if "small" in t.debug: items.append((t.dout["small"], t.small[:, :]))
    if "ctxT" in t.debug: items.append((t.dout["ctxT"], t.qT[:, :]))
    if "ctxown" in t.debug: items.append((t.dout["ctxown"], t.ctx_own[:, :]))
    def g_dump(e):
        for dst, src in items:
            e.dma_start(out=dst, in_=src).then_inc(sem["sto"], 16)
        for nm in ("upst", "gatest", "g2st", "ost", "wdqst"):
            if nm in t.debug:
                stash = {"upst": t.stash_up, "gatest": t.stash_gate, "g2st": t.stash_g2,
                         "ost": t.stash_o, "wdqst": t.stash_wdq}[nm]
                e.dma_start(out=t.dout[nm], in_=stash.ap()).then_inc(sem["sto"], 16)
        if "aout" in t.debug:
            for i in range(16):
                src = (t.aout_a if i < 8 else t.aout_b)[:, D * (i % 8):D * (i % 8 + 1)]
                e.dma_start(out=t.dout["aout"][128 * i:128 * (i + 1), :], in_=src).then_inc(sem["sto"], 16)
        e.wait_ge(sem["sto"], stv)
    nst = len(items) + sum(1 for nm in ("upst", "gatest", "g2st", "ost", "wdqst") if nm in t.debug)
    if "aout" in t.debug:
        nst += 16
    stv = sch.inc("sto", 16 * nst)
    sch.on("gpsimd", g_dump)


def prep_inputs(L0_attn_x, L0_attn_w_kqv, L0_attn_w_proj, L0_attn_b_proj,
                L0_ffn_w_gate, L0_ffn_w_up, L0_ffn_w_down):
    x = np.ascontiguousarray(np.asarray(L0_attn_x, dtype=np.float32))
    wkqv = np.asarray(L0_attn_w_kqv, dtype=np.float32).reshape(16, 64, 3, D)
    wproj = np.asarray(L0_attn_w_proj, dtype=np.float32)
    bproj = np.asarray(L0_attn_b_proj, dtype=np.float32)
    wup = np.ascontiguousarray(np.asarray(L0_ffn_w_up, dtype=np.float32))
    wgate = np.ascontiguousarray(np.asarray(L0_ffn_w_gate, dtype=np.float32))
    wdown = np.ascontiguousarray(np.asarray(L0_ffn_w_down, dtype=np.float32))
    masks, ident, ones64, ind32, ind96, onesf = make_consts()

    wupT = np.ascontiguousarray(wup.T)
    wgateT = np.ascontiguousarray(wgate.T)
    wdownT = np.ascontiguousarray(wdown.T)

    in_maps = []
    for c in range(8):
        b, g = c // 2, c % 2
        wk = wkqv[8 * g:8 * (g + 1), :, 0, :].reshape(512, D)
        wq = wkqv[8 * g:8 * (g + 1), :, 1, :].reshape(512, D)
        wv = wkqv[8 * g:8 * (g + 1), :, 2, :].reshape(512, D)
        wprojT_c = np.concatenate([wproj[:, 512 * g:512 * (g + 1)].T,
                                   wproj[:, 512 * (1 - g):512 * (2 - g)].T], axis=0)
        in_maps.append({
            "x_seq": x[b],
            "x_own": np.ascontiguousarray(x[b, TOWN * g:TOWN * (g + 1), :]),
            "wkT": np.ascontiguousarray(wk.T), "wqT": np.ascontiguousarray(wq.T),
            "wvT": np.ascontiguousarray(wv.T),
            "wprojT": np.ascontiguousarray(wprojT_c),
            "b_row": bproj.reshape(1, D),
            "wupN": wup, "wupT": wupT, "wgateN": wgate, "wgateT": wgateT,
            "wdownN": wdown, "wdownT": wdownT,
            "masks": masks, "ident": ident, "ones64": ones64, "ind32": ind32, "ind96": ind96, "onesf": onesf,
            "offs": np.array([[16 * g, c, c ^ 1, g,
                               2 * g, 2 * g + 1, 4 + 2 * g, 5 + 2 * g]], dtype=np.int32),
        })
    return in_maps


def kernel(**inputs):
    in_maps = prep_inputs(**inputs)
    nc, t = build()
    res = run_bass_kernel_spmd(nc, in_maps, core_ids=list(range(8)))
    out = np.empty((4, S, D), dtype=np.float32)
    for c in range(8):
        b, g = c // 2, c % 2
        out[b, TOWN * g:TOWN * (g + 1), :] = res.results[c]["out_own"]
    return out


# version 16
# speedup vs baseline: 1.0031x; 1.0031x over previous
import numpy as np
import contextlib
import ml_dtypes
import concourse.bass as bass
import concourse.mybir as mybir
from concourse.bass_utils import run_bass_kernel_spmd

F32 = mybir.dt.float32
BF16 = mybir.dt.bfloat16
I32 = mybir.dt.int32
OP = mybir.AluOpType
AX = mybir.AxisListType
ACT = mybir.ActivationFunctionType

CMAGIC = float(np.float32(1.5 * 2 ** 23))

S = 4096
D = 1024
TOWN = 2048
NPAIR = 4
NKC = S // 128
NQT = S // 512
FFI = 3072
NIC = FFI // 128

SB_BASE = 16512
OFF_KT = SB_BASE
OFF_QT = SB_BASE + 32768
OFF_VA = SB_BASE + 65536
OFF_XNFM = SB_BASE + 98816
OFF_XQ = OFF_XNFM
OFF_CTXOWN = SB_BASE + 131584
OFF_FIX = SB_BASE + 164352
OFF_STREAM = SB_BASE + 186368


class Bag:
    pass


class Sched:
    def __init__(self):
        self.steps = []
        self.cnt = {}

    def on(self, eng, fn):
        self.steps.append((eng, fn))

    def inc(self, name, k=1):
        self.cnt[name] = self.cnt.get(name, 0) + k
        return self.cnt[name]

    def val(self, name):
        return self.cnt.get(name, 0)


def make_consts():
    masks = np.zeros((128, 4 * 512), dtype=ml_dtypes.bfloat16)
    p = np.arange(128)[:, None]
    q = np.arange(512)[None, :]
    for dv in range(4):
        masks[:, 512 * dv:512 * (dv + 1)] = (q >= 128 * dv + p).astype(ml_dtypes.bfloat16)
    ident = np.eye(128, dtype=np.float32)
    ones64 = np.zeros((128, 2), dtype=ml_dtypes.bfloat16)
    ones64[0:64, 0] = 1.0
    ones64[64:128, 1] = 1.0
    ind32 = np.zeros((32, 1024), dtype=np.float32)
    for cc in range(8):
        for pp in range(128):
            ind32[4 * cc + pp // 32, 128 * cc + pp] = 1.0
    ind96 = np.zeros((96, 3072), dtype=np.float32)
    for ic in range(24):
        for pp in range(128):
            ind96[4 * ic + pp // 32, 128 * ic + pp] = 1.0
    onesf = np.ones((128, 128), dtype=np.float32)
    return masks, ident, ones64, ind32, ind96, onesf


def build(debug=(), phases_upto=99):
    nc = bass.Bass("TRN2", target_bir_lowering=False)
    t = Bag()
    t.debug = set(debug)

    din = {}
    def di(name, shape, dtype=F32):
        din[name] = nc.dram_tensor(name, list(shape), dtype, kind="ExternalInput").ap()
    di("x_seq", (S, D))
    di("x_own", (TOWN, D))
    di("wkT", (D, 512)); di("wqT", (D, 512)); di("wvT", (D, 512))
    di("wprojT", (D, D))
    di("b_row", (1, D))
    di("wupN", (FFI, D)); di("wupT", (D, FFI))
    di("wgateN", (FFI, D)); di("wgateT", (D, FFI))
    di("wdownN", (D, FFI)); di("wdownT", (FFI, D))
    di("masks", (128, 2048), BF16)
    di("ident", (128, 128)); di("ones64", (128, 2), BF16); di("ind32", (32, 1024)); di("ind96", (96, 3072)); di("onesf", (128, 128))
    di("offs", (1, 8), I32)
    t.din = din
    t.out_own = nc.dram_tensor("out_own", [TOWN, D], F32, kind="ExternalOutput").ap()

    t.stash_up = nc.dram_tensor("stash_up", [NIC, 128, TOWN], F32, kind="Internal")
    t.stash_gate = nc.dram_tensor("stash_gate", [NIC, 128, TOWN], F32, kind="Internal")
    t.stash_g2 = nc.dram_tensor("stash_g2", [NIC, 128, TOWN], F32, kind="Internal")
    t.stash_o = nc.dram_tensor("stash_o", [NIC, 128, TOWN], F32, kind="Internal")
    t.stash_wdq = nc.dram_tensor("stash_wdq", [NIC, 128, D], BF16, kind="Internal")
    t.ar_in = [nc.dram_tensor(f"ar_in{i}", [1, 8], F32, kind="Internal") for i in range(4)]
    t.ar_out = [nc.dram_tensor(f"ar_out{i}", [1, 8], F32, kind="Internal") for i in range(4)]
    t.lnc_hbm = nc.dram_tensor("lnc_hbm", [8, 128, 8], F32, kind="Internal")
    t.ag_in = nc.dram_tensor("ag_in", [4, 128, S], BF16, kind="Internal")
    t.ag_out = nc.dram_tensor("ag_out", [8, 4, 128, S], BF16, kind="Internal")

    t.dout = {}
    def do(name, shape, dtype):
        if name in t.debug:
            t.dout[name] = nc.dram_tensor("dbg_" + name, list(shape), dtype, kind="ExternalOutput").ap()
    do("kT", (128, 4 * S), BF16)
    do("qT", (128, 4 * S), BF16)
    do("vA", (128, NKC * 520), BF16)
    do("xq", (128, 8 * TOWN), BF16)
    do("scl", (128, 32), F32)
    do("small", (128, 16), F32)
    do("ctxT", (128, 4 * S), BF16)
    do("ctxown", (128, 8 * TOWN), BF16)
    do("aout", (TOWN, D), F32)
    do("upst", (NIC, 128, TOWN), F32)
    do("gatest", (NIC, 128, TOWN), F32)
    do("g2st", (NIC, 128, TOWN), F32)
    do("ost", (NIC, 128, TOWN), F32)
    do("wdqst", (NIC, 128, D), BF16)

    es = contextlib.ExitStack()
    with es:
        def sb(name, shape, dtype, offset):
            return nc.alloc_sbuf_tensor_at(name, list(shape), dtype, offset=offset)

        t.kT = sb("kT", (128, 4 * S), BF16, OFF_KT)
        t.qT = sb("qT", (128, 4 * S), BF16, OFF_QT)
        t.vA = sb("vA", (128, NKC * 520), BF16, OFF_VA)
        t.xn_fm = sb("xn_fm", (128, 8 * TOWN), BF16, OFF_XNFM)
        t.wkqv = sb("wkqv", (128, 2 * 8 * 512), BF16, OFF_CTXOWN)
        t.wvb = sb("wvb", (128, 8 * 512), BF16, OFF_CTXOWN + 16384)
        t.xq_fm = sb("xq_fm", (128, 8 * TOWN), BF16, OFF_XQ)
        t.shift_sb = sb("shift_sb", (128, 1024), F32, OFF_XQ)
        t.praw32 = sb("praw32", (128, 1024), F32, OFF_XQ + 4096)
        t.pfin2 = sb("pfin2", (128, 2048), BF16, OFF_XQ + 8192)
        t.slotA = sb("slotA", (128, 2048), F32, OFF_XQ + 12288)
        t.slotB = sb("slotB", (128, 2048), F32, OFF_XQ + 20480)
        t.praw32b = sb("praw32b", (128, 1024), F32, OFF_XQ + 28672)
        t.ctx_own = sb("ctx_own", (128, 8 * TOWN), BF16, OFF_CTXOWN)
        t.wprojT = sb("wprojT", (128, 8 * D), BF16, OFF_VA)
        t.aout_a = sb("aout_a", (128, 8 * D), BF16, OFF_KT)
        t.aout_b = sb("aout_b", (128, 8 * D), BF16, OFF_KT + 16384)
        t.sTu = sb("sTu", (32, FFI), F32, OFF_CTXOWN)
        t.rTu = sb("rTu", (32, FFI), F32, OFF_CTXOWN + 12288)
        t.s5 = [sb(f"s5_{i}", (128, 2048), F32, OFF_QT + 8192 * i) for i in range(4)] + \
               [sb(f"s5_{i+4}", (128, 2048), F32, OFF_CTXOWN + 8192 * i) for i in range(4)]
        t.sall = sb("sall", (128, 768), F32, OFF_CTXOWN + 24576)
        t.wqb = sb("wqb", (128, 8 * 512), BF16, OFF_STREAM)
        t.ebuf0 = sb("ebuf0", (128, 1024), F32, OFF_STREAM + 8192)
        t.ebuf1 = sb("ebuf1", (128, 1024), F32, OFF_STREAM + 12288)
        t.oqbuf = sb("oqbuf", (128, NIC * 512), BF16, OFF_QT + 16384)
        t.wdq_s = sb("wdq_s", (128, 2 * D), BF16, OFF_QT + 16384 + 24576)

        fo = OFF_FIX
        t.masks = sb("masks", (128, 2048), BF16, fo); fo += 4096
        t.ident = sb("ident", (128, 128), F32, fo); fo += 512
        t.ones64 = sb("ones64", (128, 2), BF16, fo); fo += 32
        t.ind32 = sb("ind32", (32, 1024), F32, fo); fo += 4096
        t.ind96 = sb("ind96", (96, 3072), F32, OFF_QT)
        t.rTd = sb("rTd", (96, 1024), F32, OFF_QT + 12288)
        t.onesf = sb("onesf", (128, 128), F32, fo); fo += 512
        t.btile = sb("btile", (128, D), F32, fo); fo += 4096
        t.scl = sb("scl", (128, 32), F32, fo); fo += 128
        t.offs = sb("offs", (1, 8), I32, fo); fo += 32
        t.sTd = sb("sTd", (96, D), F32, fo); fo += 4096
        t.norm2 = sb("norm2", (2, 64), F32, fo); fo += 256
        t.small = sb("small", (128, 16), F32, fo); fo += 64
        t.mxcol = sb("mxcol", (128, 48), F32, fo); fo += 192
        t.negct = sb("negct", (128, 64), F32, fo); fo += 256
        assert fo <= OFF_STREAM, fo

        so = OFF_STREAM
        t.xbuf = sb("xbuf", (128, 2 * D), F32, so)
        t.ctxstage = sb("ctxstage", (128, 1024), F32, so)
        so += 8192
        t.xnbuf = sb("xnbuf", (128, D), F32, so); so += 4096
        t.xnbf = sb("xnbf", (128, 2 * D), BF16, so); so += 4096
        t.stats = sb("stats", (128, 32), F32, so); so += 128
        t.lnt = sb("lnt", (128, 32), F32, so); so += 128
        t.praw = sb("praw", (128, 1024), BF16, so)
        t.pfin = sb("pfin", (128, 1024), BF16, so + 2048)
        t.qdt = sb("qdt", (128, D), F32, so)
        t.rnorm_e = sb("rnorm_e", (128, 1024), F32, so)
        so += 4096
        t.rnorm = sb("rnorm", (128, 512), F32, so); so += 2048
        t.sqs = sb("sqs", (128, 1024), BF16, so); so += 2048
        assert so <= 229344, so

        t.pA = es.enter_context(nc.psum_tensor("pA", [128, 2048], F32))
        t.pB = es.enter_context(nc.psum_tensor("pB", [128, 2048], F32))

        sems = {}
        for name in ["ph", "ldx", "ldw", "lda", "ldb", "tra", "trb", "trs",
                     "c1", "c2", "c3", "c4", "c5", "c6", "c7", "c8", "c9",
                     "n1", "n2", "ccs", "ccd", "stw", "str", "sto",
                     "a1", "a2", "a3", "a4", "a5"]:
            sems[name] = es.enter_context(nc.semaphore(name))
        t.sem = sems

        sch = Sched()
        c2cs = {}
        def midhook(idx, part):
            if idx == 0 and part == 0:
                c2cs["a"] = _p5_sclderiv(nc, t, sch)
            if idx <= 11:
                _p5a_chunk(nc, t, sch, 2 * idx + part, c2cs["a"])
            elif 13 <= idx <= 24:
                _p5b_chunk(nc, t, sch, 2 * (idx - 13) + part, c2cs["b"])
        def prehook(idx):
            if idx == 13:
                def v_c15(e):
                    e.drain()
                    e.tensor_scalar(t.scl[:, 15:16], t.scl[:, 8:9], -CMAGIC, None,
                                    OP.mult).then_inc(t.sem["c2"], 1)
                r = _p5_arwin(nc, t, sch, 2, _p5_sc3(t), (2, 8), post_fn=v_c15)
                c2cs["b"] = sch.inc("c2")
                return r
            if idx == 26:
                return _p5_arwin(nc, t, sch, 3, _p5_sc4(t), (2, 10))
            return []
        def gen_p3x(nc, t, sch):
            gen_p3(nc, t, sch, midhook=midhook, prehook=prehook)
        def gen_p45(nc, t, sch):
            c7p5b = sch.val("c7")
            agccd, ccs_ag = _p4_ag_start(nc, t, sch)
            agv = _p4_extract(nc, t, sch, ccs_ag)
            _p4_proj(nc, t, sch, agv)
            c8proj = sch.val("c8")
            _p5_dequant(nc, t, sch, agccd, c8proj)
            _p5_main(nc, t, sch, agccd, c7p5b)
        gens = [gen_p0, gen_p1, gen_p2a, gen_p2b, gen_p3x, gen_p45]
        nph = 0
        for gi, gen in enumerate(gens):
            if gi >= phases_upto:
                break
            gen(nc, t, sch)
            sch.steps.append(("BARRIER", dict(sch.cnt)))
            nph += 1
        gen_dbg(nc, t, sch)
        sch.steps.append(("BARRIER", dict(sch.cnt)))
        nph += 1

        block = es.enter_context(nc.Block())

        SEM_ISSUER = {"lda": "sync", "ldb": "sync", "ldx": "sync", "ldw": "gpsimd",
                      "tra": "scalar", "trb": "scalar", "trs": "sync",
                      "ccd": "gpsimd", "stw": "gpsimd", "str": "sync", "sto": "gpsimd"}

        def runner(eng):
            def body(e):
                k = 0
                for s_eng, fn in sch.steps:
                    if s_eng == "BARRIER":
                        k += 1
                        for sname, issuer in SEM_ISSUER.items():
                            if issuer == eng and fn.get(sname, 0) > 0:
                                e.wait_ge(sems[sname], fn[sname])
                        e.drain()
                        e.nop().then_inc(sems["ph"], 1)
                        e.wait_ge(sems["ph"], 5 * k)
                    elif s_eng == eng:
                        fn(e)
                return None
            return body

        block.sync(runner("sync"))
        block.scalar(runner("scalar"))
        block.vector(runner("vector"))
        block.tensor(runner("tensor"))
        block.gpsimd(runner("gpsimd"))

    return nc, t


def gen_p0(nc, t, sch):
    sem = t.sem
    din = t.din

    ldx0 = sch.val("ldx")
    def s_loads(e):
        e.dma_start(out=t.masks[:, :], in_=din["masks"]).then_inc(sem["ldx"], 16)
        e.dma_start(out=t.ident[:, :], in_=din["ident"]).then_inc(sem["ldx"], 16)
        e.dma_start(out=t.ones64[:, :], in_=din["ones64"]).then_inc(sem["ldx"], 16)
        e.dma_start(out=t.ind32[:, :], in_=din["ind32"]).then_inc(sem["ldx"], 16)
        e.dma_start(out=t.onesf[:, :], in_=din["onesf"]).then_inc(sem["ldx"], 16)
        e.dma_start(out=t.offs[:, :], in_=din["offs"]).then_inc(sem["ldx"], 16)
        e.dma_start(out=t.xbuf[0:1, 0:D], in_=din["b_row"]).then_inc(sem["ldx"], 16)
    for _ in range(7):
        sch.inc("ldx", 16)
    sch.on("sync", s_loads)
    v_ld = sch.val("ldx")

    def g_loads(e):
        e.dma_start(out=t.wkqv[:, 0:8 * 512].rearrange("p (c f) -> p c f", c=8),
                    in_=din["wkT"].rearrange("(c p) f -> p c f", p=128)).then_inc(sem["ldw"], 16)
        e.dma_start(out=t.wkqv[:, 8 * 512:16 * 512].rearrange("p (c f) -> p c f", c=8),
                    in_=din["wqT"].rearrange("(c p) f -> p c f", p=128)).then_inc(sem["ldw"], 16)
        e.dma_start(out=t.wvb[:, :].rearrange("p (c f) -> p c f", c=8),
                    in_=din["wvT"].rearrange("(c p) f -> p c f", p=128)).then_inc(sem["ldw"], 16)
        ones_ap = t.vA[:, :].rearrange("p (c h e) -> p c h e", c=NKC, h=8)[:, :, :, 64:65]
        e.memset(ones_ap, 1.0)
        e.memset(t.scl[:, :], 0.0)
        e.memset(t.small[:, :], 0.0)
        e.drain()
        e.memset(t.scl[:, 12:13], CMAGIC)
    for _ in range(3):
        sch.inc("ldw", 16)
    sch.on("gpsimd", g_loads)

    def t_btile(e):
        e.wait_ge(sem["ldx"], v_ld)
        e.matmul(t.pA[:, 0:512], t.onesf[0:1, :], t.xbuf[0:1, 0:512], start=True, stop=True)
        e.matmul(t.pA[:, 512:1024], t.onesf[0:1, :], t.xbuf[0:1, 512:1024],
                 start=True, stop=True).then_inc(sem["c1"], 1)
    c1v = sch.inc("c1")
    sch.on("tensor", t_btile)

    def v_btile(e):
        e.wait_ge(sem["c1"], c1v)
        e.tensor_copy(t.btile[:, :], t.pA[:, 0:1024])
    sch.on("vector", v_btile)


def gen_ln(nc, t, sch, tiles, x_in_fn, fm_dst_fn, absmax_base=None, qdq=False,
           loader=("sync", ("lda", "ldb")), cache_to=None, stats_cached=False):
    sem = t.sem
    NG = tiles // 8
    for g in range(NG):
        if stats_cached:
            if g == 0:
                def g_push(e):
                    e.dma_start(out=t.lnc_hbm.ap().rearrange("b p c -> p b c"),
                                in_=t.negct[:, 0:64].rearrange("p (b c) -> p b c", c=8)
                                ).then_inc(sem["ccd"], 16)
                sch.inc("ccd", 16)
                sch.on("gpsimd", g_push)
            pushv = sch.val("ccd")
            c3c = sch.val("c3")
            def g_stat(e, g=g, c3c=c3c, pushv=pushv):
                e.wait_ge(sem["ccd"], pushv)
                e.wait_ge(sem["c3"], c3c)
                mnoff = e.value_load(t.offs[0:1, (4 if g == 0 else 5):(5 if g == 0 else 6)])
                nmoff = e.value_load(t.offs[0:1, (6 if g == 0 else 7):(7 if g == 0 else 8)])
                e.dma_start(out=t.lnt[:, 0:8],
                            in_=t.lnc_hbm.ap()[bass.ts(mnoff, 1), :, :]).then_inc(sem["ccd"], 16)
                e.dma_start(out=t.lnt[:, 24:32],
                            in_=t.lnc_hbm.ap()[bass.ts(nmoff, 1), :, :]).then_inc(sem["ccd"], 16)
            ldsv = sch.inc("ccd", 32)
            sch.on("gpsimd", g_stat)
            for ji in range(8):
                j = 8 * g + ji
                c3p = sch.val("c3")
                leng = loader[0]
                lsem = loader[1][j % 2]
                def s_load2(e, j=j, c3p=c3p, lsem=lsem):
                    e.wait_ge(sem["c3"], max(0, c3p - 1))
                    e.dma_start(out=t.xbuf[:, (j % 2) * D:(j % 2 + 1) * D],
                                in_=x_in_fn(e, j)).then_inc(sem[lsem], 16)
                ldv = sch.inc(lsem, 16)
                sch.on(leng, s_load2)
                trv = sch.val(("tra", "trb")[j % 2])
                def v_apply(e, j=j, ji=ji, ldv=ldv, trv=trv, lsem=lsem,
                            tsem=("tra", "trb")[j % 2], ldsv=ldsv):
                    e.wait_ge(sem[lsem], ldv)
                    e.wait_ge(sem["ccd"], ldsv)
                    e.wait_ge(sem[tsem], trv)
                    e.drain()
                    ln = t.lnt
                    e.tensor_scalar(t.xnbuf[:, :], t.xbuf[:, (j % 2) * D:(j % 2 + 1) * D],
                                    ln[:, 24 + ji:25 + ji], ln[:, ji:ji + 1],
                                    OP.mult, OP.add).then_inc(sem["c3"], 1)
                    e.drain()
                    e.tensor_scalar(t.qdt[:, :], t.xnbuf[:, :], t.scl[:, 1:2], CMAGIC,
                                    OP.mult, OP.add)
                    e.drain()
                    e.tensor_scalar(t.xnbf[:, (j % 2) * D:(j % 2 + 1) * D], t.qdt[:, :],
                                    CMAGIC, t.scl[:, 0:1], OP.subtract, OP.mult).then_inc(sem["c2"], 1)
                sch.inc("c3")
                c2v = sch.inc("c2")
                sch.on("vector", v_apply)
                tsem = ("tra", "trb")[j % 2]
                def a_tr(e, j=j, c2v=c2v, tsem=tsem):
                    e.wait_ge(sem["c2"], c2v)
                    for c in range(8):
                        e.dma_start(out=fm_dst_fn(j, c),
                                    in_=t.xnbf[:, (j % 2) * D + 128 * c:(j % 2) * D + 128 * (c + 1)],
                                    transpose=True).then_inc(sem[tsem], 16)
                sch.inc(tsem, 16 * 8)
                sch.on("scalar", a_tr)
            continue
        for ji in range(8):
            j = 8 * g + ji
            c1p = sch.val("c1")
            c9p = sch.val("c9")
            c3p = sch.val("c3")
            leng = loader[0]
            lsem = loader[1][j % 2]
            def s_load(e, j=j, c1p=c1p, c9p=c9p, c3p=c3p, lsem=lsem):
                e.wait_ge(sem["c1"], max(0, c1p - 1))
                e.wait_ge(sem["c9"], max(0, c9p - 1))
                e.wait_ge(sem["c3"], c3p)
                e.dma_start(out=t.xbuf[:, (j % 2) * D:(j % 2 + 1) * D],
                            in_=x_in_fn(e, j)).then_inc(sem[lsem], 16)
            ldv = sch.inc(lsem, 16)
            sch.on(leng, s_load)
            c1g = sch.val("c1")
            def a_sq(e, j=j, ldv=ldv, lsem=lsem, c1g=c1g):
                e.wait_ge(sem[lsem], ldv)
                e.wait_ge(sem["c1"], c1g)
                sc = 2 * (j % 16)
                e.activation(t.qdt[:, :], t.xbuf[:, (j % 2) * D:(j % 2 + 1) * D], ACT.Square,
                             accum_out=t.stats[:, sc + 1:sc + 2]).then_inc(sem["c1"], 1)
            sch.inc("c1")
            sch.on("scalar", a_sq)
            def v_sum(e, j=j, ldv=ldv, lsem=lsem):
                e.wait_ge(sem[lsem], ldv)
                sc = 2 * (j % 16)
                e.reduce_sum(t.stats[:, sc:sc + 1],
                             t.xbuf[:, (j % 2) * D:(j % 2 + 1) * D], AX.X).then_inc(sem["c9"], 1)
            sch.inc("c9")
            sch.on("vector", v_sum)
        c1v = sch.val("c1")
        def v_chain(e, g=g, c1v=c1v):
            e.wait_ge(sem["c1"], c1v)
            e.drain()
            base = 16 * (g % 2)
            st = t.stats[:, base:base + 16].rearrange("p (j two) -> p j two", two=2)
            ln = t.lnt
            e.tensor_scalar(ln[:, 0:8], st[:, :, 0], 1.0 / D, None, OP.mult)
            e.tensor_scalar(ln[:, 8:16], st[:, :, 1], 1.0 / D, None, OP.mult)
            e.drain()
            e.tensor_tensor(ln[:, 16:24], ln[:, 0:8], ln[:, 0:8], OP.mult)
            e.drain()
            e.scalar_tensor_tensor(ln[:, 24:32], ln[:, 8:16], 1e-5, ln[:, 16:24],
                                   OP.add, OP.subtract)
            e.drain()
            e.reciprocal(ln[:, 16:24], ln[:, 24:32]).then_inc(sem["c4"], 1)
        c4v = sch.inc("c4")
        sch.on("vector", v_chain)
        def a_sqrt(e, c4v=c4v):
            e.wait_ge(sem["c4"], c4v)
            e.activation(t.lnt[:, 24:32], t.lnt[:, 16:24], ACT.Sqrt).then_inc(sem["c5"], 1)
        c5v = sch.inc("c5")
        sch.on("scalar", a_sqrt)
        def v_nmr(e, g=g, c5v=c5v, cache_to=cache_to):
            e.wait_ge(sem["c5"], c5v)
            ln = t.lnt
            e.tensor_tensor(ln[:, 16:24], ln[:, 0:8], ln[:, 24:32], OP.mult)
            e.drain()
            e.tensor_scalar(ln[:, 0:8], ln[:, 16:24], -1.0, None, OP.mult)
            e.drain()
            if cache_to is not None:
                cb = cache_to + 8 * g
                e.tensor_copy(t.negct[:, cb:cb + 8], ln[:, 0:8])
                e.tensor_copy(t.negct[:, 32 + cb:32 + cb + 8], ln[:, 24:32])
                e.drain()
        sch.on("vector", v_nmr)
        for ji in range(8):
            j = 8 * g + ji
            c3p = sch.val("c3")
            leng = loader[0]
            lsem = loader[1][j % 2]
            c1p2 = sch.val("c1")
            c9p2 = sch.val("c9")
            def s_load2(e, j=j, c3p=c3p, lsem=lsem, c1p2=c1p2, c9p2=c9p2):
                e.wait_ge(sem["c3"], max(0, c3p - 1))
                e.wait_ge(sem["c1"], c1p2)
                e.wait_ge(sem["c9"], c9p2)
                e.dma_start(out=t.xbuf[:, (j % 2) * D:(j % 2 + 1) * D],
                            in_=x_in_fn(e, j)).then_inc(sem[lsem], 16)
            ldv = sch.inc(lsem, 16)
            sch.on(leng, s_load2)
            trv = sch.val(("tra", "trb")[j % 2])
            def v_apply(e, j=j, ji=ji, ldv=ldv, trv=trv, lsem=lsem, tsem=("tra", "trb")[j % 2]):
                e.wait_ge(sem[lsem], ldv)
                e.wait_ge(sem[tsem], trv)
                e.drain()
                ln = t.lnt
                e.tensor_scalar(t.xnbuf[:, :], t.xbuf[:, (j % 2) * D:(j % 2 + 1) * D],
                                ln[:, 24 + ji:25 + ji], ln[:, ji:ji + 1],
                                OP.mult, OP.add).then_inc(sem["c3"], 1)
                e.drain()
                if not qdq:
                    e.tensor_copy(t.xnbf[:, (j % 2) * D:(j % 2 + 1) * D],
                                  t.xnbuf[:, :]).then_inc(sem["c2"], 1)
                else:
                    e.tensor_scalar(t.qdt[:, :], t.xnbuf[:, :], t.scl[:, 1:2], CMAGIC,
                                    OP.mult, OP.add)
                    e.drain()
                    e.tensor_scalar(t.xnbf[:, (j % 2) * D:(j % 2 + 1) * D], t.qdt[:, :],
                                    CMAGIC, t.scl[:, 0:1], OP.subtract, OP.mult).then_inc(sem["c2"], 1)
                if absmax_base is not None:
                    e.drain()
                    e.tensor_reduce(t.mxcol[:, absmax_base + j:absmax_base + j + 1],
                                    t.xnbuf[:, :], AX.X, OP.max, apply_absolute_value=True)
            sch.inc("c3")
            c2v = sch.inc("c2")
            sch.on("vector", v_apply)
            tsem = ("tra", "trb")[j % 2]
            def a_tr(e, j=j, c2v=c2v, tsem=tsem):
                e.wait_ge(sem["c2"], c2v)
                for c in range(8):
                    e.dma_start(out=fm_dst_fn(j, c),
                                in_=t.xnbf[:, (j % 2) * D + 128 * c:(j % 2) * D + 128 * (c + 1)],
                                transpose=True).then_inc(sem[tsem], 16)
            sch.inc(tsem, 16 * 8)
            sch.on("scalar", a_tr)


def gen_p1(nc, t, sch):
    sem = t.sem
    din = t.din
    pA, pB = t.pA, t.pB

    def x_in(e, j):
        return din["x_seq"][128 * j:128 * (j + 1), :]

    def fm_dst(j, c):
        return t.xn_fm[:, TOWN * c + 128 * j: TOWN * c + 128 * (j + 1)]

    def kqv_half(th):
        trva = sch.val("tra")
        trvb = sch.val("trb")
        ldwv = sch.val("ldw")
        for wi in range(2):
            for fc in range(4):
                j = wi * 4 + fc
                evv = sch.val("c6")
                def t_mm(e, wi=wi, fc=fc, th=th, trva=trva, trvb=trvb, ldwv=ldwv, evv=evv):
                    e.wait_ge(sem["ldw"], ldwv)
                    e.wait_ge(sem["tra"], trva)
                    e.wait_ge(sem["trb"], trvb)
                    e.wait_ge(sem["c6"], evv)
                    wbase = (wi * 8 + 0) * 512
                    for cc in range(8):
                        lhs = t.wkqv[:, wi * 4096 + 512 * cc + 128 * fc:
                                        wi * 4096 + 512 * cc + 128 * (fc + 1)]
                        e.ldweights(lhs)
                        for tq in range(4):
                            mm = e.matmul(pA[:, 512 * tq:512 * (tq + 1)], lhs,
                                          t.xn_fm[:, TOWN * cc + 512 * tq:
                                                     TOWN * cc + 512 * (tq + 1)],
                                          start=(cc == 0), stop=(cc == 7))
                    mm.then_inc(sem["c7"], 1)
                c7v = sch.inc("c7")
                sch.on("tensor", t_mm)
                def v_evac(e, wi=wi, fc=fc, th=th, c7v=c7v):
                    e.wait_ge(sem["c7"], c7v)
                    dst = (t.kT if wi == 0 else t.qT)
                    e.tensor_copy(dst[:, S * fc + 2048 * th:S * fc + 2048 * (th + 1)],
                                  pA[:, 0:2048]).then_inc(sem["c6"], 1)
                sch.inc("c6")
                sch.on("vector", v_evac)

    def v_half(th):
        trva = sch.val("tra")
        trvb = sch.val("trb")
        for tc in range(16 * th, 16 * (th + 1)):
            evv = sch.val("c9")
            slot = tc % 2
            def t_vmm(e, tc=tc, slot=slot, trva=trva, trvb=trvb, evv=evv):
                e.wait_ge(sem["tra"], trva)
                e.wait_ge(sem["trb"], trvb)
                e.wait_ge(sem["c9"], max(0, evv - 1))
                for cc in range(8):
                    tl = tc % 16
                    lhs = t.xn_fm[:, TOWN * cc + 128 * tl:TOWN * cc + 128 * (tl + 1)]
                    e.ldweights(lhs)
                    mm = e.matmul(pB[:, 512 * slot:512 * (slot + 1)], lhs, t.wvb[:, 512 * cc:512 * (cc + 1)],
                                  start=(cc == 0), stop=(cc == 7))
                mm.then_inc(sem["c8"], 1)
            c8v = sch.inc("c8")
            sch.on("tensor", t_vmm)
            def v_vevac(e, tc=tc, slot=slot, c8v=c8v):
                e.wait_ge(sem["c8"], c8v)
                src = pB[:, 512 * slot:512 * (slot + 1)].rearrange("p (h d) -> p h d", h=8)
                dst = t.vA[:, 520 * tc:520 * (tc + 1)].rearrange("p (h e) -> p h e", h=8)[:, :, 0:64]
                e.tensor_copy(dst, src).then_inc(sem["c9"], 1)
            sch.inc("c9")
            sch.on("vector", v_vevac)

    gen_ln(nc, t, sch, 16, x_in, fm_dst, absmax_base=0, cache_to=0)
    kqv_half(0)
    v_half(0)
    c7v, c8v = sch.val("c7"), sch.val("c8")
    def a_waitreuse(e, c7v=c7v, c8v=c8v):
        e.wait_ge(sem["c7"], c7v)
        e.wait_ge(sem["c8"], c8v)
    sch.on("scalar", a_waitreuse)
    def x_in2(e, j):
        return din["x_seq"][128 * (j + 16):128 * (j + 17), :]
    gen_ln(nc, t, sch, 16, x_in2, fm_dst, absmax_base=16, cache_to=16)
    kqv_half(1)
    v_half(1)

    vfin = sch.val("c3")
    n2p = sch.val("n2")
    def v_negb(e, vfin=vfin):
        e.wait_ge(sem["c3"], vfin)
        e.drain()
        e.tensor_reduce(t.small[:, 6:7], t.mxcol[:, 0:32], AX.X, OP.max).then_inc(sem["n1"], 1)
    n1v3 = sch.inc("n1")
    sch.on("vector", v_negb)
    def t_ntr3(e, n1v3=n1v3, n2p=n2p):
        e.wait_ge(sem["n1"], n1v3)
        e.wait_ge(sem["n2"], n2p)
        e.transpose(pB[0:1, 8:136], t.small[:, 6:7], t.ident[:, :]).then_inc(sem["n2"], 1)
    n2v3 = sch.inc("n2")
    sch.on("tensor", t_ntr3)
    def v_gmax(e, n2v3=n2v3):
        e.wait_ge(sem["n2"], n2v3)
        e.tensor_reduce(t.small[0:1, 7:8], pB[0:1, 8:136], AX.X, OP.max).then_inc(sem["n1"], 1)
    sch.inc("n1")
    sch.on("vector", v_gmax)


def gen_p2a(nc, t, sch):
    sem = t.sem
    din = t.din
    pB = t.pB

    ccd1 = sch.inc("ccd", 16)
    ccs1 = sch.inc("ccs", 1)
    ccd2 = sch.inc("ccd", 16)
    arv = sch.inc("c1")
    def g_ar(e, ccd1=ccd1, ccs1=ccs1, ccd2=ccd2):
        e.dma_start(out=t.ar_in[0].ap()[0:1, 0:8], in_=t.small[0:1, 0:8]).then_inc(sem["ccd"], 16)
        e.wait_ge(sem["ccd"], ccd1)
        e.collective_compute("AllReduce", OP.max, replica_groups=[list(range(8))],
                             ins=[t.ar_in[0].ap()], outs=[t.ar_out[0].ap()]).then_inc(sem["ccs"], 1)
        e.wait_ge(sem["ccs"], ccs1)
        e.dma_start(out=t.small[0:1, 8:9], in_=t.ar_out[0].ap()[0:1, 7:8]).then_inc(sem["ccd"], 16)
        e.wait_ge(sem["ccd"], ccd2)
        e.nop().then_inc(sem["c1"], 1)
    sch.on("gpsimd", g_ar)

    def v_scales(e, arv=arv):
        e.wait_ge(sem["c1"], arv)
        e.tensor_scalar(t.small[0:1, 9:10], t.small[0:1, 8:9], 1.0 / 32767.0, 1e-12, OP.mult, OP.max)
        e.drain()
        e.reciprocal(t.small[0:1, 10:11], t.small[0:1, 9:10]).then_inc(sem["c2"], 1)
    c2v = sch.inc("c2")
    sch.on("vector", v_scales)
    def t_bcast(e, c2v=c2v):
        e.wait_ge(sem["c2"], c2v)
        e.matmul(pB[:, 0:2], t.onesf[0:1, :], t.small[0:1, 9:11], start=True, stop=True).then_inc(sem["c1"], 1)
    c1v = sch.inc("c1")
    sch.on("tensor", t_bcast)
    def v_sc2(e, c1v=c1v):
        e.wait_ge(sem["c1"], c1v)
        e.tensor_copy(t.scl[:, 0:2], pB[:, 0:2]).then_inc(sem["c2"], 1)
    scv = sch.inc("c2")
    sch.on("vector", v_sc2)

    def x_in(e, j):
        return din["x_own"][128 * j:128 * (j + 1), :]

    def fm_dst(j, c):
        return t.xq_fm[:, TOWN * c + 128 * j: TOWN * c + 128 * (j + 1)]

    def v_wait(e, scv=scv):
        e.wait_ge(sem["c2"], scv)
    sch.on("vector", v_wait)
    gen_ln(nc, t, sch, 16, x_in, fm_dst, absmax_base=None, qdq=True, stats_cached=True)


def _gen_wscales(nc, t, sch, wname, nchunks, blocks_per_chunk, dst_sT, halfload):
    sem = t.sem
    din = t.din
    for i in range(nchunks):
        nb = blocks_per_chunk
        rounds = 2 if halfload else 1
        for h in range(rounds):
            cw = 1536 if halfload else nb * 32
            c6p = sch.val("c6")
            c7p = sch.val("c7")
            c8p = sch.val("c8")
            def s_ld(e, i=i, h=h, cw=cw, c6p=c6p, c7p=c7p, c8p=c8p, wname=wname):
                e.wait_ge(sem["c6"], c6p)
                e.wait_ge(sem["c7"], c7p)
                e.wait_ge(sem["c8"], c8p)
                e.dma_start(out=t.xbuf[:, 0:cw],
                            in_=din[wname][128 * i:128 * (i + 1), 1536 * h:1536 * h + cw]
                            ).then_inc(sem["lda"], 16)
            ldv = sch.inc("lda", 16)
            sch.on("sync", s_ld)
            nbh = nb // rounds
            def v_red(e, i=i, h=h, cw=cw, ldv=ldv, nb=nb, nbh=nbh):
                e.wait_ge(sem["lda"], ldv)
                wv = t.xbuf[:, 0:cw].rearrange("p (b e) -> p b e", e=32)
                e.tensor_reduce(t.qdt[:, 0:nbh], wv, AX.X, OP.max, apply_absolute_value=True)
                e.drain()
                e.tensor_scalar(t.sall[:, nb * i + nbh * h:nb * i + nbh * (h + 1)], t.qdt[:, 0:nbh],
                                1.0 / 7.0, 1e-12, OP.mult, OP.max).then_inc(sem["c6"], 1)
            sch.inc("c6")
            sch.on("vector", v_red)
    for i in range(nchunks):
        c6v = sch.val("c6")
        n2p = sch.val("n2")
        nb = blocks_per_chunk
        def t_tr(e, i=i, c6v=c6v, n2p=n2p, nb=nb):
            e.wait_ge(sem["c6"], c6v)
            e.wait_ge(sem["n2"], n2p)
            e.transpose(t.pB[0:nb, 0:128], t.sall[:, nb * i:nb * (i + 1)],
                        t.ident[:, :]).then_inc(sem["n1"], 1)
        n1v = sch.inc("n1")
        sch.on("tensor", t_tr)
        def v_ev(e, i=i, n1v=n1v, nb=nb, dst_sT=dst_sT):
            e.wait_ge(sem["n1"], n1v)
            e.tensor_copy(dst_sT[0:nb, 128 * i:128 * (i + 1)], t.pB[0:nb, 0:128]).then_inc(sem["n2"], 1)
        sch.inc("n2")
        sch.on("vector", v_ev)


def _gen_updown_mm(nc, t, sch, wTname, stash, maxcol0):
    sem = t.sem
    din = t.din
    pA, pB = t.pA, t.pB
    for isl in range(6):
        for cc in range(8):
            c7p = sch.val("c7")
            c6p = sch.val("c6")
            def s_ld(e, isl=isl, cc=cc, c7p=c7p, c6p=c6p, wTname=wTname):
                e.wait_ge(sem["c7"], max(0, c7p - 1))
                e.wait_ge(sem["c6"], c6p)
                e.dma_start(out=t.qdt[:, 512 * (cc % 2):512 * (cc % 2 + 1)],
                            in_=din[wTname][128 * cc:128 * (cc + 1), 512 * isl:512 * (isl + 1)]
                            ).then_inc(sem[("lda", "ldb")[cc % 2]], 16)
            ldv = sch.inc(("lda", "ldb")[cc % 2], 16)
            sch.on("sync", s_ld)
            n2p = sch.val("n2")
            c6r = sch.val("c6")
            def t_bc(e, isl=isl, cc=cc, n2p=n2p, c6r=c6r):
                e.wait_ge(sem["n2"], n2p)
                e.wait_ge(sem["c6"], c6r)
                e.matmul(pB[:, 1024:1536], t.ind32[:, 128 * cc:128 * (cc + 1)],
                         t.rTu[:, 512 * isl:512 * (isl + 1)],
                         start=True, stop=True)
                e.matmul(pB[:, 1536:2048], t.ind32[:, 128 * cc:128 * (cc + 1)],
                         t.sTu[:, 512 * isl:512 * (isl + 1)],
                         start=True, stop=True).then_inc(sem["n1"], 1)
            n1v = sch.inc("n1")
            sch.on("tensor", t_bc)
            mm8 = sch.val("c8")
            def v_dq(e, isl=isl, cc=cc, ldv=ldv, n1v=n1v, mm8=mm8):
                e.wait_ge(sem[("lda", "ldb")[cc % 2]], ldv)
                e.wait_ge(sem["n1"], n1v)
                e.wait_ge(sem["c8"], mm8)
                src = t.qdt[:, 512 * (cc % 2):512 * (cc % 2 + 1)]
                e.tensor_tensor(t.rnorm[:, :], src, pB[:, 1024:1536], OP.mult)
                e.drain()
                e.tensor_scalar(t.rnorm[:, :], t.rnorm[:, :], CMAGIC, CMAGIC, OP.add, OP.subtract)
                e.drain()
                e.tensor_tensor(t.wqb[:, 512 * cc:512 * (cc + 1)], t.rnorm[:, :], pB[:, 1536:2048],
                                OP.mult).then_inc(sem["n2"], 1)
                e.drain()
                e.nop().then_inc(sem["c7"], 1)
            sch.inc("n2")
            sch.inc("c7")
            sch.on("vector", v_dq)
        dqv = sch.val("n2")
        for il in range(4):
            ic = 4 * isl + il
            evp = sch.val("c5")
            def t_mm(e, il=il, dqv=dqv, evp=evp):
                e.wait_ge(sem["n2"], dqv)
                e.wait_ge(sem["c5"], evp)
                for cc in range(8):
                    lhs = t.wqb[:, 512 * cc + 128 * il:512 * cc + 128 * (il + 1)]
                    e.ldweights(lhs)
                    for tq in range(4):
                        mm = e.matmul(pA[:, 512 * tq:512 * (tq + 1)], lhs,
                                      t.xq_fm[:, TOWN * cc + 512 * tq:TOWN * cc + 512 * (tq + 1)],
                                      start=(cc == 0), stop=(cc == 7))
                mm.then_inc(sem["c8"], 1)
            c8v = sch.inc("c8")
            sch.on("tensor", t_mm)
            stp = sch.val("stw")
            def v_evac(e, ic=ic, c8v=c8v, stp=stp, maxcol0=maxcol0):
                e.wait_ge(sem["c8"], c8v)
                e.wait_ge(sem["stw"], stp)
                e.drain()
                e.tensor_copy(t.ebuf0[:, :], pA[:, 0:1024]).then_inc(sem["c5"], 1)
                e.tensor_copy(t.ebuf1[:, :], pA[:, 1024:2048]).then_inc(sem["c5"], 1)
                e.drain()
                e.tensor_reduce(t.mxcol[:, maxcol0 + 2 * (ic % 12):maxcol0 + 2 * (ic % 12) + 1],
                                t.ebuf0[:, :], AX.X, OP.max, apply_absolute_value=True)
                e.drain()
                e.tensor_reduce(t.mxcol[:, maxcol0 + 2 * (ic % 12) + 1:maxcol0 + 2 * (ic % 12) + 2],
                                t.ebuf1[:, :], AX.X, OP.max, apply_absolute_value=True).then_inc(sem["c6"], 1)
            sch.inc("c5", 2)
            c6v = sch.inc("c6")
            sch.on("vector", v_evac)
            if ic % 12 == 11:
                def v_mrg(e, ic=ic, c6v=c6v, maxcol0=maxcol0):
                    e.wait_ge(sem["c6"], c6v)
                    e.drain()
                    e.tensor_reduce(t.mxcol[:, maxcol0 + 24 + (ic // 12):maxcol0 + 25 + (ic // 12)],
                                    t.mxcol[:, maxcol0:maxcol0 + 24], AX.X, OP.max).then_inc(sem["c6"], 1)
                sch.inc("c6")
                sch.on("vector", v_mrg)
            def g_st(e, ic=ic, c6v=c6v, stash=stash):
                e.wait_ge(sem["c6"], c6v)
                e.dma_start(out=stash.ap()[ic, :, 0:1024], in_=t.ebuf0[:, :]).then_inc(sem["stw"], 16)
                e.dma_start(out=stash.ap()[ic, :, 1024:2048], in_=t.ebuf1[:, :]).then_inc(sem["stw"], 16)
            sch.inc("stw", 32)
            sch.on("gpsimd", g_st)


def gen_p2b(nc, t, sch):
    sem = t.sem
    pB = t.pB
    def gen_recip():
        c6v = sch.val("c6")
        n2v = sch.val("n2")
        def v_rc(e, c6v=c6v, n2v=n2v):
            e.wait_ge(sem["c6"], c6v)
            e.wait_ge(sem["n2"], n2v)
            e.reciprocal(t.rTu[:, :], t.sTu[:, :]).then_inc(sem["c6"], 1)
        sch.inc("c6")
        sch.on("vector", v_rc)

    _gen_wscales(nc, t, sch, "wupN", NIC, 32, t.sTu, False)
    gen_recip()
    _gen_updown_mm(nc, t, sch, "wupT", t.stash_up, 0)
    _gen_finmax(nc, t, sch, 0, 11)
    _gen_wscales(nc, t, sch, "wgateN", NIC, 32, t.sTu, False)
    gen_recip()
    _gen_updown_mm(nc, t, sch, "wgateT", t.stash_gate, 0)
    _gen_finmax(nc, t, sch, 0, 12)
    _gen_wscales(nc, t, sch, "wdownN", 8, 96, t.sTd, True)
    ccd1 = sch.inc("ccd", 16)
    ccs1 = sch.inc("ccs", 1)
    ccd2 = sch.inc("ccd", 16)
    c6f = sch.val("c6")
    arv = sch.inc("c1")
    def g_ar(e, ccd1=ccd1, ccs1=ccs1, ccd2=ccd2, c6f=c6f):
        e.wait_ge(sem["c6"], c6f)
        e.dma_start(out=t.ar_in[1].ap()[0:1, 0:8], in_=t.small[0:1, 8:16]).then_inc(sem["ccd"], 16)
        e.wait_ge(sem["ccd"], ccd1)
        e.collective_compute("AllReduce", OP.max, replica_groups=[list(range(8))],
                             ins=[t.ar_in[1].ap()], outs=[t.ar_out[1].ap()]).then_inc(sem["ccs"], 1)
        e.wait_ge(sem["ccs"], ccs1)
        e.dma_start(out=t.small[0:1, 11:13], in_=t.ar_out[1].ap()[0:1, 3:5]).then_inc(sem["ccd"], 16)
        e.wait_ge(sem["ccd"], ccd2)
        e.nop().then_inc(sem["c1"], 1)
    sch.on("gpsimd", g_ar)
    def v_s2(e, arv=arv):
        e.wait_ge(sem["c1"], arv)
        e.tensor_scalar(t.small[0:1, 0:2], t.small[0:1, 11:13], 1.0 / 32767.0, 1e-12, OP.mult, OP.max)
        e.drain()
        e.reciprocal(t.small[0:1, 2:4], t.small[0:1, 0:2]).then_inc(sem["c2"], 1)
    c2v = sch.inc("c2")
    sch.on("vector", v_s2)
    def t_bc2(e, c2v=c2v):
        e.wait_ge(sem["c2"], c2v)
        e.matmul(pB[:, 0:4], t.onesf[0:1, :], t.small[0:1, 0:4], start=True, stop=True).then_inc(sem["c1"], 1)
    c1v = sch.inc("c1")
    sch.on("tensor", t_bc2)
    def v_bc2(e, c1v=c1v):
        e.wait_ge(sem["c1"], c1v)
        e.tensor_copy(t.scl[:, 4:8], pB[:, 0:4]).then_inc(sem["c2"], 1)
    sch.inc("c2")
    sch.on("vector", v_bc2)


def _gen_finmax(nc, t, sch, maxcol0, dstcol):
    sem = t.sem
    pB = t.pB
    c6v = sch.val("c6")
    def v_f(e, c6v=c6v, maxcol0=maxcol0):
        e.wait_ge(sem["c6"], c6v)
        e.tensor_reduce(t.small[:, 14:15], t.mxcol[:, maxcol0 + 24:maxcol0 + 26],
                        AX.X, OP.max).then_inc(sem["n1"], 1)
    n1v = sch.inc("n1")
    sch.on("vector", v_f)
    n2p = sch.val("n2")
    def t_f(e, n1v=n1v, n2p=n2p):
        e.wait_ge(sem["n1"], n1v)
        e.wait_ge(sem["n2"], n2p)
        e.transpose(pB[0:1, 8:136], t.small[:, 14:15], t.ident[:, :]).then_inc(sem["n2"], 1)
    n2v = sch.inc("n2")
    sch.on("tensor", t_f)
    def v_f2(e, n2v=n2v, dstcol=dstcol):
        e.wait_ge(sem["n2"], n2v)
        e.tensor_reduce(t.small[0:1, dstcol:dstcol + 1], pB[0:1, 8:136], AX.X, OP.max).then_inc(sem["c6"], 1)
    sch.inc("c6")
    sch.on("vector", v_f2)

def gen_p3(nc, t, sch, midhook=None, prehook=None):
    sem = t.sem
    pA, pB = t.pA, t.pB
    for p_ in range(NPAIR):
        for sl in range(8):
            n2p = sch.val("n2")
            def v_ksq(e, p_=p_, sl=sl, n2p=n2p):
                e.wait_ge(sem["n2"], n2p)
                e.drain()
                kslc = t.kT[:, S * p_ + 512 * sl:S * p_ + 512 * (sl + 1)]
                e.tensor_tensor(t.sqs[:, 0:512], kslc, kslc, OP.mult).then_inc(sem["n1"], 1)
            n1v = sch.inc("n1")
            sch.on("vector", v_ksq)
            n2c = sch.val("n2")
            def t_kn(e, n1v=n1v, n2c=n2c):
                e.wait_ge(sem["n1"], n1v)
                e.wait_ge(sem["n2"], n2c)
                e.matmul(pB[0:1, 0:512], t.ones64[:, 0:1], t.sqs[:, 0:512], start=True, stop=True)
                e.matmul(pB[64:65, 0:512], t.ones64[:, 1:2], t.sqs[:, 0:512],
                         start=True, stop=True, tile_position=(0, 64)).then_inc(sem["n2"], 1)
            n2v = sch.inc("n2")
            sch.on("tensor", t_kn)
            def v_kmx(e, p_=p_, sl=sl, n2v=n2v):
                e.wait_ge(sem["n2"], n2v)
                e.drain()
                e.tensor_reduce(t.lnt[0:1, 0:1], pB[0:1, 0:512], AX.X, OP.max)
                e.tensor_reduce(t.lnt[64:65, 0:1], pB[64:65, 0:512], AX.X, OP.max)
                e.drain()
                if sl == 0:
                    e.tensor_copy(t.lnt[0:1, 4 + p_:5 + p_], t.lnt[0:1, 0:1])
                    e.tensor_copy(t.lnt[64:65, 4 + p_:5 + p_], t.lnt[64:65, 0:1])
                else:
                    e.tensor_tensor(t.lnt[0:1, 8:9], t.lnt[0:1, 0:1], t.lnt[0:1, 4 + p_:5 + p_], OP.max)
                    e.tensor_tensor(t.lnt[64:65, 8:9], t.lnt[64:65, 0:1], t.lnt[64:65, 4 + p_:5 + p_], OP.max)
                    e.drain()
                    e.tensor_copy(t.lnt[0:1, 4 + p_:5 + p_], t.lnt[0:1, 8:9])
                    e.tensor_copy(t.lnt[64:65, 4 + p_:5 + p_], t.lnt[64:65, 8:9])
                e.drain()
                e.nop().then_inc(sem["n2"], 1)
            sch.inc("n2")
            sch.on("vector", v_kmx)
    n2f = sch.val("n2")
    def a_ksqrt(e, n2f=n2f):
        e.wait_ge(sem["n2"], n2f)
        e.activation(t.lnt[0:1, 12:16], t.lnt[0:1, 4:8], ACT.Sqrt)
        e.activation(t.lnt[64:65, 12:16], t.lnt[64:65, 4:8], ACT.Sqrt).then_inc(sem["a2"], 1)
    a2k = sch.inc("a2")
    sch.on("scalar", a_ksqrt)
    def v_kneg(e, a2k=a2k):
        e.wait_ge(sem["a2"], a2k)
        e.tensor_scalar(t.lnt[0:1, 12:16], t.lnt[0:1, 12:16], -1.0, None, OP.mult)
        e.tensor_scalar(t.lnt[64:65, 12:16], t.lnt[64:65, 12:16], -1.0, None, OP.mult)
        e.drain()
        e.nop().then_inc(sem["n2"], 1)
    sch.inc("n2")
    sch.on("vector", v_kneg)
    prol_n2 = sch.val("n2")

    pend = {}

    def emit_qsq_qn(idx, a2cps, n2sqs):
        p_, qt = idx // NQT, idx % NQT
        def v_qsq(e, p_=p_, qt=qt, n2sqs=n2sqs, prol_n2=prol_n2):
            e.wait_ge(sem["n2"], max(prol_n2, n2sqs))
            e.drain()
            qslc = t.qT[:, S * p_ + 512 * qt:S * p_ + 512 * (qt + 1)]
            e.tensor_tensor(t.sqs[:, 0:512], qslc, qslc, OP.mult).then_inc(sem["n1"], 1)
        n1q = sch.inc("n1")
        sch.on("vector", v_qsq)
        def t_qn(e, n1q=n1q, a2cps=a2cps):
            e.wait_ge(sem["n1"], n1q)
            e.wait_ge(sem["a2"], a2cps)
            e.matmul(pB[0:1, 1024:1536], t.ones64[:, 0:1], t.sqs[:, 0:512],
                     start=True, stop=True)
            e.matmul(pB[64:65, 1024:1536], t.ones64[:, 1:2], t.sqs[:, 0:512],
                     start=True, stop=True, tile_position=(0, 64)).then_inc(sem["n2"], 1)
        n2qv = sch.inc("n2")
        sch.on("tensor", t_qn)
        pend[idx] = dict(n2sqs=n2qv)

    def emit_ncb_shift(idx, a2cps, shiftprev):
        p_ = idx // NQT
        n2qv = pend[idx]["n2sqs"]
        def a_qn(e, n2qv=n2qv, shiftprev=shiftprev):
            e.wait_ge(sem["n2"], max(n2qv, shiftprev))
            e.activation(t.rnorm[0:1, :], pB[0:1, 1024:1536], ACT.Sqrt)
            e.activation(t.rnorm[64:65, :], pB[64:65, 1024:1536], ACT.Sqrt).then_inc(sem["a2"], 1)
        a2qv = sch.inc("a2")
        sch.on("scalar", a_qn)
        def v_ncb(e, p_=p_, a2qv=a2qv):
            e.wait_ge(sem["a2"], a2qv)
            e.drain()
            e.tensor_scalar(t.rnorm[0:1, :], t.rnorm[0:1, :],
                            t.lnt[0:1, 12 + p_:13 + p_], 200.0, OP.mult, OP.add)
            e.tensor_scalar(t.rnorm[64:65, :], t.rnorm[64:65, :],
                            t.lnt[64:65, 12 + p_:13 + p_], 200.0, OP.mult, OP.add)
            e.drain()
            e.nop().then_inc(sem["n2"], 1)
        ncbv = sch.inc("n2")
        sch.on("vector", v_ncb)
        def t_shift(e, ncbv=ncbv, a2cps=a2cps):
            e.wait_ge(sem["n2"], ncbv)
            e.wait_ge(sem["a2"], a2cps)
            e.matmul(pB[:, 1024:1536], t.onesf[0:1, 0:128], t.rnorm[0:1, :],
                     start=True, stop=True, tile_position=(0, 0))
            e.matmul(pB[:, 1536:2048], t.onesf[64:65, 0:128], t.rnorm[64:65, :],
                     start=True, stop=True, tile_position=(64, 0)).then_inc(sem["n2"], 1)
        shiftv = sch.inc("n2")
        sch.on("tensor", t_shift)
        pend[idx]["shiftv"] = shiftv

    def emit_acps(idx):
        shiftv = pend[idx]["shiftv"]
        a5l = sch.val("a5")
        def a_cps(e, shiftv=shiftv, a5l=a5l):
            e.wait_ge(sem["n2"], shiftv)
            e.wait_ge(sem["a5"], a5l)
            e.activation(t.shift_sb[:, :], pB[:, 1024:2048], ACT.Identity).then_inc(sem["a2"], 1)
        pend[idx]["cpsv"] = sch.inc("a2")
        sch.on("scalar", a_cps)

    emit_qsq_qn(0, sch.val("a2"), 0)
    emit_ncb_shift(0, 0, 0)
    emit_acps(0)

    for p_ in range(NPAIR):
        for qt in range(NQT):
            KC = 4 * (qt + 1)
            idx3 = 8 * p_ + qt
            extra_sc = prehook(idx3) if prehook else []
            cpsv = pend[idx3]["cpsv"]
            n2sqs = pend[idx3]["n2sqs"]
            shiftv = pend[idx3]["shiftv"]

            a4st = sch.val("a4")
            a5st = sch.val("a5")
            a2ev = sch.val("a2")
            cxv = {}
            prev_cx = None

            def emit_cx(kc, wv, wsem, p_=p_, KC=KC, a2ev=a2ev):
                def t_cx(e, p_=p_, kc=kc, wv=wv, wsem=wsem, KC=KC, a2ev=a2ev):
                    e.wait_ge(sem[wsem], wv)
                    if kc == 0:
                        e.wait_ge(sem["a2"], a2ev)
                    first, last = (kc == 0), (kc == KC - 1)
                    src = t.pfin2[:, 1024 * (kc % 2):1024 * (kc % 2) + 1024]
                    e.matmul(pA[0:65, 1024:1536],
                             t.vA[:, 520 * kc + 130 * p_:520 * kc + 130 * p_ + 65],
                             src[:, 0:512], start=first, stop=last)
                    e.matmul(pA[0:65, 1536:2048],
                             t.vA[:, 520 * kc + 130 * p_ + 65:520 * kc + 130 * (p_ + 1)],
                             src[:, 512:1024], start=first, stop=last).then_inc(sem["a1"], 1)
                cxv[kc] = sch.inc("a1")
                sch.on("tensor", t_cx)

            for kc in range(KC):
                ps_sc = pA if kc % 2 == 0 else pB
                def t_sc(e, p_=p_, qt=qt, kc=kc, ps_sc=ps_sc,
                         bw=(sch.val("a5") - 1 if kc >= 2 else a5st),
                         ex=(extra_sc if kc == 1 else [])):
                    for smn, vv in ex:
                        e.wait_ge(sem[smn], vv)
                    e.wait_ge(sem["a5"], bw)
                    e.matmul(ps_sc[:, 0:512],
                             t.kT[0:64, S * p_ + 128 * kc:S * p_ + 128 * (kc + 1)],
                             t.qT[0:64, S * p_ + 512 * qt:S * p_ + 512 * (qt + 1)],
                             start=True, stop=True, tile_position=(0, 0))
                    e.matmul(ps_sc[:, 512:1024],
                             t.kT[64:128, S * p_ + 128 * kc:S * p_ + 128 * (kc + 1)],
                             t.qT[64:128, S * p_ + 512 * qt:S * p_ + 512 * (qt + 1)],
                             start=True, stop=True, tile_position=(64, 0)).then_inc(sem["a1"], 1)
                a1v = sch.inc("a1")
                sch.on("tensor", t_sc)
                if prev_cx is not None:
                    emit_cx(*prev_cx)
                a2p = sch.val("a2")
                prw = t.praw32 if kc % 2 == 0 else t.praw32b
                def v_add(e, kc=kc, a1v=a1v, a2p=a2p, ps_sc=ps_sc, prw=prw):
                    e.wait_ge(sem["a1"], a1v)
                    e.wait_ge(sem["a2"], a2p - 1 if kc >= 2 else a2p)
                    e.scalar_tensor_tensor(prw[:, :], ps_sc[:, 0:1024], 1.0,
                                           t.shift_sb[:, :], OP.mult,
                                           OP.add).then_inc(sem["a5"], 1)
                a5v = sch.inc("a5")
                sch.on("vector", v_add)
                pfw = cxv.get(kc - 2, None)
                def a_exp(e, kc=kc, a5v=a5v, pfw=pfw, a4st=a4st, prw=prw):
                    e.wait_ge(sem["a5"], a5v)
                    if pfw is not None:
                        e.wait_ge(sem["a1"], pfw)
                    else:
                        e.wait_ge(sem["a4"], a4st)
                    e.activation(t.pfin2[:, 1024 * (kc % 2):1024 * (kc % 2) + 1024],
                                 prw[:, :], ACT.Exp, scale=0.125).then_inc(sem["a2"], 1)
                a2v = sch.inc("a2")
                sch.on("scalar", a_exp)
                dg = kc - 4 * qt
                if dg >= 0:
                    def v_msk(e, qt=qt, kc=kc, dg=dg, a2v=a2v):
                        e.wait_ge(sem["a2"], a2v)
                        e.drain()
                        mv = t.masks[:, 512 * dg:512 * (dg + 1)]
                        dst = t.pfin2[:, 1024 * (kc % 2):1024 * (kc % 2) + 1024]
                        e.tensor_tensor(dst[:, 0:512], dst[:, 0:512], mv, OP.mult)
                        e.tensor_tensor(dst[:, 512:1024], dst[:, 512:1024], mv,
                                        OP.mult).then_inc(sem["a3"], 1)
                    a3v = sch.inc("a3")
                    sch.on("vector", v_msk)
                    prev_cx = (kc, a3v, "a3")
                else:
                    prev_cx = (kc, a2v, "a2")
                if kc == 1 and idx3 + 1 < 32:
                    emit_qsq_qn(idx3 + 1, cpsv, n2sqs)
                if kc == 2:
                    if idx3 + 1 < 32:
                        emit_ncb_shift(idx3 + 1, cpsv, shiftv)
                    if midhook:
                        midhook(idx3, 0)
                if midhook and kc == min(KC - 1, 14):
                    midhook(idx3, 1)
            emit_cx(*prev_cx)
            if idx3 + 1 < 32:
                emit_acps(idx3 + 1)
            a1e = sch.val("a1")
            a4p = sch.val("a4")
            def a_st(e, a1e=a1e, a4p=a4p):
                e.wait_ge(sem["a1"], a1e)
                e.wait_ge(sem["a4"], a4p)
                e.copy(t.ctxstage[0:65, :], pA[0:65, 1024:2048]).then_inc(sem["a2"], 1)
            astv = sch.inc("a2")
            sch.on("scalar", a_st)
            def v_rc(e, astv=astv):
                e.wait_ge(sem["a2"], astv)
                e.drain()
                e.reciprocal(t.rnorm_e[64:65, 0:1024], t.ctxstage[64:65, :]).then_inc(sem["a4"], 1)
            vrcv = sch.inc("a4")
            sch.on("vector", v_rc)
            def t_rb(e, vrcv=vrcv):
                e.wait_ge(sem["a4"], vrcv)
                e.matmul(pA[0:64, 1024:1536], t.onesf[64:65, 0:64], t.rnorm_e[64:65, 0:512],
                         start=True, stop=True, tile_position=(64, 0))
                e.matmul(pA[0:64, 1536:2048], t.onesf[64:65, 0:64], t.rnorm_e[64:65, 512:1024],
                         start=True, stop=True, tile_position=(64, 0)).then_inc(sem["a1"], 1)
            a1r = sch.inc("a1")
            sch.on("tensor", t_rb)
            def a_ev(e, a1r=a1r, a4p=a4p):
                e.wait_ge(sem["a1"], a1r)
                e.wait_ge(sem["a4"], a4p)
                e.copy(t.rnorm_e[0:64, 0:1024], pA[0:64, 1024:2048]).then_inc(sem["a2"], 1)
            a2r = sch.inc("a2")
            sch.on("scalar", a_ev)
            def v_nm0(e, p_=p_, qt=qt, a2r=a2r):
                e.wait_ge(sem["a2"], a2r)
                e.drain()
                e.tensor_tensor(t.qT[0:64, S * p_ + 512 * qt:S * p_ + 512 * (qt + 1)],
                                t.ctxstage[0:64, 0:512], t.rnorm_e[0:64, 0:512],
                                OP.mult).then_inc(sem["a4"], 1)
            sch.inc("a4")
            sch.on("vector", v_nm0)
            def v_nm1(e, p_=p_, qt=qt, a2r=a2r):
                e.wait_ge(sem["a2"], a2r)
                e.tensor_tensor(t.pfin2[0:64, 0:512], t.ctxstage[0:64, 512:1024],
                                t.rnorm_e[0:64, 512:1024], OP.mult)
                e.drain()
                e.stream_shuffle(t.qT[64:128, S * p_ + 512 * qt:S * p_ + 512 * (qt + 1)],
                                 t.pfin2[0:64, 0:512], list(range(32))).then_inc(sem["a4"], 1)
            sch.inc("a4")
            sch.on("vector", v_nm1)

def _p4_ag_start(nc, t, sch):
    sem = t.sem
    ccd0 = sch.val("ccd")
    def g_ag1(e, ccd0=ccd0):
        for p_ in range(4):
            e.dma_start(out=t.ag_in.ap()[p_, :, :], in_=t.qT[:, S * p_:S * (p_ + 1)]).then_inc(sem["ccd"], 16)
        e.wait_ge(sem["ccd"], ccd0 + 64)
        e.collective_compute("AllGather", OP.bypass, replica_groups=[list(range(8))],
                             ins=[t.ag_in.ap()], outs=[t.ag_out.ap()]).then_inc(sem["ccs"], 1)
    sch.inc("ccd", 64)
    ccs_ag = sch.inc("ccs", 1)
    sch.on("gpsimd", g_ag1)
    return ccd0 + 64, ccs_ag


def _p4_extract(nc, t, sch, ccs_ag):
    sem = t.sem
    din = t.din
    c7g = sch.val("c7")
    stwg = sch.val("stw")
    ccdx = sch.val("ccd")
    def g_ag2(e, ccs_ag=ccs_ag, c7g=c7g, stwg=stwg, ccdx=ccdx):
        e.wait_ge(sem["ccs"], ccs_ag)
        e.wait_ge(sem["c7"], c7g)
        e.wait_ge(sem["stw"], stwg)
        my = e.value_load(t.offs[0:1, 1:2])
        pr = e.value_load(t.offs[0:1, 2:3])
        gv = e.value_load(t.offs[0:1, 3:4])
        for p_ in range(4):
            e.dma_start(out=t.ctx_own[:, TOWN * p_:TOWN * (p_ + 1)],
                        in_=t.ag_out.ap()[bass.ts(my, 1), p_, :, bass.ts(gv, TOWN)]).then_inc(sem["ccd"], 16)
        for p_ in range(4):
            e.dma_start(out=t.ctx_own[:, TOWN * (4 + p_):TOWN * (5 + p_)],
                        in_=t.ag_out.ap()[bass.ts(pr, 1), p_, :, bass.ts(gv, TOWN)]).then_inc(sem["ccd"], 16)
        e.dma_start(out=t.wprojT[:, :].rearrange("p (c f) -> p c f", c=8),
                    in_=din["wprojT"].rearrange("(c p) f -> p c f", p=128)).then_inc(sem["ldw"], 16)
        e.wait_ge(sem["ccd"], ccdx + 128)
        e.nop().then_inc(sem["c4"], 1)
    sch.inc("ccd", 128)
    sch.inc("ldw", 16)
    agv = sch.inc("c4")
    sch.on("gpsimd", g_ag2)
    return agv


def _p4_proj(nc, t, sch, agv):
    sem = t.sem
    pA, pB = t.pA, t.pB
    ldwv = sch.val("ldw")
    for tt in range(16):
        par = tt % 2
        evp = sch.val("c5")
        def t_pj(e, tt=tt, par=par, evp=evp, agv=agv, ldwv=ldwv):
            e.wait_ge(sem["c4"], agv)
            e.wait_ge(sem["ldw"], ldwv)
            e.wait_ge(sem["c5"], evp)
            for fc in range(8):
                lhs = t.ctx_own[:, TOWN * fc + 128 * tt:TOWN * fc + 128 * (tt + 1)]
                e.ldweights(lhs)
                for hf in range(2):
                    mm = e.matmul(pA[:, 1024 * par + 512 * hf:1024 * par + 512 * (hf + 1)], lhs,
                                  t.wprojT[:, 1024 * fc + 512 * hf:1024 * fc + 512 * (hf + 1)],
                                  start=(fc == 0), stop=(fc == 7))
            mm.then_inc(sem["c8"], 1)
        c8v = sch.inc("c8")
        sch.on("tensor", t_pj)
        def v_pj(e, tt=tt, par=par, c8v=c8v):
            e.wait_ge(sem["c8"], c8v)
            dst = (t.aout_a if tt < 8 else t.aout_b)[:, D * (tt % 8):D * (tt % 8 + 1)]
            e.scalar_tensor_tensor(dst, pA[:, 1024 * par:1024 * (par + 1)], 1.0,
                                   t.btile[:, :], OP.mult, OP.add).then_inc(sem["c5"], 1)
        sch.inc("c5")
        sch.on("vector", v_pj)

def _p5_finmax(nc, t, sch, dstcol, a2g=None):
    sem = t.sem
    pB = t.pB
    c6v = sch.val("c6")
    def v_f(e, c6v=c6v):
        e.wait_ge(sem["c6"], c6v)
        e.drain()
        e.tensor_reduce(t.small[:, 14:15], t.mxcol[:, 0:24], AX.X, OP.max).then_inc(sem["n1"], 1)
    n1v = sch.inc("n1")
    sch.on("vector", v_f)
    n2p = sch.val("n2")
    def t_f(e, n1v=n1v, n2p=n2p, a2g=a2g):
        e.wait_ge(sem["n1"], n1v)
        e.wait_ge(sem["n2"], n2p)
        if a2g is not None:
            e.wait_ge(sem["a2"], a2g)
        e.transpose(pB[0:1, 8:136], t.small[:, 14:15], t.ident[:, :]).then_inc(sem["n2"], 1)
    n2v = sch.inc("n2")
    sch.on("tensor", t_f)
    def v_f2(e, n2v=n2v, dstcol=dstcol):
        e.wait_ge(sem["n2"], n2v)
        e.tensor_reduce(t.small[0:1, dstcol:dstcol + 1], pB[0:1, 8:136], AX.X, OP.max).then_inc(sem["c6"], 1)
    sch.inc("c6")
    sch.on("vector", v_f2)


def _p5_ar(nc, t, sch, idx, srccols, scale_fn, bccols):
    sem = t.sem
    pB = t.pB
    ccd1 = sch.inc("ccd", 16)
    ccs1 = sch.inc("ccs", 1)
    ccd2 = sch.inc("ccd", 16)
    c6f = sch.val("c6")
    arv = sch.inc("c1")
    def g_ar(e, idx=idx, ccd1=ccd1, ccs1=ccs1, ccd2=ccd2, c6f=c6f):
        e.wait_ge(sem["c6"], c6f)
        e.dma_start(out=t.ar_in[idx].ap()[0:1, 0:8], in_=t.small[0:1, 8:16]).then_inc(sem["ccd"], 16)
        e.wait_ge(sem["ccd"], ccd1)
        e.collective_compute("AllReduce", OP.max, replica_groups=[list(range(8))],
                             ins=[t.ar_in[idx].ap()], outs=[t.ar_out[idx].ap()]).then_inc(sem["ccs"], 1)
        e.wait_ge(sem["ccs"], ccs1)
        e.dma_start(out=t.small[0:1, 8:16], in_=t.ar_out[idx].ap()[0:1, 0:8]).then_inc(sem["ccd"], 16)
        e.wait_ge(sem["ccd"], ccd2)
        e.nop().then_inc(sem["c1"], 1)
    sch.on("gpsimd", g_ar)
    def v_s(e, arv=arv, scale_fn=scale_fn):
        e.wait_ge(sem["c1"], arv)
        e.drain()
        scale_fn(e)
    c2v = sch.inc("c2")
    sch.on("vector", v_s)
    n, d0 = bccols
    def t_bc(e, c2v=c2v, n=n):
        e.wait_ge(sem["c2"], c2v)
        e.matmul(pB[:, 0:n], t.onesf[0:1, :], t.small[0:1, 0:n], start=True, stop=True).then_inc(sem["c1"], 1)
    c1v = sch.inc("c1")
    sch.on("tensor", t_bc)
    def v_bc(e, c1v=c1v, n=n, d0=d0):
        e.wait_ge(sem["c1"], c1v)
        e.tensor_copy(t.scl[:, d0:d0 + n], pB[:, 0:n]).then_inc(sem["c2"], 1)
    sch.inc("c2")
    sch.on("vector", v_bc)


def _p5_sc3(t):
    CM = CMAGIC
    def sc3(e):
        e.tensor_scalar(t.small[0:1, 0:1], t.small[0:1, 13:14], float(2.0 ** -16 / 32767.0),
                        1e-12, OP.mult, OP.max)
        e.drain()
        e.reciprocal(t.small[0:1, 2:3], t.small[0:1, 0:1])
        e.drain()
        e.tensor_scalar(t.small[0:1, 1:2], t.small[0:1, 2:3], float(2.0 ** -16), None,
                        OP.mult).then_inc(t.sem["c2"], 1)
    return sc3


def _p5_sc4(t):
    def sc4(e):
        e.tensor_scalar(t.small[0:1, 0:1], t.small[0:1, 13:14], 1.0 / 32767.0, 1e-12,
                        OP.mult, OP.max)
        e.drain()
        e.reciprocal(t.small[0:1, 1:2], t.small[0:1, 0:1]).then_inc(t.sem["c2"], 1)
    return sc4


def _p5_sclderiv(nc, t, sch):
    sem = t.sem
    def v_c(e):
        e.drain()
        e.tensor_scalar(t.scl[:, 13:14], t.scl[:, 5:6], -CMAGIC, None, OP.mult)
        e.drain()
        e.tensor_scalar(t.scl[:, 14:15], t.scl[:, 4:5], -CMAGIC, None, OP.mult)
        e.drain()
        e.nop().then_inc(sem["c2"], 1)
    v = sch.inc("c2")
    sch.on("vector", v_c)
    return v


def _p5a_chunk(nc, t, sch, ic, c2c):
    sem = t.sem
    s5 = t.s5
    lsem = ("str", "trs")[ic % 2]
    c7p = sch.val("c7")
    stv = sch.val("stw")
    def s_ld(e, ic=ic, c7p=c7p, stv=stv, lsem=lsem):
        e.wait_ge(sem["c7"], max(0, c7p - 1))
        if ic >= 2:
            e.wait_ge(sem["stw"], stv - 16)
        e.dma_start(out=s5[4 + ic % 2][:, :], in_=t.stash_gate.ap()[ic, :, :]).then_inc(sem[lsem], 16)
    ldv = sch.inc(lsem, 16)
    sch.on("sync", s_ld)
    c7w = sch.val("c7")
    c9w = sch.val("c9")
    def a_blk(e, ic=ic, ldv=ldv, lsem=lsem, c7w=c7w, c9w=c9w, c2c=c2c):
        e.wait_ge(sem[lsem], ldv)
        e.wait_ge(sem["c7"], c7w)
        e.wait_ge(sem["c9"], c9w)
        e.wait_ge(sem["c2"], c2c)
        e.activation(s5[6][:, :], s5[4 + ic % 2][:, :], ACT.Identity,
                     bias=t.scl[:, 12:13], scale=t.scl[:, 7:8])
        e.drain()
        e.activation(s5[7][:, :], s5[6][:, :], ACT.Identity,
                     bias=t.scl[:, 13:14], scale=t.scl[:, 5:6])
        e.drain()
        e.activation(t.slotA[:, :], s5[7][:, :], ACT.Sigmoid)
        e.drain()
        e.activation(t.slotB[:, :], t.slotA[:, :], ACT.Identity,
                     bias=t.scl[:, 12:13], scale=65536.0).then_inc(sem["c1"], 1)
    c1v = sch.inc("c1")
    sch.on("scalar", a_blk)
    def v_blk(e, ic=ic, c1v=c1v):
        e.wait_ge(sem["c1"], c1v)
        e.drain()
        e.scalar_tensor_tensor(t.slotA[:, :], t.slotB[:, :], CMAGIC + 65535.0,
                               t.scl[:, 12:13].broadcast_to((128, 2048)),
                               OP.min, OP.subtract).then_inc(sem["c9"], 1)
        e.drain()
        e.tensor_tensor(s5[4 + ic % 2][:, :], s5[7][:, :], t.slotA[:, :],
                        OP.mult).then_inc(sem["c7"], 1)
    sch.inc("c9")
    c7v = sch.inc("c7")
    sch.on("vector", v_blk)
    def v_2r(e, ic=ic, c7v=c7v):
        e.wait_ge(sem["c7"], c7v)
        e.drain()
        e.tensor_reduce(t.mxcol[:, ic:ic + 1], s5[4 + ic % 2][:, :], AX.X, OP.max,
                        apply_absolute_value=True).then_inc(sem["c6"], 1)
    c6v = sch.inc("c6")
    sch.on("vector", v_2r)
    stq = sch.val("stw")
    def g_st(e, ic=ic, c6v=c6v, stq=stq):
        e.wait_ge(sem["c6"], c6v)
        e.wait_ge(sem["stw"], stq)
        e.dma_start(out=t.stash_g2.ap()[ic, :, :], in_=s5[4 + ic % 2][:, :]).then_inc(sem["stw"], 16)
    sch.inc("stw", 16)
    sch.on("gpsimd", g_st)


def _p5b_chunk(nc, t, sch, ic, c2c):
    sem = t.sem
    s5 = t.s5
    upb = (t.slotA, t.slotB)[ic % 2]
    lsem = ("str", "trs")[ic % 2]
    c7p = sch.val("c7")
    stv = sch.val("stw")
    lag = 1 if ic >= 2 else 0
    def s_ld(e, ic=ic, c7p=c7p, stv=stv, lsem=lsem, lag=lag, upb=upb):
        e.wait_ge(sem["c7"], max(0, c7p - lag))
        e.wait_ge(sem["stw"], max(0, stv - 16))
        e.dma_start(out=s5[4 + ic % 2][:, :], in_=t.stash_g2.ap()[ic, :, :]).then_inc(sem[lsem], 16)
        e.dma_start(out=upb[:, :], in_=t.stash_up.ap()[ic, :, :]).then_inc(sem[lsem], 16)
    ldv = sch.inc(lsem, 32)
    sch.on("sync", s_ld)
    c7w = sch.val("c7")
    def a_blk(e, ic=ic, ldv=ldv, lsem=lsem, c7w=c7w, c2c=c2c, upb=upb):
        e.wait_ge(sem[lsem], ldv)
        e.wait_ge(sem["c7"], c7w)
        e.wait_ge(sem["c2"], c2c)
        e.activation(s5[6][:, :], s5[4 + ic % 2][:, :], ACT.Identity,
                     bias=t.scl[:, 12:13], scale=t.scl[:, 9:10])
        e.drain()
        e.activation(s5[7][:, :], s5[6][:, :], ACT.Identity,
                     bias=t.scl[:, 15:16], scale=t.scl[:, 8:9])
        e.drain()
        e.activation(s5[6][:, :], upb[:, :], ACT.Identity,
                     bias=t.scl[:, 12:13], scale=t.scl[:, 6:7])
        e.drain()
        e.activation(upb[:, :], s5[6][:, :], ACT.Identity,
                     bias=t.scl[:, 14:15], scale=t.scl[:, 4:5]).then_inc(sem["c1"], 1)
    c1v = sch.inc("c1")
    sch.on("scalar", a_blk)
    def v_blk(e, ic=ic, c1v=c1v, upb=upb):
        e.wait_ge(sem["c1"], c1v)
        e.drain()
        e.tensor_tensor(s5[4 + ic % 2][:, :], s5[7][:, :], upb[:, :],
                        OP.mult).then_inc(sem["c7"], 1)
    c7v = sch.inc("c7")
    sch.on("vector", v_blk)
    def v_2r(e, ic=ic, c7v=c7v):
        e.wait_ge(sem["c7"], c7v)
        e.drain()
        e.tensor_reduce(t.mxcol[:, ic:ic + 1], s5[4 + ic % 2][:, :], AX.X, OP.max,
                        apply_absolute_value=True).then_inc(sem["c6"], 1)
    c6v = sch.inc("c6")
    sch.on("vector", v_2r)
    stq = sch.val("stw")
    def g_st(e, ic=ic, c6v=c6v, stq=stq):
        e.wait_ge(sem["c6"], c6v)
        e.wait_ge(sem["stw"], stq)
        e.dma_start(out=t.stash_o.ap()[ic, :, :], in_=s5[4 + ic % 2][:, :]).then_inc(sem["stw"], 16)
    sch.inc("stw", 16)
    sch.on("gpsimd", g_st)


def _p5_arwin(nc, t, sch, idx, scale_fn, bccols, post_fn=None):
    sem = t.sem
    a2g = sch.val("a2")
    _p5_finmax(nc, t, sch, 13, a2g)
    _p5_ar(nc, t, sch, idx, None, scale_fn, bccols)
    if post_fn is not None:
        sch.on("vector", post_fn)
    return [("c2", sch.val("c2"))]


def _p5_dequant(nc, t, sch, agccd, c8proj):
    sem = t.sem
    din = t.din
    pA, pB = t.pA, t.pB
    s5 = t.s5
    ldp = sch.val("ldx")
    stwp = sch.val("stw")
    c7b = sch.val("c7")
    def s_ind(e, ldp=ldp, stwp=stwp, c7b=c7b, agccd=agccd):
        e.wait_ge(sem["c7"], c7b)
        e.wait_ge(sem["stw"], stwp)
        e.wait_ge(sem["ccd"], agccd)
        e.dma_start(out=t.ind96[:, :], in_=din["ind96"]).then_inc(sem["ldx"], 16)
    ldxv = sch.inc("ldx", 16)
    sch.on("sync", s_ind)
    n2p0 = sch.val("n2")
    def v_rtd(e, n2p0=n2p0):
        e.drain()
        e.reciprocal(t.rTd[:, :], t.sTd[:, :]).then_inc(sem["c6"], 1)
    c6rtd = sch.inc("c6")
    sch.on("vector", v_rtd)
    for ic in range(NIC):
        c7p = sch.val("c7")
        def s_ld(e, ic=ic, c7p=c7p, c8proj=c8proj):
            e.wait_ge(sem["c7"], max(0, c7p - 1))
            if ic < 2:
                e.wait_ge(sem["c8"], c8proj)
            e.dma_start(out=s5[4 + ic % 2][:, 0:1024], in_=din["wdownT"][128 * ic:128 * (ic + 1), :]
                        ).then_inc(sem[("str", "trs")[ic % 2]], 16)
        ldv = sch.inc(("str", "trs")[ic % 2], 16)
        sch.on("sync", s_ld)
        n2p = sch.val("n2")
        def t_bc(e, ic=ic, n2p=n2p, ldxv=ldxv, c6rtd=c6rtd):
            e.wait_ge(sem["n2"], n2p)
            e.wait_ge(sem["ldx"], ldxv)
            e.wait_ge(sem["c6"], c6rtd)
            e.matmul(pB[:, 0:512], t.ind96[:, 128 * ic:128 * (ic + 1)], t.rTd[:, 0:512],
                     start=True, stop=True)
            e.matmul(pB[:, 512:1024], t.ind96[:, 128 * ic:128 * (ic + 1)], t.rTd[:, 512:1024],
                     start=True, stop=True)
            e.matmul(pB[:, 1024:1536], t.ind96[:, 128 * ic:128 * (ic + 1)], t.sTd[:, 0:512],
                     start=True, stop=True)
            e.matmul(pB[:, 1536:2048], t.ind96[:, 128 * ic:128 * (ic + 1)], t.sTd[:, 512:1024],
                     start=True, stop=True).then_inc(sem["n1"], 1)
        n1v = sch.inc("n1")
        sch.on("tensor", t_bc)
        stp = sch.val("stw")
        def v_dq(e, ic=ic, ldv=ldv, n1v=n1v, stp=stp):
            e.wait_ge(sem[("str", "trs")[ic % 2]], ldv)
            e.wait_ge(sem["n1"], n1v)
            e.wait_ge(sem["stw"], stp)
            e.drain()
            e.tensor_tensor(s5[6][:, 0:1024], s5[4 + ic % 2][:, 0:1024], pB[:, 0:1024], OP.mult)
            e.drain()
            e.tensor_scalar(s5[6][:, 0:1024], s5[6][:, 0:1024], CMAGIC, CMAGIC, OP.add, OP.subtract)
            e.drain()
            e.tensor_tensor(t.wdq_s[:, 1024 * (ic % 2):1024 * (ic % 2 + 1)], s5[6][:, 0:1024],
                            pB[:, 1024:2048], OP.mult).then_inc(sem["n2"], 1)
            e.drain()
            e.nop().then_inc(sem["c7"], 1)
        sch.inc("n2")
        sch.inc("c7")
        sch.on("vector", v_dq)
        def g_st(e, ic=ic):
            pass
        n2v2 = sch.val("n2")
        def g_st2(e, ic=ic, n2v2=n2v2):
            e.wait_ge(sem["n2"], n2v2)
            e.dma_start(out=t.stash_wdq.ap()[ic, :, :],
                        in_=t.wdq_s[:, 1024 * (ic % 2):1024 * (ic % 2 + 1)]).then_inc(sem["stw"], 16)
        sch.inc("stw", 16)
        sch.on("gpsimd", g_st2)


def _p5_main(nc, t, sch, agccd, c7p5b):
    sem = t.sem
    din = t.din
    pA, pB = t.pA, t.pB
    s5 = t.s5
    stwdq = sch.val("stw")
    oqv_map = {}
    aoq_map = {}
    dn_map = {}

    def emit_oq(tqv, ic):
        lsem = ("lda", "ldb")[ic % 2]
        a1p = aoq_map.get((tqv, ic - 2) if ic >= 2 else (tqv - 1, ic + 22), 0)
        def s_lo(e, tqv=tqv, ic=ic, lsem=lsem, a1p=a1p):
            e.wait_ge(sem["c1"], a1p)
            if tqv == 0 and ic < 2:
                e.wait_ge(sem["c7"], c7p5b)
            e.dma_start(out=t.slotA[:, 512 * (ic % 2):512 * (ic % 2 + 1)],
                        in_=t.stash_o.ap()[ic, :, 512 * tqv:512 * (tqv + 1)]
                        ).then_inc(sem[lsem], 16)
        ldv = sch.inc(lsem, 16)
        sch.on("sync", s_lo)
        c2q = oqv_map.get((tqv, ic - 2) if ic >= 2 else (tqv - 1, ic + 22), 0)
        def a_oq(e, tqv=tqv, ic=ic, ldv=ldv, lsem=lsem, c2q=c2q):
            e.wait_ge(sem[lsem], ldv)
            e.wait_ge(sem["c2"], c2q)
            e.activation(t.slotB[:, 512 * (ic % 2):512 * (ic % 2 + 1)],
                         t.slotA[:, 512 * (ic % 2):512 * (ic % 2 + 1)], ACT.Identity,
                         bias=t.scl[:, 12:13], scale=t.scl[:, 11:12]).then_inc(sem["c1"], 1)
        aoq_map[(tqv, ic)] = sch.inc("c1")
        sch.on("scalar", a_oq)
        dnw = dn_map.get((tqv - 1, ic), 0)
        def v_oq(e, tqv=tqv, ic=ic, dnw=dnw, c1v=aoq_map[(tqv, ic)], agccd=agccd):
            e.wait_ge(sem["c1"], c1v)
            e.wait_ge(sem["c8"], dnw)
            if tqv == 0 and ic == 0:
                e.wait_ge(sem["ccd"], agccd)
            e.drain()
            e.tensor_scalar(t.oqbuf[:, 512 * ic:512 * (ic + 1)],
                            t.slotB[:, 512 * (ic % 2):512 * (ic % 2 + 1)], CMAGIC,
                            t.scl[:, 10:11], OP.subtract, OP.mult).then_inc(sem["c2"], 1)
        oqv_map[(tqv, ic)] = sch.inc("c2")
        sch.on("vector", v_oq)

    for ic in range(NIC):
        emit_oq(0, ic)

    for tqv in range(4):
        evp = sch.val("c5")
        c5q = sch.val("c5")
        for ic in range(NIC):
            c8p = sch.val("c8")
            def s_ldw(e, ic=ic, tqv=tqv, c8p=c8p, stwdq=stwdq, c5q=c5q):
                e.wait_ge(sem["c8"], max(0, c8p - 1))
                if ic < 2:
                    if tqv == 0:
                        e.wait_ge(sem["stw"], stwdq)
                    else:
                        e.wait_ge(sem["c5"], c5q)
                e.dma_start(out=t.wdq_s[:, 1024 * (ic % 2):1024 * (ic % 2 + 1)],
                            in_=t.stash_wdq.ap()[ic, :, :]).then_inc(sem[("str", "trs")[ic % 2]], 16)
            ldv = sch.inc(("str", "trs")[ic % 2], 16)
            sch.on("sync", s_ldw)
            def t_dn(e, ic=ic, ldv=ldv, oqw=oqv_map[(tqv, ic)], evp=evp):
                e.wait_ge(sem[("str", "trs")[ic % 2]], ldv)
                e.wait_ge(sem["c2"], oqw)
                if ic == 0:
                    e.wait_ge(sem["c5"], evp)
                mm = None
                for ts_ in range(4):
                    lhs = t.oqbuf[:, 512 * ic + 128 * ts_:512 * ic + 128 * (ts_ + 1)]
                    e.ldweights(lhs)
                    ps = (pA if ts_ < 2 else pB)
                    for hf in range(2):
                        mm = e.matmul(ps[:, 1024 * (ts_ % 2) + 512 * hf:1024 * (ts_ % 2) + 512 * (hf + 1)],
                                      lhs, t.wdq_s[:, 1024 * (ic % 2) + 512 * hf:1024 * (ic % 2) + 512 * (hf + 1)],
                                      start=(ic == 0), stop=(ic == NIC - 1))
                mm.then_inc(sem["c8"], 1)
            dn_map[(tqv, ic)] = sch.inc("c8")
            sch.on("tensor", t_dn)
            if tqv < 3:
                emit_oq(tqv + 1, ic)
        dnv = sch.val("c8")
        for ts_ in range(4):
            tt = 4 * tqv + ts_
            stop_ = sch.val("sto")
            c5g = sch.val("c5")
            def s_lx(e, tt=tt, stop_=stop_, dnv=dnv, c5g=c5g):
                e.wait_ge(sem["sto"], max(0, stop_ - 32))
                e.wait_ge(sem["c8"], dnv)
                e.wait_ge(sem["c5"], c5g)
                e.dma_start(out=s5[7][:, 0:1024], in_=din["x_own"][128 * tt:128 * (tt + 1), :]
                            ).then_inc(sem["str"], 16)
            ldv = sch.inc("str", 16)
            sch.on("sync", s_lx)
            stov = sch.val("sto")
            def v_ev(e, ts_=ts_, tt=tt, ldv=ldv, dnv=dnv, stov=stov):
                e.wait_ge(sem["c8"], dnv)
                e.wait_ge(sem["str"], ldv)
                e.wait_ge(sem["sto"], stov)
                e.drain()
                ps = (pA if ts_ < 2 else pB)
                src = ps[:, 1024 * (ts_ % 2):1024 * (ts_ % 2 + 1)]
                at = (t.aout_a if tt < 8 else t.aout_b)[:, D * (tt % 8):D * (tt % 8 + 1)]
                e.scalar_tensor_tensor(s5[6][:, 0:1024], src, 1.0, at, OP.mult, OP.add)
                e.drain()
                e.tensor_tensor(s5[6][:, 1024:2048], s5[6][:, 0:1024], s5[7][:, 0:1024],
                                OP.add).then_inc(sem["c5"], 1)
            c5v = sch.inc("c5")
            sch.on("vector", v_ev)
            def g_out(e, tt=tt, c5v=c5v):
                e.wait_ge(sem["c5"], c5v)
                e.dma_start(out=t.out_own[128 * tt:128 * (tt + 1), :],
                            in_=s5[6][:, 1024:2048]).then_inc(sem["sto"], 16)
            sch.inc("sto", 16)
            sch.on("gpsimd", g_out)
    stof = sch.val("sto")
    def g_fin(e, stof=stof):
        e.wait_ge(sem["sto"], stof)
    sch.on("gpsimd", g_fin)


def gen_dbg(nc, t, sch):
    sem = t.sem
    items = []
    if "kT" in t.debug: items.append((t.dout["kT"], t.kT[:, :]))
    if "qT" in t.debug: items.append((t.dout["qT"], t.qT[:, :]))
    if "vA" in t.debug: items.append((t.dout["vA"], t.vA[:, :]))
    if "xq" in t.debug: items.append((t.dout["xq"], t.xq_fm[:, :]))
    if "scl" in t.debug: items.append((t.dout["scl"], t.scl[:, :]))
    if "small" in t.debug: items.append((t.dout["small"], t.small[:, :]))
    if "ctxT" in t.debug: items.append((t.dout["ctxT"], t.qT[:, :]))
    if "ctxown" in t.debug: items.append((t.dout["ctxown"], t.ctx_own[:, :]))
    def g_dump(e):
        for dst, src in items:
            e.dma_start(out=dst, in_=src).then_inc(sem["sto"], 16)
        for nm in ("upst", "gatest", "g2st", "ost", "wdqst"):
            if nm in t.debug:
                stash = {"upst": t.stash_up, "gatest": t.stash_gate, "g2st": t.stash_g2,
                         "ost": t.stash_o, "wdqst": t.stash_wdq}[nm]
                e.dma_start(out=t.dout[nm], in_=stash.ap()).then_inc(sem["sto"], 16)
        if "aout" in t.debug:
            for i in range(16):
                src = (t.aout_a if i < 8 else t.aout_b)[:, D * (i % 8):D * (i % 8 + 1)]
                e.dma_start(out=t.dout["aout"][128 * i:128 * (i + 1), :], in_=src).then_inc(sem["sto"], 16)
        e.wait_ge(sem["sto"], stv)
    nst = len(items) + sum(1 for nm in ("upst", "gatest", "g2st", "ost", "wdqst") if nm in t.debug)
    if "aout" in t.debug:
        nst += 16
    stv = sch.inc("sto", 16 * nst)
    sch.on("gpsimd", g_dump)


def prep_inputs(L0_attn_x, L0_attn_w_kqv, L0_attn_w_proj, L0_attn_b_proj,
                L0_ffn_w_gate, L0_ffn_w_up, L0_ffn_w_down):
    x = np.ascontiguousarray(np.asarray(L0_attn_x, dtype=np.float32))
    wkqv = np.asarray(L0_attn_w_kqv, dtype=np.float32).reshape(16, 64, 3, D)
    wproj = np.asarray(L0_attn_w_proj, dtype=np.float32)
    bproj = np.asarray(L0_attn_b_proj, dtype=np.float32)
    wup = np.ascontiguousarray(np.asarray(L0_ffn_w_up, dtype=np.float32))
    wgate = np.ascontiguousarray(np.asarray(L0_ffn_w_gate, dtype=np.float32))
    wdown = np.ascontiguousarray(np.asarray(L0_ffn_w_down, dtype=np.float32))
    masks, ident, ones64, ind32, ind96, onesf = make_consts()

    wupT = np.ascontiguousarray(wup.T)
    wgateT = np.ascontiguousarray(wgate.T)
    wdownT = np.ascontiguousarray(wdown.T)

    in_maps = []
    for c in range(8):
        b, g = c // 2, c % 2
        wk = wkqv[8 * g:8 * (g + 1), :, 0, :].reshape(512, D)
        wq = wkqv[8 * g:8 * (g + 1), :, 1, :].reshape(512, D)
        wv = wkqv[8 * g:8 * (g + 1), :, 2, :].reshape(512, D)
        wprojT_c = np.concatenate([wproj[:, 512 * g:512 * (g + 1)].T,
                                   wproj[:, 512 * (1 - g):512 * (2 - g)].T], axis=0)
        in_maps.append({
            "x_seq": x[b],
            "x_own": np.ascontiguousarray(x[b, TOWN * g:TOWN * (g + 1), :]),
            "wkT": np.ascontiguousarray(wk.T), "wqT": np.ascontiguousarray(wq.T),
            "wvT": np.ascontiguousarray(wv.T),
            "wprojT": np.ascontiguousarray(wprojT_c),
            "b_row": bproj.reshape(1, D),
            "wupN": wup, "wupT": wupT, "wgateN": wgate, "wgateT": wgateT,
            "wdownN": wdown, "wdownT": wdownT,
            "masks": masks, "ident": ident, "ones64": ones64, "ind32": ind32, "ind96": ind96, "onesf": onesf,
            "offs": np.array([[16 * g, c, c ^ 1, g,
                               2 * g, 2 * g + 1, 4 + 2 * g, 5 + 2 * g]], dtype=np.int32),
        })
    return in_maps


def kernel(**inputs):
    in_maps = prep_inputs(**inputs)
    nc, t = build()
    res = run_bass_kernel_spmd(nc, in_maps, core_ids=list(range(8)))
    out = np.empty((4, S, D), dtype=np.float32)
    for c in range(8):
        b, g = c // 2, c % 2
        out[b, TOWN * g:TOWN * (g + 1), :] = res.results[c]["out_own"]
    return out


# version 17
# speedup vs baseline: 1.0067x; 1.0036x over previous
import numpy as np
import contextlib
import ml_dtypes
import concourse.bass as bass
import concourse.mybir as mybir
from concourse.bass_utils import run_bass_kernel_spmd

F32 = mybir.dt.float32
BF16 = mybir.dt.bfloat16
I32 = mybir.dt.int32
OP = mybir.AluOpType
AX = mybir.AxisListType
ACT = mybir.ActivationFunctionType

CMAGIC = float(np.float32(1.5 * 2 ** 23))

S = 4096
D = 1024
TOWN = 2048
NPAIR = 4
NKC = S // 128
NQT = S // 512
FFI = 3072
NIC = FFI // 128

SB_BASE = 16512
OFF_KT = SB_BASE
OFF_QT = SB_BASE + 32768
OFF_VA = SB_BASE + 65536
OFF_XNFM = SB_BASE + 98816
OFF_XQ = OFF_XNFM
OFF_CTXOWN = SB_BASE + 131584
OFF_FIX = SB_BASE + 164352
OFF_STREAM = SB_BASE + 186368


class Bag:
    pass


class Sched:
    def __init__(self):
        self.steps = []
        self.cnt = {}

    def on(self, eng, fn):
        self.steps.append((eng, fn))

    def inc(self, name, k=1):
        self.cnt[name] = self.cnt.get(name, 0) + k
        return self.cnt[name]

    def val(self, name):
        return self.cnt.get(name, 0)


def make_consts():
    masks = np.zeros((128, 4 * 512), dtype=ml_dtypes.bfloat16)
    p = np.arange(128)[:, None]
    q = np.arange(512)[None, :]
    for dv in range(4):
        masks[:, 512 * dv:512 * (dv + 1)] = (q >= 128 * dv + p).astype(ml_dtypes.bfloat16)
    ident = np.eye(128, dtype=np.float32)
    ones64 = np.zeros((128, 2), dtype=ml_dtypes.bfloat16)
    ones64[0:64, 0] = 1.0
    ones64[64:128, 1] = 1.0
    ind32 = np.zeros((32, 1024), dtype=np.float32)
    for cc in range(8):
        for pp in range(128):
            ind32[4 * cc + pp // 32, 128 * cc + pp] = 1.0
    ind96 = np.zeros((96, 3072), dtype=np.float32)
    for ic in range(24):
        for pp in range(128):
            ind96[4 * ic + pp // 32, 128 * ic + pp] = 1.0
    onesf = np.ones((128, 128), dtype=np.float32)
    return masks, ident, ones64, ind32, ind96, onesf


def build(debug=(), phases_upto=99):
    nc = bass.Bass("TRN2", target_bir_lowering=False)
    t = Bag()
    t.debug = set(debug)

    din = {}
    def di(name, shape, dtype=F32):
        din[name] = nc.dram_tensor(name, list(shape), dtype, kind="ExternalInput").ap()
    di("x_seq", (S, D))
    di("x_own", (TOWN, D))
    di("wkT", (D, 512)); di("wqT", (D, 512)); di("wvT", (D, 512))
    di("wprojT", (D, D))
    di("b_row", (1, D))
    di("wupN", (FFI, D)); di("wupT", (D, FFI))
    di("wgateN", (FFI, D)); di("wgateT", (D, FFI))
    di("wdownN", (D, FFI)); di("wdownT", (FFI, D))
    di("masks", (128, 2048), BF16)
    di("ident", (128, 128)); di("ones64", (128, 2), BF16); di("ind32", (32, 1024)); di("ind96", (96, 3072)); di("onesf", (128, 128))
    di("offs", (1, 8), I32)
    t.din = din
    t.out_own = nc.dram_tensor("out_own", [TOWN, D], F32, kind="ExternalOutput").ap()

    t.stash_up = nc.dram_tensor("stash_up", [NIC, 128, TOWN], F32, kind="Internal")
    t.stash_gate = nc.dram_tensor("stash_gate", [NIC, 128, TOWN], F32, kind="Internal")
    t.stash_g2 = nc.dram_tensor("stash_g2", [NIC, 128, TOWN], F32, kind="Internal")
    t.stash_o = nc.dram_tensor("stash_o", [NIC, 128, TOWN], F32, kind="Internal")
    t.stash_wdq = nc.dram_tensor("stash_wdq", [NIC, 128, D], BF16, kind="Internal")
    t.ar_in = [nc.dram_tensor(f"ar_in{i}", [1, 8], F32, kind="Internal") for i in range(4)]
    t.ar_out = [nc.dram_tensor(f"ar_out{i}", [1, 8], F32, kind="Internal") for i in range(4)]
    t.lnc_hbm = nc.dram_tensor("lnc_hbm", [8, 128, 8], F32, kind="Internal")
    t.ag_in = nc.dram_tensor("ag_in", [4, 128, S], BF16, kind="Internal")
    t.ag_out = nc.dram_tensor("ag_out", [8, 4, 128, S], BF16, kind="Internal")

    t.dout = {}
    def do(name, shape, dtype):
        if name in t.debug:
            t.dout[name] = nc.dram_tensor("dbg_" + name, list(shape), dtype, kind="ExternalOutput").ap()
    do("kT", (128, 4 * S), BF16)
    do("qT", (128, 4 * S), BF16)
    do("vA", (128, NKC * 520), BF16)
    do("xq", (128, 8 * TOWN), BF16)
    do("scl", (128, 32), F32)
    do("small", (128, 16), F32)
    do("ctxT", (128, 4 * S), BF16)
    do("ctxown", (128, 8 * TOWN), BF16)
    do("aout", (TOWN, D), F32)
    do("upst", (NIC, 128, TOWN), F32)
    do("gatest", (NIC, 128, TOWN), F32)
    do("g2st", (NIC, 128, TOWN), F32)
    do("ost", (NIC, 128, TOWN), F32)
    do("wdqst", (NIC, 128, D), BF16)

    es = contextlib.ExitStack()
    with es:
        def sb(name, shape, dtype, offset):
            return nc.alloc_sbuf_tensor_at(name, list(shape), dtype, offset=offset)

        t.kT = sb("kT", (128, 4 * S), BF16, OFF_KT)
        t.qT = sb("qT", (128, 4 * S), BF16, OFF_QT)
        t.vA = sb("vA", (128, NKC * 520), BF16, OFF_VA)
        t.xn_fm = sb("xn_fm", (128, 8 * TOWN), BF16, OFF_XNFM)
        t.wkqv = sb("wkqv", (128, 2 * 8 * 512), BF16, OFF_CTXOWN)
        t.wvb = sb("wvb", (128, 8 * 512), BF16, OFF_CTXOWN + 16384)
        t.xq_fm = sb("xq_fm", (128, 8 * TOWN), BF16, OFF_XQ)
        t.shift_sb = sb("shift_sb", (128, 1024), F32, OFF_XQ)
        t.praw32 = sb("praw32", (128, 1024), F32, OFF_XQ + 4096)
        t.pfin2 = sb("pfin2", (128, 2048), BF16, OFF_XQ + 8192)
        t.slotA = sb("slotA", (128, 2048), F32, OFF_XQ + 12288)
        t.slotB = sb("slotB", (128, 2048), F32, OFF_XQ + 20480)
        t.praw32b = sb("praw32b", (128, 1024), F32, OFF_XQ + 28672)
        t.ctx_own = sb("ctx_own", (128, 8 * TOWN), BF16, OFF_CTXOWN)
        t.wprojT = sb("wprojT", (128, 8 * D), BF16, OFF_VA)
        t.aout_a = sb("aout_a", (128, 8 * D), BF16, OFF_KT)
        t.aout_b = sb("aout_b", (128, 8 * D), BF16, OFF_KT + 16384)
        t.sTu = sb("sTu", (32, FFI), F32, OFF_CTXOWN)
        t.rTu = sb("rTu", (32, FFI), F32, OFF_CTXOWN + 12288)
        t.s5 = [sb(f"s5_{i}", (128, 2048), F32, OFF_QT + 8192 * i) for i in range(4)] + \
               [sb(f"s5_{i+4}", (128, 2048), F32, OFF_CTXOWN + 8192 * i) for i in range(4)]
        t.sall = sb("sall", (128, 768), F32, OFF_CTXOWN + 24576)
        t.wqb = sb("wqb", (128, 8 * 512), BF16, OFF_STREAM)
        t.ebuf0 = sb("ebuf0", (128, 1024), F32, OFF_STREAM + 8192)
        t.ebuf1 = sb("ebuf1", (128, 1024), F32, OFF_STREAM + 12288)
        t.oqbuf = sb("oqbuf", (128, NIC * 512), BF16, OFF_QT + 16384)
        t.wdq_s = sb("wdq_s", (128, 2 * D), BF16, OFF_QT + 16384 + 24576)

        fo = OFF_FIX
        t.masks = sb("masks", (128, 2048), BF16, fo); fo += 4096
        t.ident = sb("ident", (128, 128), F32, fo); fo += 512
        t.ones64 = sb("ones64", (128, 2), BF16, fo); fo += 32
        t.ind32 = sb("ind32", (32, 1024), F32, fo); fo += 4096
        t.ind96 = sb("ind96", (96, 3072), F32, OFF_QT)
        t.rTd = sb("rTd", (96, 1024), F32, OFF_QT + 12288)
        t.onesf = sb("onesf", (128, 128), F32, fo); fo += 512
        t.btile = sb("btile", (128, D), F32, fo); fo += 4096
        t.scl = sb("scl", (128, 32), F32, fo); fo += 128
        t.offs = sb("offs", (1, 8), I32, fo); fo += 32
        t.sTd = sb("sTd", (96, D), F32, fo); fo += 4096
        t.norm2 = sb("norm2", (2, 64), F32, fo); fo += 256
        t.small = sb("small", (128, 16), F32, fo); fo += 64
        t.mxcol = sb("mxcol", (128, 48), F32, fo); fo += 192
        t.negct = sb("negct", (128, 64), F32, fo); fo += 256
        assert fo <= OFF_STREAM, fo

        so = OFF_STREAM
        t.xbuf = sb("xbuf", (128, 2 * D), F32, so)
        t.ctxstage = sb("ctxstage", (128, 1024), F32, so)
        so += 8192
        t.xnbuf = sb("xnbuf", (128, D), F32, so); so += 4096
        t.xnbf = sb("xnbf", (128, 2 * D), BF16, so); so += 4096
        t.stats = sb("stats", (128, 32), F32, so); so += 128
        t.lnt = sb("lnt", (128, 32), F32, so); so += 128
        t.praw = sb("praw", (128, 1024), BF16, so)
        t.pfin = sb("pfin", (128, 1024), BF16, so + 2048)
        t.qdt = sb("qdt", (128, D), F32, so)
        t.rnorm_e = sb("rnorm_e", (128, 1024), F32, so)
        so += 4096
        t.rnorm = sb("rnorm", (128, 512), F32, so); so += 2048
        t.sqs = sb("sqs", (128, 1024), BF16, so); so += 2048
        assert so <= 229344, so

        t.pA = es.enter_context(nc.psum_tensor("pA", [128, 2048], F32))
        t.pB = es.enter_context(nc.psum_tensor("pB", [128, 2048], F32))

        sems = {}
        for name in ["ph", "ldx", "ldw", "lda", "ldb", "tra", "trb", "trs",
                     "c1", "c2", "c3", "c4", "c5", "c6", "c7", "c8", "c9",
                     "n1", "n2", "ccs", "ccd", "stw", "str", "sto",
                     "a1", "a2", "a3", "a4", "a5"]:
            sems[name] = es.enter_context(nc.semaphore(name))
        t.sem = sems

        sch = Sched()
        c2cs = {}
        def midhook(idx, part):
            if idx == 0 and part == 0:
                c2cs["a"] = _p5_sclderiv(nc, t, sch)
            if idx <= 11:
                _p5a_chunk(nc, t, sch, 2 * idx + part, c2cs["a"])
            elif 13 <= idx <= 24:
                _p5b_chunk(nc, t, sch, 2 * (idx - 13) + part, c2cs["b"])
        def prehook(idx):
            if idx == 13:
                def v_c15(e):
                    e.drain()
                    e.tensor_scalar(t.scl[:, 15:16], t.scl[:, 8:9], -CMAGIC, None,
                                    OP.mult).then_inc(t.sem["c2"], 1)
                r = _p5_arwin(nc, t, sch, 2, _p5_sc3(t), (2, 8), post_fn=v_c15)
                c2cs["b"] = sch.inc("c2")
                return r
            if idx == 26:
                return _p5_arwin(nc, t, sch, 3, _p5_sc4(t), (2, 10))
            return []
        def gen_p3x(nc, t, sch):
            gen_p3(nc, t, sch, midhook=midhook, prehook=prehook)
        def gen_p45(nc, t, sch):
            c7p5b = sch.val("c7")
            agccd, ccs_ag = _p4_ag_start(nc, t, sch)
            agv = _p4_extract(nc, t, sch, ccs_ag)
            _p4_proj(nc, t, sch, agv)
            c8proj = sch.val("c8")
            _p5_dequant(nc, t, sch, agccd, c8proj)
            _p5_main(nc, t, sch, agccd, c7p5b)
        gens = [gen_p0, gen_p1, gen_p2a, gen_p2b, gen_p3x, gen_p45]
        nph = 0
        for gi, gen in enumerate(gens):
            if gi >= phases_upto:
                break
            gen(nc, t, sch)
            sch.steps.append(("BARRIER", dict(sch.cnt)))
            nph += 1
        gen_dbg(nc, t, sch)
        sch.steps.append(("BARRIER", dict(sch.cnt)))
        nph += 1

        block = es.enter_context(nc.Block())

        SEM_ISSUER = {"lda": "sync", "ldb": "sync", "ldx": "sync", "ldw": "gpsimd",
                      "tra": "scalar", "trb": "scalar", "trs": "sync",
                      "ccd": "gpsimd", "stw": "gpsimd", "str": "sync", "sto": "gpsimd"}

        def runner(eng):
            def body(e):
                k = 0
                for s_eng, fn in sch.steps:
                    if s_eng == "BARRIER":
                        k += 1
                        for sname, issuer in SEM_ISSUER.items():
                            if issuer == eng and fn.get(sname, 0) > 0:
                                e.wait_ge(sems[sname], fn[sname])
                        e.drain()
                        e.nop().then_inc(sems["ph"], 1)
                        e.wait_ge(sems["ph"], 5 * k)
                    elif s_eng == eng:
                        fn(e)
                return None
            return body

        block.sync(runner("sync"))
        block.scalar(runner("scalar"))
        block.vector(runner("vector"))
        block.tensor(runner("tensor"))
        block.gpsimd(runner("gpsimd"))

    return nc, t


def gen_p0(nc, t, sch):
    sem = t.sem
    din = t.din

    ldx0 = sch.val("ldx")
    def s_loads(e):
        e.dma_start(out=t.masks[:, :], in_=din["masks"]).then_inc(sem["ldx"], 16)
        e.dma_start(out=t.ident[:, :], in_=din["ident"]).then_inc(sem["ldx"], 16)
        e.dma_start(out=t.ones64[:, :], in_=din["ones64"]).then_inc(sem["ldx"], 16)
        e.dma_start(out=t.ind32[:, :], in_=din["ind32"]).then_inc(sem["ldx"], 16)
        e.dma_start(out=t.onesf[:, :], in_=din["onesf"]).then_inc(sem["ldx"], 16)
        e.dma_start(out=t.offs[:, :], in_=din["offs"]).then_inc(sem["ldx"], 16)
        e.dma_start(out=t.xbuf[0:1, 0:D], in_=din["b_row"]).then_inc(sem["ldx"], 16)
    for _ in range(7):
        sch.inc("ldx", 16)
    sch.on("sync", s_loads)
    v_ld = sch.val("ldx")

    def g_loads(e):
        e.dma_start(out=t.wkqv[:, 0:8 * 512].rearrange("p (c f) -> p c f", c=8),
                    in_=din["wkT"].rearrange("(c p) f -> p c f", p=128)).then_inc(sem["ldw"], 16)
        e.dma_start(out=t.wkqv[:, 8 * 512:16 * 512].rearrange("p (c f) -> p c f", c=8),
                    in_=din["wqT"].rearrange("(c p) f -> p c f", p=128)).then_inc(sem["ldw"], 16)
        e.dma_start(out=t.wvb[:, :].rearrange("p (c f) -> p c f", c=8),
                    in_=din["wvT"].rearrange("(c p) f -> p c f", p=128)).then_inc(sem["ldw"], 16)
        ones_ap = t.vA[:, :].rearrange("p (c h e) -> p c h e", c=NKC, h=8)[:, :, :, 64:65]
        e.memset(ones_ap, 1.0)
        e.memset(t.scl[:, :], 0.0)
        e.memset(t.small[:, :], 0.0)
        e.drain()
        e.memset(t.scl[:, 12:13], CMAGIC)
    for _ in range(3):
        sch.inc("ldw", 16)
    sch.on("gpsimd", g_loads)

    def t_btile(e):
        e.wait_ge(sem["ldx"], v_ld)
        e.matmul(t.pA[:, 0:512], t.onesf[0:1, :], t.xbuf[0:1, 0:512], start=True, stop=True)
        e.matmul(t.pA[:, 512:1024], t.onesf[0:1, :], t.xbuf[0:1, 512:1024],
                 start=True, stop=True).then_inc(sem["c1"], 1)
    c1v = sch.inc("c1")
    sch.on("tensor", t_btile)

    def v_btile(e):
        e.wait_ge(sem["c1"], c1v)
        e.tensor_copy(t.btile[:, :], t.pA[:, 0:1024])
    sch.on("vector", v_btile)


def gen_ln(nc, t, sch, tiles, x_in_fn, fm_dst_fn, absmax_base=None, qdq=False,
           loader=("sync", ("lda", "ldb")), cache_to=None, stats_cached=False):
    sem = t.sem
    NG = tiles // 8
    for g in range(NG):
        if stats_cached:
            if g == 0:
                def g_push(e):
                    e.dma_start(out=t.lnc_hbm.ap().rearrange("b p c -> p b c"),
                                in_=t.negct[:, 0:64].rearrange("p (b c) -> p b c", c=8)
                                ).then_inc(sem["ccd"], 16)
                sch.inc("ccd", 16)
                sch.on("gpsimd", g_push)
            pushv = sch.val("ccd")
            c3c = sch.val("c3")
            def g_stat(e, g=g, c3c=c3c, pushv=pushv):
                e.wait_ge(sem["ccd"], pushv)
                e.wait_ge(sem["c3"], c3c)
                mnoff = e.value_load(t.offs[0:1, (4 if g == 0 else 5):(5 if g == 0 else 6)])
                nmoff = e.value_load(t.offs[0:1, (6 if g == 0 else 7):(7 if g == 0 else 8)])
                e.dma_start(out=t.lnt[:, 0:8],
                            in_=t.lnc_hbm.ap()[bass.ts(mnoff, 1), :, :]).then_inc(sem["ccd"], 16)
                e.dma_start(out=t.lnt[:, 24:32],
                            in_=t.lnc_hbm.ap()[bass.ts(nmoff, 1), :, :]).then_inc(sem["ccd"], 16)
            ldsv = sch.inc("ccd", 32)
            sch.on("gpsimd", g_stat)
            for ji in range(8):
                j = 8 * g + ji
                c3p = sch.val("c3")
                leng = loader[0]
                lsem = loader[1][j % 2]
                def s_load2(e, j=j, c3p=c3p, lsem=lsem):
                    e.wait_ge(sem["c3"], max(0, c3p - 1))
                    e.dma_start(out=t.xbuf[:, (j % 2) * D:(j % 2 + 1) * D],
                                in_=x_in_fn(e, j)).then_inc(sem[lsem], 16)
                ldv = sch.inc(lsem, 16)
                sch.on(leng, s_load2)
                trv = sch.val(("tra", "trb")[j % 2])
                def v_apply(e, j=j, ji=ji, ldv=ldv, trv=trv, lsem=lsem,
                            tsem=("tra", "trb")[j % 2], ldsv=ldsv):
                    e.wait_ge(sem[lsem], ldv)
                    e.wait_ge(sem["ccd"], ldsv)
                    e.wait_ge(sem[tsem], trv)
                    e.drain()
                    ln = t.lnt
                    e.tensor_scalar(t.xnbuf[:, :], t.xbuf[:, (j % 2) * D:(j % 2 + 1) * D],
                                    ln[:, 24 + ji:25 + ji], ln[:, ji:ji + 1],
                                    OP.mult, OP.add).then_inc(sem["c3"], 1)
                    e.drain()
                    e.tensor_scalar(t.qdt[:, :], t.xnbuf[:, :], t.scl[:, 1:2], CMAGIC,
                                    OP.mult, OP.add)
                    e.drain()
                    e.tensor_scalar(t.xnbf[:, (j % 2) * D:(j % 2 + 1) * D], t.qdt[:, :],
                                    CMAGIC, t.scl[:, 0:1], OP.subtract, OP.mult).then_inc(sem["c2"], 1)
                sch.inc("c3")
                c2v = sch.inc("c2")
                sch.on("vector", v_apply)
                tsem = ("tra", "trb")[j % 2]
                def a_tr(e, j=j, c2v=c2v, tsem=tsem):
                    e.wait_ge(sem["c2"], c2v)
                    for c in range(8):
                        e.dma_start(out=fm_dst_fn(j, c),
                                    in_=t.xnbf[:, (j % 2) * D + 128 * c:(j % 2) * D + 128 * (c + 1)],
                                    transpose=True).then_inc(sem[tsem], 16)
                sch.inc(tsem, 16 * 8)
                sch.on("scalar", a_tr)
            continue
        for ji in range(8):
            j = 8 * g + ji
            c1p = sch.val("c1")
            c9p = sch.val("c9")
            c3p = sch.val("c3")
            leng = loader[0]
            lsem = loader[1][j % 2]
            def s_load(e, j=j, c1p=c1p, c9p=c9p, c3p=c3p, lsem=lsem):
                e.wait_ge(sem["c1"], max(0, c1p - 1))
                e.wait_ge(sem["c9"], max(0, c9p - 1))
                e.wait_ge(sem["c3"], c3p)
                e.dma_start(out=t.xbuf[:, (j % 2) * D:(j % 2 + 1) * D],
                            in_=x_in_fn(e, j)).then_inc(sem[lsem], 16)
            ldv = sch.inc(lsem, 16)
            sch.on(leng, s_load)
            c1g = sch.val("c1")
            def a_sq(e, j=j, ldv=ldv, lsem=lsem, c1g=c1g):
                e.wait_ge(sem[lsem], ldv)
                e.wait_ge(sem["c1"], c1g)
                sc = 2 * (j % 16)
                e.activation(t.qdt[:, :], t.xbuf[:, (j % 2) * D:(j % 2 + 1) * D], ACT.Square,
                             accum_out=t.stats[:, sc + 1:sc + 2]).then_inc(sem["c1"], 1)
            sch.inc("c1")
            sch.on("scalar", a_sq)
            def v_sum(e, j=j, ldv=ldv, lsem=lsem):
                e.wait_ge(sem[lsem], ldv)
                sc = 2 * (j % 16)
                e.reduce_sum(t.stats[:, sc:sc + 1],
                             t.xbuf[:, (j % 2) * D:(j % 2 + 1) * D], AX.X).then_inc(sem["c9"], 1)
            sch.inc("c9")
            sch.on("vector", v_sum)
        c1v = sch.val("c1")
        def v_chain(e, g=g, c1v=c1v):
            e.wait_ge(sem["c1"], c1v)
            e.drain()
            base = 16 * (g % 2)
            st = t.stats[:, base:base + 16].rearrange("p (j two) -> p j two", two=2)
            ln = t.lnt
            e.tensor_scalar(ln[:, 0:8], st[:, :, 0], 1.0 / D, None, OP.mult)
            e.tensor_scalar(ln[:, 8:16], st[:, :, 1], 1.0 / D, None, OP.mult)
            e.drain()
            e.tensor_tensor(ln[:, 16:24], ln[:, 0:8], ln[:, 0:8], OP.mult)
            e.drain()
            e.scalar_tensor_tensor(ln[:, 24:32], ln[:, 8:16], 1e-5, ln[:, 16:24],
                                   OP.add, OP.subtract)
            e.drain()
            e.reciprocal(ln[:, 16:24], ln[:, 24:32]).then_inc(sem["c4"], 1)
        c4v = sch.inc("c4")
        sch.on("vector", v_chain)
        def a_sqrt(e, c4v=c4v):
            e.wait_ge(sem["c4"], c4v)
            e.activation(t.lnt[:, 24:32], t.lnt[:, 16:24], ACT.Sqrt).then_inc(sem["c5"], 1)
        c5v = sch.inc("c5")
        sch.on("scalar", a_sqrt)
        def v_nmr(e, g=g, c5v=c5v, cache_to=cache_to):
            e.wait_ge(sem["c5"], c5v)
            ln = t.lnt
            e.tensor_tensor(ln[:, 16:24], ln[:, 0:8], ln[:, 24:32], OP.mult)
            e.drain()
            e.tensor_scalar(ln[:, 0:8], ln[:, 16:24], -1.0, None, OP.mult)
            e.drain()
            if cache_to is not None:
                cb = cache_to + 8 * g
                e.tensor_copy(t.negct[:, cb:cb + 8], ln[:, 0:8])
                e.tensor_copy(t.negct[:, 32 + cb:32 + cb + 8], ln[:, 24:32])
                e.drain()
        sch.on("vector", v_nmr)
        for ji in range(8):
            j = 8 * g + ji
            c3p = sch.val("c3")
            leng = loader[0]
            lsem = loader[1][j % 2]
            c1p2 = sch.val("c1")
            c9p2 = sch.val("c9")
            def s_load2(e, j=j, c3p=c3p, lsem=lsem, c1p2=c1p2, c9p2=c9p2):
                e.wait_ge(sem["c3"], max(0, c3p - 1))
                e.wait_ge(sem["c1"], c1p2)
                e.wait_ge(sem["c9"], c9p2)
                e.dma_start(out=t.xbuf[:, (j % 2) * D:(j % 2 + 1) * D],
                            in_=x_in_fn(e, j)).then_inc(sem[lsem], 16)
            ldv = sch.inc(lsem, 16)
            sch.on(leng, s_load2)
            trv = sch.val(("tra", "trb")[j % 2])
            def v_apply(e, j=j, ji=ji, ldv=ldv, trv=trv, lsem=lsem, tsem=("tra", "trb")[j % 2]):
                e.wait_ge(sem[lsem], ldv)
                e.wait_ge(sem[tsem], trv)
                e.drain()
                ln = t.lnt
                e.tensor_scalar(t.xnbuf[:, :], t.xbuf[:, (j % 2) * D:(j % 2 + 1) * D],
                                ln[:, 24 + ji:25 + ji], ln[:, ji:ji + 1],
                                OP.mult, OP.add).then_inc(sem["c3"], 1)
                e.drain()
                if not qdq:
                    e.tensor_copy(t.xnbf[:, (j % 2) * D:(j % 2 + 1) * D],
                                  t.xnbuf[:, :]).then_inc(sem["c2"], 1)
                else:
                    e.tensor_scalar(t.qdt[:, :], t.xnbuf[:, :], t.scl[:, 1:2], CMAGIC,
                                    OP.mult, OP.add)
                    e.drain()
                    e.tensor_scalar(t.xnbf[:, (j % 2) * D:(j % 2 + 1) * D], t.qdt[:, :],
                                    CMAGIC, t.scl[:, 0:1], OP.subtract, OP.mult).then_inc(sem["c2"], 1)
                if absmax_base is not None:
                    e.drain()
                    e.tensor_reduce(t.mxcol[:, absmax_base + j:absmax_base + j + 1],
                                    t.xnbuf[:, :], AX.X, OP.max, apply_absolute_value=True)
            sch.inc("c3")
            c2v = sch.inc("c2")
            sch.on("vector", v_apply)
            tsem = ("tra", "trb")[j % 2]
            def a_tr(e, j=j, c2v=c2v, tsem=tsem):
                e.wait_ge(sem["c2"], c2v)
                for c in range(8):
                    e.dma_start(out=fm_dst_fn(j, c),
                                in_=t.xnbf[:, (j % 2) * D + 128 * c:(j % 2) * D + 128 * (c + 1)],
                                transpose=True).then_inc(sem[tsem], 16)
            sch.inc(tsem, 16 * 8)
            sch.on("scalar", a_tr)


def gen_p1(nc, t, sch):
    sem = t.sem
    din = t.din
    pA, pB = t.pA, t.pB

    def x_in(e, j):
        return din["x_seq"][128 * j:128 * (j + 1), :]

    def fm_dst(j, c):
        return t.xn_fm[:, TOWN * c + 128 * j: TOWN * c + 128 * (j + 1)]

    def kqv_half(th):
        trva = sch.val("tra")
        trvb = sch.val("trb")
        ldwv = sch.val("ldw")
        for wi in range(2):
            for fc in range(4):
                j = wi * 4 + fc
                evv = sch.val("c6")
                def t_mm(e, wi=wi, fc=fc, th=th, trva=trva, trvb=trvb, ldwv=ldwv, evv=evv):
                    e.wait_ge(sem["ldw"], ldwv)
                    e.wait_ge(sem["tra"], trva)
                    e.wait_ge(sem["trb"], trvb)
                    e.wait_ge(sem["c6"], evv)
                    wbase = (wi * 8 + 0) * 512
                    for cc in range(8):
                        lhs = t.wkqv[:, wi * 4096 + 512 * cc + 128 * fc:
                                        wi * 4096 + 512 * cc + 128 * (fc + 1)]
                        e.ldweights(lhs)
                        for tq in range(4):
                            mm = e.matmul(pA[:, 512 * tq:512 * (tq + 1)], lhs,
                                          t.xn_fm[:, TOWN * cc + 512 * tq:
                                                     TOWN * cc + 512 * (tq + 1)],
                                          start=(cc == 0), stop=(cc == 7))
                    mm.then_inc(sem["c7"], 1)
                c7v = sch.inc("c7")
                sch.on("tensor", t_mm)
                def v_evac(e, wi=wi, fc=fc, th=th, c7v=c7v):
                    e.wait_ge(sem["c7"], c7v)
                    dst = (t.kT if wi == 0 else t.qT)
                    e.tensor_copy(dst[:, S * fc + 2048 * th:S * fc + 2048 * (th + 1)],
                                  pA[:, 0:2048]).then_inc(sem["c6"], 1)
                sch.inc("c6")
                sch.on("vector", v_evac)

    def v_half(th):
        trva = sch.val("tra")
        trvb = sch.val("trb")
        for tc in range(16 * th, 16 * (th + 1)):
            evv = sch.val("c9")
            slot = tc % 2
            def t_vmm(e, tc=tc, slot=slot, trva=trva, trvb=trvb, evv=evv):
                e.wait_ge(sem["tra"], trva)
                e.wait_ge(sem["trb"], trvb)
                e.wait_ge(sem["c9"], max(0, evv - 1))
                for cc in range(8):
                    tl = tc % 16
                    lhs = t.xn_fm[:, TOWN * cc + 128 * tl:TOWN * cc + 128 * (tl + 1)]
                    e.ldweights(lhs)
                    mm = e.matmul(pB[:, 512 * slot:512 * (slot + 1)], lhs, t.wvb[:, 512 * cc:512 * (cc + 1)],
                                  start=(cc == 0), stop=(cc == 7))
                mm.then_inc(sem["c8"], 1)
            c8v = sch.inc("c8")
            sch.on("tensor", t_vmm)
            def v_vevac(e, tc=tc, slot=slot, c8v=c8v):
                e.wait_ge(sem["c8"], c8v)
                src = pB[:, 512 * slot:512 * (slot + 1)].rearrange("p (h d) -> p h d", h=8)
                dst = t.vA[:, 520 * tc:520 * (tc + 1)].rearrange("p (h e) -> p h e", h=8)[:, :, 0:64]
                e.tensor_copy(dst, src).then_inc(sem["c9"], 1)
            sch.inc("c9")
            sch.on("vector", v_vevac)

    gen_ln(nc, t, sch, 16, x_in, fm_dst, absmax_base=0, cache_to=0)
    kqv_half(0)
    v_half(0)
    c7v, c8v = sch.val("c7"), sch.val("c8")
    def a_waitreuse(e, c7v=c7v, c8v=c8v):
        e.wait_ge(sem["c7"], c7v)
        e.wait_ge(sem["c8"], c8v)
    sch.on("scalar", a_waitreuse)
    def x_in2(e, j):
        return din["x_seq"][128 * (j + 16):128 * (j + 17), :]
    gen_ln(nc, t, sch, 16, x_in2, fm_dst, absmax_base=16, cache_to=16)
    kqv_half(1)
    v_half(1)

    vfin = sch.val("c3")
    n2p = sch.val("n2")
    def v_negb(e, vfin=vfin):
        e.wait_ge(sem["c3"], vfin)
        e.drain()
        e.tensor_reduce(t.small[:, 6:7], t.mxcol[:, 0:32], AX.X, OP.max).then_inc(sem["n1"], 1)
    n1v3 = sch.inc("n1")
    sch.on("vector", v_negb)
    def t_ntr3(e, n1v3=n1v3, n2p=n2p):
        e.wait_ge(sem["n1"], n1v3)
        e.wait_ge(sem["n2"], n2p)
        e.transpose(pB[0:1, 8:136], t.small[:, 6:7], t.ident[:, :]).then_inc(sem["n2"], 1)
    n2v3 = sch.inc("n2")
    sch.on("tensor", t_ntr3)
    def v_gmax(e, n2v3=n2v3):
        e.wait_ge(sem["n2"], n2v3)
        e.tensor_reduce(t.small[0:1, 7:8], pB[0:1, 8:136], AX.X, OP.max).then_inc(sem["n1"], 1)
    sch.inc("n1")
    sch.on("vector", v_gmax)


def gen_p2a(nc, t, sch):
    sem = t.sem
    din = t.din
    pB = t.pB

    ccd1 = sch.inc("ccd", 16)
    ccs1 = sch.inc("ccs", 1)
    ccd2 = sch.inc("ccd", 16)
    arv = sch.inc("c1")
    def g_ar(e, ccd1=ccd1, ccs1=ccs1, ccd2=ccd2):
        e.dma_start(out=t.ar_in[0].ap()[0:1, 0:8], in_=t.small[0:1, 0:8]).then_inc(sem["ccd"], 16)
        e.wait_ge(sem["ccd"], ccd1)
        e.collective_compute("AllReduce", OP.max, replica_groups=[list(range(8))],
                             ins=[t.ar_in[0].ap()], outs=[t.ar_out[0].ap()]).then_inc(sem["ccs"], 1)
        e.wait_ge(sem["ccs"], ccs1)
        e.dma_start(out=t.small[0:1, 8:9], in_=t.ar_out[0].ap()[0:1, 7:8]).then_inc(sem["ccd"], 16)
        e.wait_ge(sem["ccd"], ccd2)
        e.nop().then_inc(sem["c1"], 1)
    sch.on("gpsimd", g_ar)

    def v_scales(e, arv=arv):
        e.wait_ge(sem["c1"], arv)
        e.tensor_scalar(t.small[0:1, 9:10], t.small[0:1, 8:9], 1.0 / 32767.0, 1e-12, OP.mult, OP.max)
        e.drain()
        e.reciprocal(t.small[0:1, 10:11], t.small[0:1, 9:10]).then_inc(sem["c2"], 1)
    c2v = sch.inc("c2")
    sch.on("vector", v_scales)
    def t_bcast(e, c2v=c2v):
        e.wait_ge(sem["c2"], c2v)
        e.matmul(pB[:, 0:2], t.onesf[0:1, :], t.small[0:1, 9:11], start=True, stop=True).then_inc(sem["c1"], 1)
    c1v = sch.inc("c1")
    sch.on("tensor", t_bcast)
    def v_sc2(e, c1v=c1v):
        e.wait_ge(sem["c1"], c1v)
        e.tensor_copy(t.scl[:, 0:2], pB[:, 0:2]).then_inc(sem["c2"], 1)
    scv = sch.inc("c2")
    sch.on("vector", v_sc2)

    def x_in(e, j):
        return din["x_own"][128 * j:128 * (j + 1), :]

    def fm_dst(j, c):
        return t.xq_fm[:, TOWN * c + 128 * j: TOWN * c + 128 * (j + 1)]

    def v_wait(e, scv=scv):
        e.wait_ge(sem["c2"], scv)
    sch.on("vector", v_wait)
    gen_ln(nc, t, sch, 16, x_in, fm_dst, absmax_base=None, qdq=True, stats_cached=True)


def _gen_wscales(nc, t, sch, wname, nchunks, blocks_per_chunk, dst_sT, halfload):
    sem = t.sem
    din = t.din
    for i in range(nchunks):
        nb = blocks_per_chunk
        rounds = 2 if halfload else 1
        for h in range(rounds):
            cw = 1536 if halfload else nb * 32
            c6p = sch.val("c6")
            c7p = sch.val("c7")
            c8p = sch.val("c8")
            def s_ld(e, i=i, h=h, cw=cw, c6p=c6p, c7p=c7p, c8p=c8p, wname=wname):
                e.wait_ge(sem["c6"], c6p)
                e.wait_ge(sem["c7"], c7p)
                e.wait_ge(sem["c8"], c8p)
                e.dma_start(out=t.xbuf[:, 0:cw],
                            in_=din[wname][128 * i:128 * (i + 1), 1536 * h:1536 * h + cw]
                            ).then_inc(sem["lda"], 16)
            ldv = sch.inc("lda", 16)
            sch.on("sync", s_ld)
            nbh = nb // rounds
            def v_red(e, i=i, h=h, cw=cw, ldv=ldv, nb=nb, nbh=nbh):
                e.wait_ge(sem["lda"], ldv)
                wv = t.xbuf[:, 0:cw].rearrange("p (b e) -> p b e", e=32)
                e.tensor_reduce(t.qdt[:, 0:nbh], wv, AX.X, OP.max, apply_absolute_value=True)
                e.drain()
                e.tensor_scalar(t.sall[:, nb * i + nbh * h:nb * i + nbh * (h + 1)], t.qdt[:, 0:nbh],
                                1.0 / 7.0, 1e-12, OP.mult, OP.max).then_inc(sem["c6"], 1)
            sch.inc("c6")
            sch.on("vector", v_red)
    for i in range(nchunks):
        c6v = sch.val("c6")
        n2p = sch.val("n2")
        nb = blocks_per_chunk
        def t_tr(e, i=i, c6v=c6v, n2p=n2p, nb=nb):
            e.wait_ge(sem["c6"], c6v)
            e.wait_ge(sem["n2"], n2p)
            e.transpose(t.pB[0:nb, 0:128], t.sall[:, nb * i:nb * (i + 1)],
                        t.ident[:, :]).then_inc(sem["n1"], 1)
        n1v = sch.inc("n1")
        sch.on("tensor", t_tr)
        def v_ev(e, i=i, n1v=n1v, nb=nb, dst_sT=dst_sT):
            e.wait_ge(sem["n1"], n1v)
            e.tensor_copy(dst_sT[0:nb, 128 * i:128 * (i + 1)], t.pB[0:nb, 0:128]).then_inc(sem["n2"], 1)
        sch.inc("n2")
        sch.on("vector", v_ev)


def _gen_updown_mm(nc, t, sch, wTname, stash, maxcol0):
    sem = t.sem
    din = t.din
    pA, pB = t.pA, t.pB
    for isl in range(6):
        for cc in range(8):
            c7p = sch.val("c7")
            c6p = sch.val("c6")
            def s_ld(e, isl=isl, cc=cc, c7p=c7p, c6p=c6p, wTname=wTname):
                e.wait_ge(sem["c7"], max(0, c7p - 1))
                e.wait_ge(sem["c6"], c6p)
                e.dma_start(out=t.qdt[:, 512 * (cc % 2):512 * (cc % 2 + 1)],
                            in_=din[wTname][128 * cc:128 * (cc + 1), 512 * isl:512 * (isl + 1)]
                            ).then_inc(sem[("lda", "ldb")[cc % 2]], 16)
            ldv = sch.inc(("lda", "ldb")[cc % 2], 16)
            sch.on("sync", s_ld)
            n2p = sch.val("n2")
            c6r = sch.val("c6")
            def t_bc(e, isl=isl, cc=cc, n2p=n2p, c6r=c6r):
                e.wait_ge(sem["n2"], n2p)
                e.wait_ge(sem["c6"], c6r)
                e.matmul(pB[:, 1024:1536], t.ind32[:, 128 * cc:128 * (cc + 1)],
                         t.rTu[:, 512 * isl:512 * (isl + 1)],
                         start=True, stop=True)
                e.matmul(pB[:, 1536:2048], t.ind32[:, 128 * cc:128 * (cc + 1)],
                         t.sTu[:, 512 * isl:512 * (isl + 1)],
                         start=True, stop=True).then_inc(sem["n1"], 1)
            n1v = sch.inc("n1")
            sch.on("tensor", t_bc)
            mm8 = sch.val("c8")
            def v_dq(e, isl=isl, cc=cc, ldv=ldv, n1v=n1v, mm8=mm8):
                e.wait_ge(sem[("lda", "ldb")[cc % 2]], ldv)
                e.wait_ge(sem["n1"], n1v)
                e.wait_ge(sem["c8"], mm8)
                src = t.qdt[:, 512 * (cc % 2):512 * (cc % 2 + 1)]
                e.tensor_tensor(t.rnorm[:, :], src, pB[:, 1024:1536], OP.mult)
                e.drain()
                e.tensor_scalar(t.rnorm[:, :], t.rnorm[:, :], CMAGIC, CMAGIC, OP.add, OP.subtract)
                e.drain()
                e.tensor_tensor(t.wqb[:, 512 * cc:512 * (cc + 1)], t.rnorm[:, :], pB[:, 1536:2048],
                                OP.mult).then_inc(sem["n2"], 1)
                e.drain()
                e.nop().then_inc(sem["c7"], 1)
            sch.inc("n2")
            sch.inc("c7")
            sch.on("vector", v_dq)
        dqv = sch.val("n2")
        for il in range(4):
            ic = 4 * isl + il
            evp = sch.val("c5")
            def t_mm(e, il=il, dqv=dqv, evp=evp):
                e.wait_ge(sem["n2"], dqv)
                e.wait_ge(sem["c5"], evp)
                for cc in range(8):
                    lhs = t.wqb[:, 512 * cc + 128 * il:512 * cc + 128 * (il + 1)]
                    e.ldweights(lhs)
                    for tq in range(4):
                        mm = e.matmul(pA[:, 512 * tq:512 * (tq + 1)], lhs,
                                      t.xq_fm[:, TOWN * cc + 512 * tq:TOWN * cc + 512 * (tq + 1)],
                                      start=(cc == 0), stop=(cc == 7))
                mm.then_inc(sem["c8"], 1)
            c8v = sch.inc("c8")
            sch.on("tensor", t_mm)
            stp = sch.val("stw")
            def v_evac(e, ic=ic, c8v=c8v, stp=stp, maxcol0=maxcol0):
                e.wait_ge(sem["c8"], c8v)
                e.wait_ge(sem["stw"], stp)
                e.drain()
                e.tensor_copy(t.ebuf0[:, :], pA[:, 0:1024]).then_inc(sem["c5"], 1)
                e.tensor_copy(t.ebuf1[:, :], pA[:, 1024:2048]).then_inc(sem["c5"], 1)
                e.drain()
                e.tensor_reduce(t.mxcol[:, maxcol0 + 2 * (ic % 12):maxcol0 + 2 * (ic % 12) + 1],
                                t.ebuf0[:, :], AX.X, OP.max, apply_absolute_value=True)
                e.drain()
                e.tensor_reduce(t.mxcol[:, maxcol0 + 2 * (ic % 12) + 1:maxcol0 + 2 * (ic % 12) + 2],
                                t.ebuf1[:, :], AX.X, OP.max, apply_absolute_value=True).then_inc(sem["c6"], 1)
            sch.inc("c5", 2)
            c6v = sch.inc("c6")
            sch.on("vector", v_evac)
            if ic % 12 == 11:
                def v_mrg(e, ic=ic, c6v=c6v, maxcol0=maxcol0):
                    e.wait_ge(sem["c6"], c6v)
                    e.drain()
                    e.tensor_reduce(t.mxcol[:, maxcol0 + 24 + (ic // 12):maxcol0 + 25 + (ic // 12)],
                                    t.mxcol[:, maxcol0:maxcol0 + 24], AX.X, OP.max).then_inc(sem["c6"], 1)
                sch.inc("c6")
                sch.on("vector", v_mrg)
            def g_st(e, ic=ic, c6v=c6v, stash=stash):
                e.wait_ge(sem["c6"], c6v)
                e.dma_start(out=stash.ap()[ic, :, 0:1024], in_=t.ebuf0[:, :]).then_inc(sem["stw"], 16)
                e.dma_start(out=stash.ap()[ic, :, 1024:2048], in_=t.ebuf1[:, :]).then_inc(sem["stw"], 16)
            sch.inc("stw", 32)
            sch.on("gpsimd", g_st)


def gen_p2b(nc, t, sch):
    sem = t.sem
    pB = t.pB
    def gen_recip():
        c6v = sch.val("c6")
        n2v = sch.val("n2")
        def v_rc(e, c6v=c6v, n2v=n2v):
            e.wait_ge(sem["c6"], c6v)
            e.wait_ge(sem["n2"], n2v)
            e.reciprocal(t.rTu[:, :], t.sTu[:, :]).then_inc(sem["c6"], 1)
        sch.inc("c6")
        sch.on("vector", v_rc)

    _gen_wscales(nc, t, sch, "wupN", NIC, 32, t.sTu, False)
    gen_recip()
    _gen_updown_mm(nc, t, sch, "wupT", t.stash_up, 0)
    _gen_finmax(nc, t, sch, 0, 11)
    _gen_wscales(nc, t, sch, "wgateN", NIC, 32, t.sTu, False)
    gen_recip()
    _gen_updown_mm(nc, t, sch, "wgateT", t.stash_gate, 0)
    _gen_finmax(nc, t, sch, 0, 12)
    _gen_wscales(nc, t, sch, "wdownN", 8, 96, t.sTd, True)
    ccd1 = sch.inc("ccd", 16)
    ccs1 = sch.inc("ccs", 1)
    ccd2 = sch.inc("ccd", 16)
    c6f = sch.val("c6")
    arv = sch.inc("c1")
    def g_ar(e, ccd1=ccd1, ccs1=ccs1, ccd2=ccd2, c6f=c6f):
        e.wait_ge(sem["c6"], c6f)
        e.dma_start(out=t.ar_in[1].ap()[0:1, 0:8], in_=t.small[0:1, 8:16]).then_inc(sem["ccd"], 16)
        e.wait_ge(sem["ccd"], ccd1)
        e.collective_compute("AllReduce", OP.max, replica_groups=[list(range(8))],
                             ins=[t.ar_in[1].ap()], outs=[t.ar_out[1].ap()]).then_inc(sem["ccs"], 1)
        e.wait_ge(sem["ccs"], ccs1)
        e.dma_start(out=t.small[0:1, 11:13], in_=t.ar_out[1].ap()[0:1, 3:5]).then_inc(sem["ccd"], 16)
        e.wait_ge(sem["ccd"], ccd2)
        e.nop().then_inc(sem["c1"], 1)
    sch.on("gpsimd", g_ar)
    def v_s2(e, arv=arv):
        e.wait_ge(sem["c1"], arv)
        e.tensor_scalar(t.small[0:1, 0:2], t.small[0:1, 11:13], 1.0 / 32767.0, 1e-12, OP.mult, OP.max)
        e.drain()
        e.reciprocal(t.small[0:1, 2:4], t.small[0:1, 0:2]).then_inc(sem["c2"], 1)
    c2v = sch.inc("c2")
    sch.on("vector", v_s2)
    def t_bc2(e, c2v=c2v):
        e.wait_ge(sem["c2"], c2v)
        e.matmul(pB[:, 0:4], t.onesf[0:1, :], t.small[0:1, 0:4], start=True, stop=True).then_inc(sem["c1"], 1)
    c1v = sch.inc("c1")
    sch.on("tensor", t_bc2)
    def v_bc2(e, c1v=c1v):
        e.wait_ge(sem["c1"], c1v)
        e.tensor_copy(t.scl[:, 4:8], pB[:, 0:4]).then_inc(sem["c2"], 1)
    sch.inc("c2")
    sch.on("vector", v_bc2)


def _gen_finmax(nc, t, sch, maxcol0, dstcol):
    sem = t.sem
    pB = t.pB
    c6v = sch.val("c6")
    def v_f(e, c6v=c6v, maxcol0=maxcol0):
        e.wait_ge(sem["c6"], c6v)
        e.tensor_reduce(t.small[:, 14:15], t.mxcol[:, maxcol0 + 24:maxcol0 + 26],
                        AX.X, OP.max).then_inc(sem["n1"], 1)
    n1v = sch.inc("n1")
    sch.on("vector", v_f)
    n2p = sch.val("n2")
    def t_f(e, n1v=n1v, n2p=n2p):
        e.wait_ge(sem["n1"], n1v)
        e.wait_ge(sem["n2"], n2p)
        e.transpose(pB[0:1, 8:136], t.small[:, 14:15], t.ident[:, :]).then_inc(sem["n2"], 1)
    n2v = sch.inc("n2")
    sch.on("tensor", t_f)
    def v_f2(e, n2v=n2v, dstcol=dstcol):
        e.wait_ge(sem["n2"], n2v)
        e.tensor_reduce(t.small[0:1, dstcol:dstcol + 1], pB[0:1, 8:136], AX.X, OP.max).then_inc(sem["c6"], 1)
    sch.inc("c6")
    sch.on("vector", v_f2)

def gen_p3(nc, t, sch, midhook=None, prehook=None):
    sem = t.sem
    pA, pB = t.pA, t.pB
    for p_ in range(NPAIR):
        for sl in range(8):
            n2p = sch.val("n2")
            def v_ksq(e, p_=p_, sl=sl, n2p=n2p):
                e.wait_ge(sem["n2"], n2p)
                e.drain()
                kslc = t.kT[:, S * p_ + 512 * sl:S * p_ + 512 * (sl + 1)]
                e.tensor_tensor(t.sqs[:, 0:512], kslc, kslc, OP.mult).then_inc(sem["n1"], 1)
            n1v = sch.inc("n1")
            sch.on("vector", v_ksq)
            n2c = sch.val("n2")
            def t_kn(e, n1v=n1v, n2c=n2c):
                e.wait_ge(sem["n1"], n1v)
                e.wait_ge(sem["n2"], n2c)
                e.matmul(pB[0:1, 0:512], t.ones64[:, 0:1], t.sqs[:, 0:512], start=True, stop=True)
                e.matmul(pB[64:65, 0:512], t.ones64[:, 1:2], t.sqs[:, 0:512],
                         start=True, stop=True, tile_position=(0, 64)).then_inc(sem["n2"], 1)
            n2v = sch.inc("n2")
            sch.on("tensor", t_kn)
            def v_kmx(e, p_=p_, sl=sl, n2v=n2v):
                e.wait_ge(sem["n2"], n2v)
                e.drain()
                e.tensor_reduce(t.lnt[0:1, 0:1], pB[0:1, 0:512], AX.X, OP.max)
                e.tensor_reduce(t.lnt[64:65, 0:1], pB[64:65, 0:512], AX.X, OP.max)
                e.drain()
                if sl == 0:
                    e.tensor_copy(t.lnt[0:1, 4 + p_:5 + p_], t.lnt[0:1, 0:1])
                    e.tensor_copy(t.lnt[64:65, 4 + p_:5 + p_], t.lnt[64:65, 0:1])
                else:
                    e.tensor_tensor(t.lnt[0:1, 8:9], t.lnt[0:1, 0:1], t.lnt[0:1, 4 + p_:5 + p_], OP.max)
                    e.tensor_tensor(t.lnt[64:65, 8:9], t.lnt[64:65, 0:1], t.lnt[64:65, 4 + p_:5 + p_], OP.max)
                    e.drain()
                    e.tensor_copy(t.lnt[0:1, 4 + p_:5 + p_], t.lnt[0:1, 8:9])
                    e.tensor_copy(t.lnt[64:65, 4 + p_:5 + p_], t.lnt[64:65, 8:9])
                e.drain()
                e.nop().then_inc(sem["n2"], 1)
            sch.inc("n2")
            sch.on("vector", v_kmx)
    n2f = sch.val("n2")
    def a_ksqrt(e, n2f=n2f):
        e.wait_ge(sem["n2"], n2f)
        e.activation(t.lnt[0:1, 12:16], t.lnt[0:1, 4:8], ACT.Sqrt)
        e.activation(t.lnt[64:65, 12:16], t.lnt[64:65, 4:8], ACT.Sqrt).then_inc(sem["a2"], 1)
    a2k = sch.inc("a2")
    sch.on("scalar", a_ksqrt)
    def v_kneg(e, a2k=a2k):
        e.wait_ge(sem["a2"], a2k)
        e.tensor_scalar(t.lnt[0:1, 12:16], t.lnt[0:1, 12:16], -1.0, None, OP.mult)
        e.tensor_scalar(t.lnt[64:65, 12:16], t.lnt[64:65, 12:16], -1.0, None, OP.mult)
        e.drain()
        e.nop().then_inc(sem["n2"], 1)
    sch.inc("n2")
    sch.on("vector", v_kneg)
    prol_n2 = sch.val("n2")

    pend = {}

    def emit_qsq_qn(idx, a2cps, n2sqs):
        p_, qt = idx // NQT, idx % NQT
        def v_qsq(e, p_=p_, qt=qt, n2sqs=n2sqs, prol_n2=prol_n2):
            e.wait_ge(sem["n2"], max(prol_n2, n2sqs))
            e.drain()
            qslc = t.qT[:, S * p_ + 512 * qt:S * p_ + 512 * (qt + 1)]
            e.tensor_tensor(t.sqs[:, 0:512], qslc, qslc, OP.mult).then_inc(sem["n1"], 1)
        n1q = sch.inc("n1")
        sch.on("vector", v_qsq)
        def t_qn(e, n1q=n1q, a2cps=a2cps):
            e.wait_ge(sem["n1"], n1q)
            e.wait_ge(sem["a2"], a2cps)
            e.matmul(pB[0:1, 1024:1536], t.ones64[:, 0:1], t.sqs[:, 0:512],
                     start=True, stop=True)
            e.matmul(pB[64:65, 1024:1536], t.ones64[:, 1:2], t.sqs[:, 0:512],
                     start=True, stop=True, tile_position=(0, 64)).then_inc(sem["n2"], 1)
        n2qv = sch.inc("n2")
        sch.on("tensor", t_qn)
        pend[idx] = dict(n2sqs=n2qv)

    def emit_ncb_shift(idx, a2cps, shiftprev):
        p_ = idx // NQT
        n2qv = pend[idx]["n2sqs"]
        def a_qn(e, n2qv=n2qv, shiftprev=shiftprev):
            e.wait_ge(sem["n2"], max(n2qv, shiftprev))
            e.activation(t.rnorm[0:1, :], pB[0:1, 1024:1536], ACT.Sqrt)
            e.activation(t.rnorm[64:65, :], pB[64:65, 1024:1536], ACT.Sqrt).then_inc(sem["a2"], 1)
        a2qv = sch.inc("a2")
        sch.on("scalar", a_qn)
        def v_ncb(e, p_=p_, a2qv=a2qv):
            e.wait_ge(sem["a2"], a2qv)
            e.drain()
            e.tensor_scalar(t.rnorm[0:1, :], t.rnorm[0:1, :],
                            t.lnt[0:1, 12 + p_:13 + p_], 200.0, OP.mult, OP.add)
            e.tensor_scalar(t.rnorm[64:65, :], t.rnorm[64:65, :],
                            t.lnt[64:65, 12 + p_:13 + p_], 200.0, OP.mult, OP.add)
            e.drain()
            e.nop().then_inc(sem["n2"], 1)
        ncbv = sch.inc("n2")
        sch.on("vector", v_ncb)
        def t_shift(e, ncbv=ncbv, a2cps=a2cps):
            e.wait_ge(sem["n2"], ncbv)
            e.wait_ge(sem["a2"], a2cps)
            e.matmul(pB[:, 1024:1536], t.onesf[0:1, 0:128], t.rnorm[0:1, :],
                     start=True, stop=True, tile_position=(0, 0))
            e.matmul(pB[:, 1536:2048], t.onesf[64:65, 0:128], t.rnorm[64:65, :],
                     start=True, stop=True, tile_position=(64, 0)).then_inc(sem["n2"], 1)
        shiftv = sch.inc("n2")
        sch.on("tensor", t_shift)
        pend[idx]["shiftv"] = shiftv

    def emit_acps(idx):
        shiftv = pend[idx]["shiftv"]
        a5l = sch.val("a5")
        def a_cps(e, shiftv=shiftv, a5l=a5l):
            e.wait_ge(sem["n2"], shiftv)
            e.wait_ge(sem["a5"], a5l)
            e.activation(t.shift_sb[:, :], pB[:, 1024:2048], ACT.Identity).then_inc(sem["a2"], 1)
        pend[idx]["cpsv"] = sch.inc("a2")
        sch.on("scalar", a_cps)

    emit_qsq_qn(0, sch.val("a2"), 0)
    emit_ncb_shift(0, 0, 0)
    emit_acps(0)

    for p_ in range(NPAIR):
        for qt in range(NQT):
            KC = 4 * (qt + 1)
            idx3 = 8 * p_ + qt
            extra_sc = prehook(idx3) if prehook else []
            cpsv = pend[idx3]["cpsv"]
            n2sqs = pend[idx3]["n2sqs"]
            shiftv = pend[idx3]["shiftv"]

            a4st = sch.val("a4")
            a5st = sch.val("a5")
            a2ev = sch.val("a2")
            cxv = {}
            prev_cx = None

            def emit_cx(kc, wv, wsem, p_=p_, KC=KC, a2ev=a2ev):
                def t_cx(e, p_=p_, kc=kc, wv=wv, wsem=wsem, KC=KC, a2ev=a2ev):
                    e.wait_ge(sem[wsem], wv)
                    if kc == 0:
                        e.wait_ge(sem["a2"], a2ev)
                    first, last = (kc == 0), (kc == KC - 1)
                    src = t.pfin2[:, 1024 * (kc % 2):1024 * (kc % 2) + 1024]
                    e.matmul(pA[0:65, 1024:1536],
                             t.vA[:, 520 * kc + 130 * p_:520 * kc + 130 * p_ + 65],
                             src[:, 0:512], start=first, stop=last)
                    e.matmul(pA[0:65, 1536:2048],
                             t.vA[:, 520 * kc + 130 * p_ + 65:520 * kc + 130 * (p_ + 1)],
                             src[:, 512:1024], start=first, stop=last).then_inc(sem["a1"], 1)
                cxv[kc] = sch.inc("a1")
                sch.on("tensor", t_cx)

            for kc in range(KC):
                ps_sc = pA if kc % 2 == 0 else pB
                def t_sc(e, p_=p_, qt=qt, kc=kc, ps_sc=ps_sc,
                         bw=(sch.val("a5") - 1 if kc >= 2 else a5st),
                         ex=(extra_sc if kc == 1 else [])):
                    for smn, vv in ex:
                        e.wait_ge(sem[smn], vv)
                    e.wait_ge(sem["a5"], bw)
                    e.matmul(ps_sc[:, 0:512],
                             t.kT[0:64, S * p_ + 128 * kc:S * p_ + 128 * (kc + 1)],
                             t.qT[0:64, S * p_ + 512 * qt:S * p_ + 512 * (qt + 1)],
                             start=True, stop=True, tile_position=(0, 0))
                    e.matmul(ps_sc[:, 512:1024],
                             t.kT[64:128, S * p_ + 128 * kc:S * p_ + 128 * (kc + 1)],
                             t.qT[64:128, S * p_ + 512 * qt:S * p_ + 512 * (qt + 1)],
                             start=True, stop=True, tile_position=(64, 0)).then_inc(sem["a1"], 1)
                a1v = sch.inc("a1")
                sch.on("tensor", t_sc)
                if prev_cx is not None:
                    emit_cx(*prev_cx)
                a2p = sch.val("a2")
                prw = t.praw32 if kc % 2 == 0 else t.praw32b
                def v_add(e, kc=kc, a1v=a1v, a2p=a2p, ps_sc=ps_sc, prw=prw):
                    e.wait_ge(sem["a1"], a1v)
                    e.wait_ge(sem["a2"], a2p - 1 if kc >= 2 else a2p)
                    e.scalar_tensor_tensor(prw[:, :], ps_sc[:, 0:1024], 1.0,
                                           t.shift_sb[:, :], OP.mult,
                                           OP.add).then_inc(sem["a5"], 1)
                a5v = sch.inc("a5")
                sch.on("vector", v_add)
                pfw = cxv.get(kc - 2, None)
                def a_exp(e, kc=kc, a5v=a5v, pfw=pfw, a4st=a4st, prw=prw):
                    e.wait_ge(sem["a5"], a5v)
                    if pfw is not None:
                        e.wait_ge(sem["a1"], pfw)
                    else:
                        e.wait_ge(sem["a4"], a4st)
                    e.activation(t.pfin2[:, 1024 * (kc % 2):1024 * (kc % 2) + 1024],
                                 prw[:, :], ACT.Exp, scale=0.125).then_inc(sem["a2"], 1)
                a2v = sch.inc("a2")
                sch.on("scalar", a_exp)
                dg = kc - 4 * qt
                if dg >= 0:
                    def v_msk(e, qt=qt, kc=kc, dg=dg, a2v=a2v):
                        e.wait_ge(sem["a2"], a2v)
                        e.drain()
                        mv = t.masks[:, 512 * dg:512 * (dg + 1)]
                        dst = t.pfin2[:, 1024 * (kc % 2):1024 * (kc % 2) + 1024]
                        e.tensor_tensor(dst[:, 0:512], dst[:, 0:512], mv, OP.mult)
                        e.tensor_tensor(dst[:, 512:1024], dst[:, 512:1024], mv,
                                        OP.mult).then_inc(sem["a3"], 1)
                    a3v = sch.inc("a3")
                    sch.on("vector", v_msk)
                    prev_cx = (kc, a3v, "a3")
                else:
                    prev_cx = (kc, a2v, "a2")
                if kc == 1 and idx3 + 1 < 32:
                    emit_qsq_qn(idx3 + 1, cpsv, n2sqs)
                if kc == 2:
                    if idx3 + 1 < 32:
                        emit_ncb_shift(idx3 + 1, cpsv, shiftv)
                    if midhook:
                        midhook(idx3, 0)
                if midhook and kc == min(KC - 1, 18):
                    midhook(idx3, 1)
            emit_cx(*prev_cx)
            if idx3 + 1 < 32:
                emit_acps(idx3 + 1)
            a1e = sch.val("a1")
            a4p = sch.val("a4")
            def a_st(e, a1e=a1e, a4p=a4p):
                e.wait_ge(sem["a1"], a1e)
                e.wait_ge(sem["a4"], a4p)
                e.copy(t.ctxstage[0:65, :], pA[0:65, 1024:2048]).then_inc(sem["a2"], 1)
            astv = sch.inc("a2")
            sch.on("scalar", a_st)
            def v_rc(e, astv=astv):
                e.wait_ge(sem["a2"], astv)
                e.drain()
                e.reciprocal(t.rnorm_e[64:65, 0:1024], t.ctxstage[64:65, :]).then_inc(sem["a4"], 1)
            vrcv = sch.inc("a4")
            sch.on("vector", v_rc)
            def t_rb(e, vrcv=vrcv):
                e.wait_ge(sem["a4"], vrcv)
                e.matmul(pA[0:64, 1024:1536], t.onesf[64:65, 0:64], t.rnorm_e[64:65, 0:512],
                         start=True, stop=True, tile_position=(64, 0))
                e.matmul(pA[0:64, 1536:2048], t.onesf[64:65, 0:64], t.rnorm_e[64:65, 512:1024],
                         start=True, stop=True, tile_position=(64, 0)).then_inc(sem["a1"], 1)
            a1r = sch.inc("a1")
            sch.on("tensor", t_rb)
            def a_ev(e, a1r=a1r, a4p=a4p):
                e.wait_ge(sem["a1"], a1r)
                e.wait_ge(sem["a4"], a4p)
                e.copy(t.rnorm_e[0:64, 0:1024], pA[0:64, 1024:2048]).then_inc(sem["a2"], 1)
            a2r = sch.inc("a2")
            sch.on("scalar", a_ev)
            def v_nm0(e, p_=p_, qt=qt, a2r=a2r):
                e.wait_ge(sem["a2"], a2r)
                e.drain()
                e.tensor_tensor(t.qT[0:64, S * p_ + 512 * qt:S * p_ + 512 * (qt + 1)],
                                t.ctxstage[0:64, 0:512], t.rnorm_e[0:64, 0:512],
                                OP.mult).then_inc(sem["a4"], 1)
            sch.inc("a4")
            sch.on("vector", v_nm0)
            def v_nm1(e, p_=p_, qt=qt, a2r=a2r):
                e.wait_ge(sem["a2"], a2r)
                e.tensor_tensor(t.pfin2[0:64, 0:512], t.ctxstage[0:64, 512:1024],
                                t.rnorm_e[0:64, 512:1024], OP.mult)
                e.drain()
                e.stream_shuffle(t.qT[64:128, S * p_ + 512 * qt:S * p_ + 512 * (qt + 1)],
                                 t.pfin2[0:64, 0:512], list(range(32))).then_inc(sem["a4"], 1)
            sch.inc("a4")
            sch.on("vector", v_nm1)

def _p4_ag_start(nc, t, sch):
    sem = t.sem
    ccd0 = sch.val("ccd")
    def g_ag1(e, ccd0=ccd0):
        for p_ in range(4):
            e.dma_start(out=t.ag_in.ap()[p_, :, :], in_=t.qT[:, S * p_:S * (p_ + 1)]).then_inc(sem["ccd"], 16)
        e.wait_ge(sem["ccd"], ccd0 + 64)
        e.collective_compute("AllGather", OP.bypass, replica_groups=[list(range(8))],
                             ins=[t.ag_in.ap()], outs=[t.ag_out.ap()]).then_inc(sem["ccs"], 1)
    sch.inc("ccd", 64)
    ccs_ag = sch.inc("ccs", 1)
    sch.on("gpsimd", g_ag1)
    return ccd0 + 64, ccs_ag


def _p4_extract(nc, t, sch, ccs_ag):
    sem = t.sem
    din = t.din
    c7g = sch.val("c7")
    stwg = sch.val("stw")
    ccdx = sch.val("ccd")
    def g_ag2(e, ccs_ag=ccs_ag, c7g=c7g, stwg=stwg, ccdx=ccdx):
        e.wait_ge(sem["ccs"], ccs_ag)
        e.wait_ge(sem["c7"], c7g)
        e.wait_ge(sem["stw"], stwg)
        my = e.value_load(t.offs[0:1, 1:2])
        pr = e.value_load(t.offs[0:1, 2:3])
        gv = e.value_load(t.offs[0:1, 3:4])
        for p_ in range(4):
            e.dma_start(out=t.ctx_own[:, TOWN * p_:TOWN * (p_ + 1)],
                        in_=t.ag_out.ap()[bass.ts(my, 1), p_, :, bass.ts(gv, TOWN)]).then_inc(sem["ccd"], 16)
        for p_ in range(4):
            e.dma_start(out=t.ctx_own[:, TOWN * (4 + p_):TOWN * (5 + p_)],
                        in_=t.ag_out.ap()[bass.ts(pr, 1), p_, :, bass.ts(gv, TOWN)]).then_inc(sem["ccd"], 16)
        e.dma_start(out=t.wprojT[:, :].rearrange("p (c f) -> p c f", c=8),
                    in_=din["wprojT"].rearrange("(c p) f -> p c f", p=128)).then_inc(sem["ldw"], 16)
        e.wait_ge(sem["ccd"], ccdx + 128)
        e.nop().then_inc(sem["c4"], 1)
    sch.inc("ccd", 128)
    sch.inc("ldw", 16)
    agv = sch.inc("c4")
    sch.on("gpsimd", g_ag2)
    return agv


def _p4_proj(nc, t, sch, agv):
    sem = t.sem
    pA, pB = t.pA, t.pB
    ldwv = sch.val("ldw")
    for tt in range(16):
        par = tt % 2
        evp = sch.val("c5")
        def t_pj(e, tt=tt, par=par, evp=evp, agv=agv, ldwv=ldwv):
            e.wait_ge(sem["c4"], agv)
            e.wait_ge(sem["ldw"], ldwv)
            e.wait_ge(sem["c5"], evp)
            for fc in range(8):
                lhs = t.ctx_own[:, TOWN * fc + 128 * tt:TOWN * fc + 128 * (tt + 1)]
                e.ldweights(lhs)
                for hf in range(2):
                    mm = e.matmul(pA[:, 1024 * par + 512 * hf:1024 * par + 512 * (hf + 1)], lhs,
                                  t.wprojT[:, 1024 * fc + 512 * hf:1024 * fc + 512 * (hf + 1)],
                                  start=(fc == 0), stop=(fc == 7))
            mm.then_inc(sem["c8"], 1)
        c8v = sch.inc("c8")
        sch.on("tensor", t_pj)
        def v_pj(e, tt=tt, par=par, c8v=c8v):
            e.wait_ge(sem["c8"], c8v)
            dst = (t.aout_a if tt < 8 else t.aout_b)[:, D * (tt % 8):D * (tt % 8 + 1)]
            e.scalar_tensor_tensor(dst, pA[:, 1024 * par:1024 * (par + 1)], 1.0,
                                   t.btile[:, :], OP.mult, OP.add).then_inc(sem["c5"], 1)
        sch.inc("c5")
        sch.on("vector", v_pj)

def _p5_finmax(nc, t, sch, dstcol, a2g=None):
    sem = t.sem
    pB = t.pB
    c6v = sch.val("c6")
    def v_f(e, c6v=c6v):
        e.wait_ge(sem["c6"], c6v)
        e.drain()
        e.tensor_reduce(t.small[:, 14:15], t.mxcol[:, 0:24], AX.X, OP.max).then_inc(sem["n1"], 1)
    n1v = sch.inc("n1")
    sch.on("vector", v_f)
    n2p = sch.val("n2")
    def t_f(e, n1v=n1v, n2p=n2p, a2g=a2g):
        e.wait_ge(sem["n1"], n1v)
        e.wait_ge(sem["n2"], n2p)
        if a2g is not None:
            e.wait_ge(sem["a2"], a2g)
        e.transpose(pB[0:1, 8:136], t.small[:, 14:15], t.ident[:, :]).then_inc(sem["n2"], 1)
    n2v = sch.inc("n2")
    sch.on("tensor", t_f)
    def v_f2(e, n2v=n2v, dstcol=dstcol):
        e.wait_ge(sem["n2"], n2v)
        e.tensor_reduce(t.small[0:1, dstcol:dstcol + 1], pB[0:1, 8:136], AX.X, OP.max).then_inc(sem["c6"], 1)
    sch.inc("c6")
    sch.on("vector", v_f2)


def _p5_ar(nc, t, sch, idx, srccols, scale_fn, bccols):
    sem = t.sem
    pB = t.pB
    ccd1 = sch.inc("ccd", 16)
    ccs1 = sch.inc("ccs", 1)
    ccd2 = sch.inc("ccd", 16)
    c6f = sch.val("c6")
    arv = sch.inc("c1")
    def g_ar(e, idx=idx, ccd1=ccd1, ccs1=ccs1, ccd2=ccd2, c6f=c6f):
        e.wait_ge(sem["c6"], c6f)
        e.dma_start(out=t.ar_in[idx].ap()[0:1, 0:8], in_=t.small[0:1, 8:16]).then_inc(sem["ccd"], 16)
        e.wait_ge(sem["ccd"], ccd1)
        e.collective_compute("AllReduce", OP.max, replica_groups=[list(range(8))],
                             ins=[t.ar_in[idx].ap()], outs=[t.ar_out[idx].ap()]).then_inc(sem["ccs"], 1)
        e.wait_ge(sem["ccs"], ccs1)
        e.dma_start(out=t.small[0:1, 8:16], in_=t.ar_out[idx].ap()[0:1, 0:8]).then_inc(sem["ccd"], 16)
        e.wait_ge(sem["ccd"], ccd2)
        e.nop().then_inc(sem["c1"], 1)
    sch.on("gpsimd", g_ar)
    def v_s(e, arv=arv, scale_fn=scale_fn):
        e.wait_ge(sem["c1"], arv)
        e.drain()
        scale_fn(e)
    c2v = sch.inc("c2")
    sch.on("vector", v_s)
    n, d0 = bccols
    def t_bc(e, c2v=c2v, n=n):
        e.wait_ge(sem["c2"], c2v)
        e.matmul(pB[:, 0:n], t.onesf[0:1, :], t.small[0:1, 0:n], start=True, stop=True).then_inc(sem["c1"], 1)
    c1v = sch.inc("c1")
    sch.on("tensor", t_bc)
    def v_bc(e, c1v=c1v, n=n, d0=d0):
        e.wait_ge(sem["c1"], c1v)
        e.tensor_copy(t.scl[:, d0:d0 + n], pB[:, 0:n]).then_inc(sem["c2"], 1)
    sch.inc("c2")
    sch.on("vector", v_bc)


def _p5_sc3(t):
    CM = CMAGIC
    def sc3(e):
        e.tensor_scalar(t.small[0:1, 0:1], t.small[0:1, 13:14], float(2.0 ** -16 / 32767.0),
                        1e-12, OP.mult, OP.max)
        e.drain()
        e.reciprocal(t.small[0:1, 2:3], t.small[0:1, 0:1])
        e.drain()
        e.tensor_scalar(t.small[0:1, 1:2], t.small[0:1, 2:3], float(2.0 ** -16), None,
                        OP.mult).then_inc(t.sem["c2"], 1)
    return sc3


def _p5_sc4(t):
    def sc4(e):
        e.tensor_scalar(t.small[0:1, 0:1], t.small[0:1, 13:14], 1.0 / 32767.0, 1e-12,
                        OP.mult, OP.max)
        e.drain()
        e.reciprocal(t.small[0:1, 1:2], t.small[0:1, 0:1]).then_inc(t.sem["c2"], 1)
    return sc4


def _p5_sclderiv(nc, t, sch):
    sem = t.sem
    def v_c(e):
        e.drain()
        e.tensor_scalar(t.scl[:, 13:14], t.scl[:, 5:6], -CMAGIC, None, OP.mult)
        e.drain()
        e.tensor_scalar(t.scl[:, 14:15], t.scl[:, 4:5], -CMAGIC, None, OP.mult)
        e.drain()
        e.nop().then_inc(sem["c2"], 1)
    v = sch.inc("c2")
    sch.on("vector", v_c)
    return v


def _p5a_chunk(nc, t, sch, ic, c2c):
    sem = t.sem
    s5 = t.s5
    lsem = ("str", "trs")[ic % 2]
    c7p = sch.val("c7")
    stv = sch.val("stw")
    def s_ld(e, ic=ic, c7p=c7p, stv=stv, lsem=lsem):
        e.wait_ge(sem["c7"], max(0, c7p - 1))
        if ic >= 2:
            e.wait_ge(sem["stw"], stv - 16)
        e.dma_start(out=s5[4 + ic % 2][:, :], in_=t.stash_gate.ap()[ic, :, :]).then_inc(sem[lsem], 16)
    ldv = sch.inc(lsem, 16)
    sch.on("sync", s_ld)
    c7w = sch.val("c7")
    c9w = sch.val("c9")
    def a_blk(e, ic=ic, ldv=ldv, lsem=lsem, c7w=c7w, c9w=c9w, c2c=c2c):
        e.wait_ge(sem[lsem], ldv)
        e.wait_ge(sem["c7"], c7w)
        e.wait_ge(sem["c9"], c9w)
        e.wait_ge(sem["c2"], c2c)
        e.activation(s5[6][:, :], s5[4 + ic % 2][:, :], ACT.Identity,
                     bias=t.scl[:, 12:13], scale=t.scl[:, 7:8])
        e.drain()
        e.activation(s5[7][:, :], s5[6][:, :], ACT.Identity,
                     bias=t.scl[:, 13:14], scale=t.scl[:, 5:6])
        e.drain()
        e.activation(t.slotA[:, :], s5[7][:, :], ACT.Sigmoid)
        e.drain()
        e.activation(t.slotB[:, :], t.slotA[:, :], ACT.Identity,
                     bias=t.scl[:, 12:13], scale=65536.0).then_inc(sem["c1"], 1)
    c1v = sch.inc("c1")
    sch.on("scalar", a_blk)
    def v_blk(e, ic=ic, c1v=c1v):
        e.wait_ge(sem["c1"], c1v)
        e.drain()
        e.scalar_tensor_tensor(t.slotA[:, :], t.slotB[:, :], CMAGIC + 65535.0,
                               t.scl[:, 12:13].broadcast_to((128, 2048)),
                               OP.min, OP.subtract).then_inc(sem["c9"], 1)
        e.drain()
        e.tensor_tensor(s5[4 + ic % 2][:, :], s5[7][:, :], t.slotA[:, :],
                        OP.mult).then_inc(sem["c7"], 1)
    sch.inc("c9")
    c7v = sch.inc("c7")
    sch.on("vector", v_blk)
    def v_2r(e, ic=ic, c7v=c7v):
        e.wait_ge(sem["c7"], c7v)
        e.drain()
        e.tensor_reduce(t.mxcol[:, ic:ic + 1], s5[4 + ic % 2][:, :], AX.X, OP.max,
                        apply_absolute_value=True).then_inc(sem["c6"], 1)
    c6v = sch.inc("c6")
    sch.on("vector", v_2r)
    stq = sch.val("stw")
    def g_st(e, ic=ic, c6v=c6v, stq=stq):
        e.wait_ge(sem["c6"], c6v)
        e.wait_ge(sem["stw"], stq)
        e.dma_start(out=t.stash_g2.ap()[ic, :, :], in_=s5[4 + ic % 2][:, :]).then_inc(sem["stw"], 16)
    sch.inc("stw", 16)
    sch.on("gpsimd", g_st)


def _p5b_chunk(nc, t, sch, ic, c2c):
    sem = t.sem
    s5 = t.s5
    upb = (t.slotA, t.slotB)[ic % 2]
    lsem = ("str", "trs")[ic % 2]
    c7p = sch.val("c7")
    stv = sch.val("stw")
    lag = 1 if ic >= 2 else 0
    def s_ld(e, ic=ic, c7p=c7p, stv=stv, lsem=lsem, lag=lag, upb=upb):
        e.wait_ge(sem["c7"], max(0, c7p - lag))
        e.wait_ge(sem["stw"], max(0, stv - 16))
        e.dma_start(out=s5[4 + ic % 2][:, :], in_=t.stash_g2.ap()[ic, :, :]).then_inc(sem[lsem], 16)
        e.dma_start(out=upb[:, :], in_=t.stash_up.ap()[ic, :, :]).then_inc(sem[lsem], 16)
    ldv = sch.inc(lsem, 32)
    sch.on("sync", s_ld)
    c7w = sch.val("c7")
    def a_blk(e, ic=ic, ldv=ldv, lsem=lsem, c7w=c7w, c2c=c2c, upb=upb):
        e.wait_ge(sem[lsem], ldv)
        e.wait_ge(sem["c7"], c7w)
        e.wait_ge(sem["c2"], c2c)
        e.activation(s5[6][:, :], s5[4 + ic % 2][:, :], ACT.Identity,
                     bias=t.scl[:, 12:13], scale=t.scl[:, 9:10])
        e.drain()
        e.activation(s5[7][:, :], s5[6][:, :], ACT.Identity,
                     bias=t.scl[:, 15:16], scale=t.scl[:, 8:9])
        e.drain()
        e.activation(s5[6][:, :], upb[:, :], ACT.Identity,
                     bias=t.scl[:, 12:13], scale=t.scl[:, 6:7])
        e.drain()
        e.activation(upb[:, :], s5[6][:, :], ACT.Identity,
                     bias=t.scl[:, 14:15], scale=t.scl[:, 4:5]).then_inc(sem["c1"], 1)
    c1v = sch.inc("c1")
    sch.on("scalar", a_blk)
    def v_blk(e, ic=ic, c1v=c1v, upb=upb):
        e.wait_ge(sem["c1"], c1v)
        e.drain()
        e.tensor_tensor(s5[4 + ic % 2][:, :], s5[7][:, :], upb[:, :],
                        OP.mult).then_inc(sem["c7"], 1)
    c7v = sch.inc("c7")
    sch.on("vector", v_blk)
    def v_2r(e, ic=ic, c7v=c7v):
        e.wait_ge(sem["c7"], c7v)
        e.drain()
        e.tensor_reduce(t.mxcol[:, ic:ic + 1], s5[4 + ic % 2][:, :], AX.X, OP.max,
                        apply_absolute_value=True).then_inc(sem["c6"], 1)
    c6v = sch.inc("c6")
    sch.on("vector", v_2r)
    stq = sch.val("stw")
    def g_st(e, ic=ic, c6v=c6v, stq=stq):
        e.wait_ge(sem["c6"], c6v)
        e.wait_ge(sem["stw"], stq)
        e.dma_start(out=t.stash_o.ap()[ic, :, :], in_=s5[4 + ic % 2][:, :]).then_inc(sem["stw"], 16)
    sch.inc("stw", 16)
    sch.on("gpsimd", g_st)


def _p5_arwin(nc, t, sch, idx, scale_fn, bccols, post_fn=None):
    sem = t.sem
    a2g = sch.val("a2")
    _p5_finmax(nc, t, sch, 13, a2g)
    _p5_ar(nc, t, sch, idx, None, scale_fn, bccols)
    if post_fn is not None:
        sch.on("vector", post_fn)
    return [("c2", sch.val("c2"))]


def _p5_dequant(nc, t, sch, agccd, c8proj):
    sem = t.sem
    din = t.din
    pA, pB = t.pA, t.pB
    s5 = t.s5
    ldp = sch.val("ldx")
    stwp = sch.val("stw")
    c7b = sch.val("c7")
    def s_ind(e, ldp=ldp, stwp=stwp, c7b=c7b, agccd=agccd):
        e.wait_ge(sem["c7"], c7b)
        e.wait_ge(sem["stw"], stwp)
        e.wait_ge(sem["ccd"], agccd)
        e.dma_start(out=t.ind96[:, :], in_=din["ind96"]).then_inc(sem["ldx"], 16)
    ldxv = sch.inc("ldx", 16)
    sch.on("sync", s_ind)
    n2p0 = sch.val("n2")
    def v_rtd(e, n2p0=n2p0):
        e.drain()
        e.reciprocal(t.rTd[:, :], t.sTd[:, :]).then_inc(sem["c6"], 1)
    c6rtd = sch.inc("c6")
    sch.on("vector", v_rtd)
    for ic in range(NIC):
        c7p = sch.val("c7")
        def s_ld(e, ic=ic, c7p=c7p, c8proj=c8proj):
            e.wait_ge(sem["c7"], max(0, c7p - 1))
            if ic < 2:
                e.wait_ge(sem["c8"], c8proj)
            e.dma_start(out=s5[4 + ic % 2][:, 0:1024], in_=din["wdownT"][128 * ic:128 * (ic + 1), :]
                        ).then_inc(sem[("str", "trs")[ic % 2]], 16)
        ldv = sch.inc(("str", "trs")[ic % 2], 16)
        sch.on("sync", s_ld)
        n2p = sch.val("n2")
        def t_bc(e, ic=ic, n2p=n2p, ldxv=ldxv, c6rtd=c6rtd):
            e.wait_ge(sem["n2"], n2p)
            e.wait_ge(sem["ldx"], ldxv)
            e.wait_ge(sem["c6"], c6rtd)
            e.matmul(pB[:, 0:512], t.ind96[:, 128 * ic:128 * (ic + 1)], t.rTd[:, 0:512],
                     start=True, stop=True)
            e.matmul(pB[:, 512:1024], t.ind96[:, 128 * ic:128 * (ic + 1)], t.rTd[:, 512:1024],
                     start=True, stop=True)
            e.matmul(pB[:, 1024:1536], t.ind96[:, 128 * ic:128 * (ic + 1)], t.sTd[:, 0:512],
                     start=True, stop=True)
            e.matmul(pB[:, 1536:2048], t.ind96[:, 128 * ic:128 * (ic + 1)], t.sTd[:, 512:1024],
                     start=True, stop=True).then_inc(sem["n1"], 1)
        n1v = sch.inc("n1")
        sch.on("tensor", t_bc)
        stp = sch.val("stw")
        def v_dq(e, ic=ic, ldv=ldv, n1v=n1v, stp=stp):
            e.wait_ge(sem[("str", "trs")[ic % 2]], ldv)
            e.wait_ge(sem["n1"], n1v)
            e.wait_ge(sem["stw"], stp)
            e.drain()
            e.tensor_tensor(s5[6][:, 0:1024], s5[4 + ic % 2][:, 0:1024], pB[:, 0:1024], OP.mult)
            e.drain()
            e.tensor_scalar(s5[6][:, 0:1024], s5[6][:, 0:1024], CMAGIC, CMAGIC, OP.add, OP.subtract)
            e.drain()
            e.tensor_tensor(t.wdq_s[:, 1024 * (ic % 2):1024 * (ic % 2 + 1)], s5[6][:, 0:1024],
                            pB[:, 1024:2048], OP.mult).then_inc(sem["n2"], 1)
            e.drain()
            e.nop().then_inc(sem["c7"], 1)
        sch.inc("n2")
        sch.inc("c7")
        sch.on("vector", v_dq)
        def g_st(e, ic=ic):
            pass
        n2v2 = sch.val("n2")
        def g_st2(e, ic=ic, n2v2=n2v2):
            e.wait_ge(sem["n2"], n2v2)
            e.dma_start(out=t.stash_wdq.ap()[ic, :, :],
                        in_=t.wdq_s[:, 1024 * (ic % 2):1024 * (ic % 2 + 1)]).then_inc(sem["stw"], 16)
        sch.inc("stw", 16)
        sch.on("gpsimd", g_st2)


def _p5_main(nc, t, sch, agccd, c7p5b):
    sem = t.sem
    din = t.din
    pA, pB = t.pA, t.pB
    s5 = t.s5
    stwdq = sch.val("stw")
    oqv_map = {}
    aoq_map = {}
    dn_map = {}

    def emit_oq(tqv, ic):
        lsem = ("lda", "ldb")[ic % 2]
        a1p = aoq_map.get((tqv, ic - 2) if ic >= 2 else (tqv - 1, ic + 22), 0)
        def s_lo(e, tqv=tqv, ic=ic, lsem=lsem, a1p=a1p):
            e.wait_ge(sem["c1"], a1p)
            if tqv == 0 and ic < 2:
                e.wait_ge(sem["c7"], c7p5b)
            e.dma_start(out=t.slotA[:, 512 * (ic % 2):512 * (ic % 2 + 1)],
                        in_=t.stash_o.ap()[ic, :, 512 * tqv:512 * (tqv + 1)]
                        ).then_inc(sem[lsem], 16)
        ldv = sch.inc(lsem, 16)
        sch.on("sync", s_lo)
        c2q = oqv_map.get((tqv, ic - 2) if ic >= 2 else (tqv - 1, ic + 22), 0)
        def a_oq(e, tqv=tqv, ic=ic, ldv=ldv, lsem=lsem, c2q=c2q):
            e.wait_ge(sem[lsem], ldv)
            e.wait_ge(sem["c2"], c2q)
            e.activation(t.slotB[:, 512 * (ic % 2):512 * (ic % 2 + 1)],
                         t.slotA[:, 512 * (ic % 2):512 * (ic % 2 + 1)], ACT.Identity,
                         bias=t.scl[:, 12:13], scale=t.scl[:, 11:12]).then_inc(sem["c1"], 1)
        aoq_map[(tqv, ic)] = sch.inc("c1")
        sch.on("scalar", a_oq)
        dnw = dn_map.get((tqv - 1, ic), 0)
        def v_oq(e, tqv=tqv, ic=ic, dnw=dnw, c1v=aoq_map[(tqv, ic)], agccd=agccd):
            e.wait_ge(sem["c1"], c1v)
            e.wait_ge(sem["c8"], dnw)
            if tqv == 0 and ic == 0:
                e.wait_ge(sem["ccd"], agccd)
            e.drain()
            e.tensor_scalar(t.oqbuf[:, 512 * ic:512 * (ic + 1)],
                            t.slotB[:, 512 * (ic % 2):512 * (ic % 2 + 1)], CMAGIC,
                            t.scl[:, 10:11], OP.subtract, OP.mult).then_inc(sem["c2"], 1)
        oqv_map[(tqv, ic)] = sch.inc("c2")
        sch.on("vector", v_oq)

    for ic in range(NIC):
        emit_oq(0, ic)

    for tqv in range(4):
        evp = sch.val("c5")
        c5q = sch.val("c5")
        for ic in range(NIC):
            c8p = sch.val("c8")
            def s_ldw(e, ic=ic, tqv=tqv, c8p=c8p, stwdq=stwdq, c5q=c5q):
                e.wait_ge(sem["c8"], max(0, c8p - 1))
                if ic < 2:
                    if tqv == 0:
                        e.wait_ge(sem["stw"], stwdq)
                    else:
                        e.wait_ge(sem["c5"], c5q)
                e.dma_start(out=t.wdq_s[:, 1024 * (ic % 2):1024 * (ic % 2 + 1)],
                            in_=t.stash_wdq.ap()[ic, :, :]).then_inc(sem[("str", "trs")[ic % 2]], 16)
            ldv = sch.inc(("str", "trs")[ic % 2], 16)
            sch.on("sync", s_ldw)
            def t_dn(e, ic=ic, ldv=ldv, oqw=oqv_map[(tqv, ic)], evp=evp):
                e.wait_ge(sem[("str", "trs")[ic % 2]], ldv)
                e.wait_ge(sem["c2"], oqw)
                if ic == 0:
                    e.wait_ge(sem["c5"], evp)
                mm = None
                for ts_ in range(4):
                    lhs = t.oqbuf[:, 512 * ic + 128 * ts_:512 * ic + 128 * (ts_ + 1)]
                    e.ldweights(lhs)
                    ps = (pA if ts_ < 2 else pB)
                    for hf in range(2):
                        mm = e.matmul(ps[:, 1024 * (ts_ % 2) + 512 * hf:1024 * (ts_ % 2) + 512 * (hf + 1)],
                                      lhs, t.wdq_s[:, 1024 * (ic % 2) + 512 * hf:1024 * (ic % 2) + 512 * (hf + 1)],
                                      start=(ic == 0), stop=(ic == NIC - 1))
                mm.then_inc(sem["c8"], 1)
            dn_map[(tqv, ic)] = sch.inc("c8")
            sch.on("tensor", t_dn)
            if tqv < 3:
                emit_oq(tqv + 1, ic)
        dnv = sch.val("c8")
        for ts_ in range(4):
            tt = 4 * tqv + ts_
            stop_ = sch.val("sto")
            c5g = sch.val("c5")
            def s_lx(e, tt=tt, stop_=stop_, dnv=dnv, c5g=c5g):
                e.wait_ge(sem["sto"], max(0, stop_ - 32))
                e.wait_ge(sem["c8"], dnv)
                e.wait_ge(sem["c5"], c5g)
                e.dma_start(out=s5[7][:, 0:1024], in_=din["x_own"][128 * tt:128 * (tt + 1), :]
                            ).then_inc(sem["str"], 16)
            ldv = sch.inc("str", 16)
            sch.on("sync", s_lx)
            stov = sch.val("sto")
            def v_ev(e, ts_=ts_, tt=tt, ldv=ldv, dnv=dnv, stov=stov):
                e.wait_ge(sem["c8"], dnv)
                e.wait_ge(sem["str"], ldv)
                e.wait_ge(sem["sto"], stov)
                e.drain()
                ps = (pA if ts_ < 2 else pB)
                src = ps[:, 1024 * (ts_ % 2):1024 * (ts_ % 2 + 1)]
                at = (t.aout_a if tt < 8 else t.aout_b)[:, D * (tt % 8):D * (tt % 8 + 1)]
                e.scalar_tensor_tensor(s5[6][:, 0:1024], src, 1.0, at, OP.mult, OP.add)
                e.drain()
                e.tensor_tensor(s5[6][:, 1024:2048], s5[6][:, 0:1024], s5[7][:, 0:1024],
                                OP.add).then_inc(sem["c5"], 1)
            c5v = sch.inc("c5")
            sch.on("vector", v_ev)
            def g_out(e, tt=tt, c5v=c5v):
                e.wait_ge(sem["c5"], c5v)
                e.dma_start(out=t.out_own[128 * tt:128 * (tt + 1), :],
                            in_=s5[6][:, 1024:2048]).then_inc(sem["sto"], 16)
            sch.inc("sto", 16)
            sch.on("gpsimd", g_out)
    stof = sch.val("sto")
    def g_fin(e, stof=stof):
        e.wait_ge(sem["sto"], stof)
    sch.on("gpsimd", g_fin)


def gen_dbg(nc, t, sch):
    sem = t.sem
    items = []
    if "kT" in t.debug: items.append((t.dout["kT"], t.kT[:, :]))
    if "qT" in t.debug: items.append((t.dout["qT"], t.qT[:, :]))
    if "vA" in t.debug: items.append((t.dout["vA"], t.vA[:, :]))
    if "xq" in t.debug: items.append((t.dout["xq"], t.xq_fm[:, :]))
    if "scl" in t.debug: items.append((t.dout["scl"], t.scl[:, :]))
    if "small" in t.debug: items.append((t.dout["small"], t.small[:, :]))
    if "ctxT" in t.debug: items.append((t.dout["ctxT"], t.qT[:, :]))
    if "ctxown" in t.debug: items.append((t.dout["ctxown"], t.ctx_own[:, :]))
    def g_dump(e):
        for dst, src in items:
            e.dma_start(out=dst, in_=src).then_inc(sem["sto"], 16)
        for nm in ("upst", "gatest", "g2st", "ost", "wdqst"):
            if nm in t.debug:
                stash = {"upst": t.stash_up, "gatest": t.stash_gate, "g2st": t.stash_g2,
                         "ost": t.stash_o, "wdqst": t.stash_wdq}[nm]
                e.dma_start(out=t.dout[nm], in_=stash.ap()).then_inc(sem["sto"], 16)
        if "aout" in t.debug:
            for i in range(16):
                src = (t.aout_a if i < 8 else t.aout_b)[:, D * (i % 8):D * (i % 8 + 1)]
                e.dma_start(out=t.dout["aout"][128 * i:128 * (i + 1), :], in_=src).then_inc(sem["sto"], 16)
        e.wait_ge(sem["sto"], stv)
    nst = len(items) + sum(1 for nm in ("upst", "gatest", "g2st", "ost", "wdqst") if nm in t.debug)
    if "aout" in t.debug:
        nst += 16
    stv = sch.inc("sto", 16 * nst)
    sch.on("gpsimd", g_dump)


def prep_inputs(L0_attn_x, L0_attn_w_kqv, L0_attn_w_proj, L0_attn_b_proj,
                L0_ffn_w_gate, L0_ffn_w_up, L0_ffn_w_down):
    x = np.ascontiguousarray(np.asarray(L0_attn_x, dtype=np.float32))
    wkqv = np.asarray(L0_attn_w_kqv, dtype=np.float32).reshape(16, 64, 3, D)
    wproj = np.asarray(L0_attn_w_proj, dtype=np.float32)
    bproj = np.asarray(L0_attn_b_proj, dtype=np.float32)
    wup = np.ascontiguousarray(np.asarray(L0_ffn_w_up, dtype=np.float32))
    wgate = np.ascontiguousarray(np.asarray(L0_ffn_w_gate, dtype=np.float32))
    wdown = np.ascontiguousarray(np.asarray(L0_ffn_w_down, dtype=np.float32))
    masks, ident, ones64, ind32, ind96, onesf = make_consts()

    wupT = np.ascontiguousarray(wup.T)
    wgateT = np.ascontiguousarray(wgate.T)
    wdownT = np.ascontiguousarray(wdown.T)

    in_maps = []
    for c in range(8):
        b, g = c // 2, c % 2
        wk = wkqv[8 * g:8 * (g + 1), :, 0, :].reshape(512, D)
        wq = wkqv[8 * g:8 * (g + 1), :, 1, :].reshape(512, D)
        wv = wkqv[8 * g:8 * (g + 1), :, 2, :].reshape(512, D)
        wprojT_c = np.concatenate([wproj[:, 512 * g:512 * (g + 1)].T,
                                   wproj[:, 512 * (1 - g):512 * (2 - g)].T], axis=0)
        in_maps.append({
            "x_seq": x[b],
            "x_own": np.ascontiguousarray(x[b, TOWN * g:TOWN * (g + 1), :]),
            "wkT": np.ascontiguousarray(wk.T), "wqT": np.ascontiguousarray(wq.T),
            "wvT": np.ascontiguousarray(wv.T),
            "wprojT": np.ascontiguousarray(wprojT_c),
            "b_row": bproj.reshape(1, D),
            "wupN": wup, "wupT": wupT, "wgateN": wgate, "wgateT": wgateT,
            "wdownN": wdown, "wdownT": wdownT,
            "masks": masks, "ident": ident, "ones64": ones64, "ind32": ind32, "ind96": ind96, "onesf": onesf,
            "offs": np.array([[16 * g, c, c ^ 1, g,
                               2 * g, 2 * g + 1, 4 + 2 * g, 5 + 2 * g]], dtype=np.int32),
        })
    return in_maps


def kernel(**inputs):
    in_maps = prep_inputs(**inputs)
    nc, t = build()
    res = run_bass_kernel_spmd(nc, in_maps, core_ids=list(range(8)))
    out = np.empty((4, S, D), dtype=np.float32)
    for c in range(8):
        b, g = c // 2, c % 2
        out[b, TOWN * g:TOWN * (g + 1), :] = res.results[c]["out_own"]
    return out


# version 18 (final)
# speedup vs baseline: 1.0405x; 1.0336x over previous
import numpy as np
import contextlib
import ml_dtypes
import concourse.bass as bass
import concourse.mybir as mybir
from concourse.bass_utils import run_bass_kernel_spmd

F32 = mybir.dt.float32
BF16 = mybir.dt.bfloat16
I32 = mybir.dt.int32
OP = mybir.AluOpType
AX = mybir.AxisListType
ACT = mybir.ActivationFunctionType

CMAGIC = float(np.float32(1.5 * 2 ** 23))

S = 4096
D = 1024
TOWN = 2048
NPAIR = 4
NKC = S // 128
NQT = S // 512
FFI = 3072
NIC = FFI // 128

SB_BASE = 16512
OFF_KT = SB_BASE
OFF_QT = SB_BASE + 32768
OFF_VA = SB_BASE + 65536
OFF_XNFM = SB_BASE + 98816
OFF_XQ = OFF_XNFM
OFF_CTXOWN = SB_BASE + 131584
OFF_FIX = SB_BASE + 164352
OFF_STREAM = SB_BASE + 186368


class Bag:
    pass


class Sched:
    def __init__(self):
        self.steps = []
        self.cnt = {}

    def on(self, eng, fn):
        self.steps.append((eng, fn))

    def inc(self, name, k=1):
        self.cnt[name] = self.cnt.get(name, 0) + k
        return self.cnt[name]

    def val(self, name):
        return self.cnt.get(name, 0)


def make_consts():
    masks = np.zeros((128, 4 * 512), dtype=ml_dtypes.bfloat16)
    p = np.arange(128)[:, None]
    q = np.arange(512)[None, :]
    for dv in range(4):
        masks[:, 512 * dv:512 * (dv + 1)] = (q >= 128 * dv + p).astype(ml_dtypes.bfloat16)
    ident = np.eye(128, dtype=np.float32)
    ones64 = np.zeros((128, 2), dtype=ml_dtypes.bfloat16)
    ones64[0:64, 0] = 1.0
    ones64[64:128, 1] = 1.0
    ind32 = np.zeros((32, 1024), dtype=np.float32)
    for cc in range(8):
        for pp in range(128):
            ind32[4 * cc + pp // 32, 128 * cc + pp] = 1.0
    ind96 = np.zeros((96, 3072), dtype=np.float32)
    for ic in range(24):
        for pp in range(128):
            ind96[4 * ic + pp // 32, 128 * ic + pp] = 1.0
    onesf = np.ones((128, 128), dtype=np.float32)
    return masks, ident, ones64, ind32, ind96, onesf


def build(debug=(), phases_upto=99):
    nc = bass.Bass("TRN2", target_bir_lowering=False)
    t = Bag()
    t.debug = set(debug)

    din = {}
    def di(name, shape, dtype=F32):
        din[name] = nc.dram_tensor(name, list(shape), dtype, kind="ExternalInput").ap()
    di("x_seq", (S, D))
    di("x_own", (TOWN, D))
    di("wkT", (D, 512)); di("wqT", (D, 512)); di("wvT", (D, 512))
    di("wprojT", (D, D))
    di("b_row", (1, D))
    di("wupN", (FFI, D)); di("wupT", (D, FFI))
    di("wgateN", (FFI, D)); di("wgateT", (D, FFI))
    di("wdownN", (D, FFI)); di("wdownT", (FFI, D))
    di("masks", (128, 2048), BF16)
    di("ident", (128, 128)); di("ones64", (128, 2), BF16); di("ind32", (32, 1024)); di("ind96", (96, 3072)); di("onesf", (128, 128))
    di("offs", (1, 8), I32)
    t.din = din
    t.out_own = nc.dram_tensor("out_own", [TOWN, D], F32, kind="ExternalOutput").ap()

    t.stash_up = nc.dram_tensor("stash_up", [NIC, 128, TOWN], F32, kind="Internal")
    t.stash_gate = nc.dram_tensor("stash_gate", [NIC, 128, TOWN], F32, kind="Internal")
    t.stash_g2 = nc.dram_tensor("stash_g2", [NIC, 128, TOWN], F32, kind="Internal")
    t.stash_o = nc.dram_tensor("stash_o", [NIC, 128, TOWN], F32, kind="Internal")
    t.stash_wdq = nc.dram_tensor("stash_wdq", [NIC, 128, D], BF16, kind="Internal")
    t.ar_in = [nc.dram_tensor(f"ar_in{i}", [1, 8], F32, kind="Internal") for i in range(4)]
    t.ar_out = [nc.dram_tensor(f"ar_out{i}", [1, 8], F32, kind="Internal") for i in range(4)]
    t.lnc_hbm = nc.dram_tensor("lnc_hbm", [8, 128, 8], F32, kind="Internal")
    t.ag_in = nc.dram_tensor("ag_in", [4, 128, S], BF16, kind="Internal")
    t.ag_out = nc.dram_tensor("ag_out", [8, 4, 128, S], BF16, kind="Internal")

    t.dout = {}
    def do(name, shape, dtype):
        if name in t.debug:
            t.dout[name] = nc.dram_tensor("dbg_" + name, list(shape), dtype, kind="ExternalOutput").ap()
    do("kT", (128, 4 * S), BF16)
    do("qT", (128, 4 * S), BF16)
    do("vA", (128, NKC * 520), BF16)
    do("xq", (128, 8 * TOWN), BF16)
    do("scl", (128, 32), F32)
    do("small", (128, 16), F32)
    do("ctxT", (128, 4 * S), BF16)
    do("ctxown", (128, 8 * TOWN), BF16)
    do("aout", (TOWN, D), F32)
    do("upst", (NIC, 128, TOWN), F32)
    do("gatest", (NIC, 128, TOWN), F32)
    do("g2st", (NIC, 128, TOWN), F32)
    do("ost", (NIC, 128, TOWN), F32)
    do("wdqst", (NIC, 128, D), BF16)

    es = contextlib.ExitStack()
    with es:
        def sb(name, shape, dtype, offset):
            return nc.alloc_sbuf_tensor_at(name, list(shape), dtype, offset=offset)

        t.kT = sb("kT", (128, 4 * S), BF16, OFF_KT)
        t.qT = sb("qT", (128, 4 * S), BF16, OFF_QT)
        t.vA = sb("vA", (128, NKC * 520), BF16, OFF_VA)
        t.xn_fm = sb("xn_fm", (128, 8 * TOWN), BF16, OFF_XNFM)
        t.wkqv = sb("wkqv", (128, 2 * 8 * 512), BF16, OFF_CTXOWN)
        t.wvb = sb("wvb", (128, 8 * 512), BF16, OFF_CTXOWN + 16384)
        t.xq_fm = sb("xq_fm", (128, 8 * TOWN), BF16, OFF_XQ)
        t.shift_sb = sb("shift_sb", (128, 1024), F32, OFF_XQ)
        t.praw32 = sb("praw32", (128, 1024), F32, OFF_XQ + 4096)
        t.pfin2 = sb("pfin2", (128, 2048), BF16, OFF_XQ + 8192)
        t.slotA = sb("slotA", (128, 2048), F32, OFF_XQ + 12288)
        t.slotB = sb("slotB", (128, 2048), F32, OFF_XQ + 20480)
        t.praw32b = sb("praw32b", (128, 1024), F32, OFF_XQ + 28672)
        t.ctx_own = sb("ctx_own", (128, 8 * TOWN), BF16, OFF_CTXOWN)
        t.wprojT = sb("wprojT", (128, 8 * D), BF16, OFF_VA)
        t.aout_a = sb("aout_a", (128, 8 * D), BF16, OFF_KT)
        t.aout_b = sb("aout_b", (128, 8 * D), BF16, OFF_KT + 16384)
        t.sTu = sb("sTu", (32, FFI), F32, OFF_CTXOWN)
        t.rTu = sb("rTu", (32, FFI), F32, OFF_CTXOWN + 12288)
        t.s5 = [sb(f"s5_{i}", (128, 2048), F32, OFF_QT + 8192 * i) for i in range(4)] + \
               [sb(f"s5_{i+4}", (128, 2048), F32, OFF_CTXOWN + 8192 * i) for i in range(4)]
        t.sall = sb("sall", (128, 768), F32, OFF_CTXOWN + 24576)
        t.wqb = sb("wqb", (128, 8 * 512), BF16, OFF_STREAM)
        t.ebuf0 = sb("ebuf0", (128, 1024), F32, OFF_STREAM + 8192)
        t.ebuf1 = sb("ebuf1", (128, 1024), F32, OFF_STREAM + 12288)
        t.oqbuf = sb("oqbuf", (128, NIC * 512), BF16, OFF_QT + 16384)
        t.wdq_s = sb("wdq_s", (128, 2 * D), BF16, OFF_QT + 16384 + 24576)

        fo = OFF_FIX
        t.masks = sb("masks", (128, 2048), BF16, fo); fo += 4096
        t.ident = sb("ident", (128, 128), F32, fo); fo += 512
        t.ones64 = sb("ones64", (128, 2), BF16, fo); fo += 32
        t.ind32 = sb("ind32", (32, 1024), F32, fo); fo += 4096
        t.ind96 = sb("ind96", (96, 3072), F32, OFF_QT)
        t.rTd = sb("rTd", (96, 1024), F32, OFF_QT + 12288)
        t.onesf = sb("onesf", (128, 128), F32, fo); fo += 512
        t.btile = sb("btile", (128, D), F32, fo); fo += 4096
        t.scl = sb("scl", (128, 32), F32, fo); fo += 128
        t.offs = sb("offs", (1, 8), I32, fo); fo += 32
        t.sTd = sb("sTd", (96, D), F32, fo); fo += 4096
        t.norm2 = sb("norm2", (2, 64), F32, fo); fo += 256
        t.small = sb("small", (128, 16), F32, fo); fo += 64
        t.mxcol = sb("mxcol", (128, 48), F32, fo); fo += 192
        t.negct = sb("negct", (128, 64), F32, fo); fo += 256
        assert fo <= OFF_STREAM, fo

        so = OFF_STREAM
        t.xbuf = sb("xbuf", (128, 2 * D), F32, so)
        t.ctxstage = sb("ctxstage", (128, 1024), F32, so)
        so += 8192
        t.xnbuf = sb("xnbuf", (128, D), F32, so); so += 4096
        t.xnbf = sb("xnbf", (128, 2 * D), BF16, so); so += 4096
        t.stats = sb("stats", (128, 32), F32, so); so += 128
        t.lnt = sb("lnt", (128, 32), F32, so); so += 128
        t.praw = sb("praw", (128, 1024), BF16, so)
        t.pfin = sb("pfin", (128, 1024), BF16, so + 2048)
        t.qdt = sb("qdt", (128, D), F32, so)
        t.rnorm_e = sb("rnorm_e", (128, 1024), F32, so)
        so += 4096
        t.rnorm = sb("rnorm", (128, 512), F32, so); so += 2048
        t.sqs = sb("sqs", (128, 1024), BF16, so); so += 2048
        assert so <= 229344, so

        t.pA = es.enter_context(nc.psum_tensor("pA", [128, 2048], F32))
        t.pB = es.enter_context(nc.psum_tensor("pB", [128, 2048], F32))

        sems = {}
        for name in ["ph", "ldx", "ldw", "lda", "ldb", "tra", "trb", "trs",
                     "c1", "c2", "c3", "c4", "c5", "c6", "c7", "c8", "c9",
                     "n1", "n2", "ccs", "ccd", "stw", "str", "sto",
                     "a1", "a2", "a3", "a4", "a5"]:
            sems[name] = es.enter_context(nc.semaphore(name))
        t.sem = sems

        sch = Sched()
        c2cs = {}
        def midhook(idx, part):
            if idx == 0 and part == 0:
                c2cs["a"] = _p5_sclderiv(nc, t, sch)
            if idx <= 11:
                _p5a_chunk(nc, t, sch, 2 * idx + part, c2cs["a"])
            elif 13 <= idx <= 24:
                _p5b_chunk(nc, t, sch, 2 * (idx - 13) + part, c2cs["b"])
        def prehook(idx):
            if idx == 13:
                def v_c15(e):
                    e.drain()
                    e.tensor_scalar(t.scl[:, 15:16], t.scl[:, 8:9], -CMAGIC, None,
                                    OP.mult).then_inc(t.sem["c2"], 1)
                r = _p5_arwin(nc, t, sch, 2, _p5_sc3(t), (2, 8), post_fn=v_c15)
                c2cs["b"] = sch.inc("c2")
                return r
            if idx == 26:
                return _p5_arwin(nc, t, sch, 3, _p5_sc4(t), (2, 10))
            return []
        def gen_p3x(nc, t, sch):
            gen_p3(nc, t, sch, midhook=midhook, prehook=prehook)
        def gen_p45(nc, t, sch):
            c7p5b = sch.val("c7")
            agccd, ccs_ag = _p4_ag_start(nc, t, sch)
            agv = _p4_extract(nc, t, sch, ccs_ag)
            _p4_proj(nc, t, sch, agv)
            c8proj = sch.val("c8")
            _p5_dequant(nc, t, sch, agccd, c8proj)
            _p5_main(nc, t, sch, agccd, c7p5b)
        gens = [gen_p0, gen_p1, gen_p2a, gen_p2b, gen_p3x, gen_p45]
        nph = 0
        for gi, gen in enumerate(gens):
            if gi >= phases_upto:
                break
            gen(nc, t, sch)
            sch.steps.append(("BARRIER", dict(sch.cnt)))
            nph += 1
        gen_dbg(nc, t, sch)
        sch.steps.append(("BARRIER", dict(sch.cnt)))
        nph += 1

        block = es.enter_context(nc.Block())

        SEM_ISSUER = {"lda": "sync", "ldb": "sync", "ldx": "sync", "ldw": "gpsimd",
                      "tra": "scalar", "trb": "scalar", "trs": "sync",
                      "ccd": "gpsimd", "stw": "gpsimd", "str": "sync", "sto": "gpsimd"}

        def runner(eng):
            def body(e):
                k = 0
                for s_eng, fn in sch.steps:
                    if s_eng == "BARRIER":
                        k += 1
                        for sname, issuer in SEM_ISSUER.items():
                            if issuer == eng and fn.get(sname, 0) > 0:
                                e.wait_ge(sems[sname], fn[sname])
                        e.drain()
                        e.nop().then_inc(sems["ph"], 1)
                        e.wait_ge(sems["ph"], 5 * k)
                    elif s_eng == eng:
                        fn(e)
                return None
            return body

        block.sync(runner("sync"))
        block.scalar(runner("scalar"))
        block.vector(runner("vector"))
        block.tensor(runner("tensor"))
        block.gpsimd(runner("gpsimd"))

    return nc, t


def gen_p0(nc, t, sch):
    sem = t.sem
    din = t.din

    ldx0 = sch.val("ldx")
    def s_loads(e):
        e.dma_start(out=t.masks[:, :], in_=din["masks"]).then_inc(sem["ldx"], 16)
        e.dma_start(out=t.ident[:, :], in_=din["ident"]).then_inc(sem["ldx"], 16)
        e.dma_start(out=t.ones64[:, :], in_=din["ones64"]).then_inc(sem["ldx"], 16)
        e.dma_start(out=t.ind32[:, :], in_=din["ind32"]).then_inc(sem["ldx"], 16)
        e.dma_start(out=t.onesf[:, :], in_=din["onesf"]).then_inc(sem["ldx"], 16)
        e.dma_start(out=t.offs[:, :], in_=din["offs"]).then_inc(sem["ldx"], 16)
        e.dma_start(out=t.xbuf[0:1, 0:D], in_=din["b_row"]).then_inc(sem["ldx"], 16)
    for _ in range(7):
        sch.inc("ldx", 16)
    sch.on("sync", s_loads)
    v_ld = sch.val("ldx")

    def g_loads(e):
        e.dma_start(out=t.wkqv[:, 0:8 * 512].rearrange("p (c f) -> p c f", c=8),
                    in_=din["wkT"].rearrange("(c p) f -> p c f", p=128)).then_inc(sem["ldw"], 16)
        e.dma_start(out=t.wkqv[:, 8 * 512:16 * 512].rearrange("p (c f) -> p c f", c=8),
                    in_=din["wqT"].rearrange("(c p) f -> p c f", p=128)).then_inc(sem["ldw"], 16)
        e.dma_start(out=t.wvb[:, :].rearrange("p (c f) -> p c f", c=8),
                    in_=din["wvT"].rearrange("(c p) f -> p c f", p=128)).then_inc(sem["ldw"], 16)
        ones_ap = t.vA[:, :].rearrange("p (c h e) -> p c h e", c=NKC, h=8)[:, :, :, 64:65]
        e.memset(ones_ap, 1.0)
        e.memset(t.scl[:, :], 0.0)
        e.memset(t.small[:, :], 0.0)
        e.drain()
        e.memset(t.scl[:, 12:13], CMAGIC)
    for _ in range(3):
        sch.inc("ldw", 16)
    sch.on("gpsimd", g_loads)

    def t_btile(e):
        e.wait_ge(sem["ldx"], v_ld)
        e.matmul(t.pA[:, 0:512], t.onesf[0:1, :], t.xbuf[0:1, 0:512], start=True, stop=True)
        e.matmul(t.pA[:, 512:1024], t.onesf[0:1, :], t.xbuf[0:1, 512:1024],
                 start=True, stop=True).then_inc(sem["c1"], 1)
    c1v = sch.inc("c1")
    sch.on("tensor", t_btile)

    def v_btile(e):
        e.wait_ge(sem["c1"], c1v)
        e.tensor_copy(t.btile[:, :], t.pA[:, 0:1024])
    sch.on("vector", v_btile)


def gen_ln(nc, t, sch, tiles, x_in_fn, fm_dst_fn, absmax_base=None, qdq=False,
           loader=("sync", ("lda", "ldb")), cache_to=None, stats_cached=False):
    sem = t.sem
    NG = tiles // 8
    for g in range(NG):
        if stats_cached:
            if g == 0:
                def g_push(e):
                    e.dma_start(out=t.lnc_hbm.ap().rearrange("b p c -> p b c"),
                                in_=t.negct[:, 0:64].rearrange("p (b c) -> p b c", c=8)
                                ).then_inc(sem["ccd"], 16)
                sch.inc("ccd", 16)
                sch.on("gpsimd", g_push)
            pushv = sch.val("ccd")
            c3c = sch.val("c3")
            def g_stat(e, g=g, c3c=c3c, pushv=pushv):
                e.wait_ge(sem["ccd"], pushv)
                e.wait_ge(sem["c3"], c3c)
                mnoff = e.value_load(t.offs[0:1, (4 if g == 0 else 5):(5 if g == 0 else 6)])
                nmoff = e.value_load(t.offs[0:1, (6 if g == 0 else 7):(7 if g == 0 else 8)])
                e.dma_start(out=t.lnt[:, 0:8],
                            in_=t.lnc_hbm.ap()[bass.ts(mnoff, 1), :, :]).then_inc(sem["ccd"], 16)
                e.dma_start(out=t.lnt[:, 24:32],
                            in_=t.lnc_hbm.ap()[bass.ts(nmoff, 1), :, :]).then_inc(sem["ccd"], 16)
            ldsv = sch.inc("ccd", 32)
            sch.on("gpsimd", g_stat)
            for ji in range(8):
                j = 8 * g + ji
                c3p = sch.val("c3")
                leng = loader[0]
                lsem = loader[1][j % 2]
                def s_load2(e, j=j, c3p=c3p, lsem=lsem):
                    e.wait_ge(sem["c3"], max(0, c3p - 1))
                    e.dma_start(out=t.xbuf[:, (j % 2) * D:(j % 2 + 1) * D],
                                in_=x_in_fn(e, j)).then_inc(sem[lsem], 16)
                ldv = sch.inc(lsem, 16)
                sch.on(leng, s_load2)
                trv = sch.val(("tra", "trb")[j % 2])
                def v_apply(e, j=j, ji=ji, ldv=ldv, trv=trv, lsem=lsem,
                            tsem=("tra", "trb")[j % 2], ldsv=ldsv):
                    e.wait_ge(sem[lsem], ldv)
                    e.wait_ge(sem["ccd"], ldsv)
                    e.wait_ge(sem[tsem], trv)
                    e.drain()
                    ln = t.lnt
                    e.tensor_scalar(t.xnbuf[:, :], t.xbuf[:, (j % 2) * D:(j % 2 + 1) * D],
                                    ln[:, 24 + ji:25 + ji], ln[:, ji:ji + 1],
                                    OP.mult, OP.add).then_inc(sem["c3"], 1)
                    e.drain()
                    e.tensor_scalar(t.qdt[:, :], t.xnbuf[:, :], t.scl[:, 1:2], CMAGIC,
                                    OP.mult, OP.add)
                    e.drain()
                    e.tensor_scalar(t.xnbf[:, (j % 2) * D:(j % 2 + 1) * D], t.qdt[:, :],
                                    CMAGIC, t.scl[:, 0:1], OP.subtract, OP.mult).then_inc(sem["c2"], 1)
                sch.inc("c3")
                c2v = sch.inc("c2")
                sch.on("vector", v_apply)
                tsem = ("tra", "trb")[j % 2]
                def a_tr(e, j=j, c2v=c2v, tsem=tsem):
                    e.wait_ge(sem["c2"], c2v)
                    for c in range(8):
                        e.dma_start(out=fm_dst_fn(j, c),
                                    in_=t.xnbf[:, (j % 2) * D + 128 * c:(j % 2) * D + 128 * (c + 1)],
                                    transpose=True).then_inc(sem[tsem], 16)
                sch.inc(tsem, 16 * 8)
                sch.on("scalar", a_tr)
            continue
        for ji in range(8):
            j = 8 * g + ji
            c1p = sch.val("c1")
            c9p = sch.val("c9")
            c3p = sch.val("c3")
            leng = loader[0]
            lsem = loader[1][j % 2]
            def s_load(e, j=j, c1p=c1p, c9p=c9p, c3p=c3p, lsem=lsem):
                e.wait_ge(sem["c1"], max(0, c1p - 1))
                e.wait_ge(sem["c9"], max(0, c9p - 1))
                e.wait_ge(sem["c3"], c3p)
                e.dma_start(out=t.xbuf[:, (j % 2) * D:(j % 2 + 1) * D],
                            in_=x_in_fn(e, j)).then_inc(sem[lsem], 16)
            ldv = sch.inc(lsem, 16)
            sch.on(leng, s_load)
            c1g = sch.val("c1")
            def a_sq(e, j=j, ldv=ldv, lsem=lsem, c1g=c1g):
                e.wait_ge(sem[lsem], ldv)
                e.wait_ge(sem["c1"], c1g)
                sc = 2 * (j % 16)
                e.activation(t.qdt[:, :], t.xbuf[:, (j % 2) * D:(j % 2 + 1) * D], ACT.Square,
                             accum_out=t.stats[:, sc + 1:sc + 2]).then_inc(sem["c1"], 1)
            sch.inc("c1")
            sch.on("scalar", a_sq)
            def v_sum(e, j=j, ldv=ldv, lsem=lsem):
                e.wait_ge(sem[lsem], ldv)
                sc = 2 * (j % 16)
                e.reduce_sum(t.stats[:, sc:sc + 1],
                             t.xbuf[:, (j % 2) * D:(j % 2 + 1) * D], AX.X).then_inc(sem["c9"], 1)
            sch.inc("c9")
            sch.on("vector", v_sum)
        c1v = sch.val("c1")
        def v_chain(e, g=g, c1v=c1v):
            e.wait_ge(sem["c1"], c1v)
            e.drain()
            base = 16 * (g % 2)
            st = t.stats[:, base:base + 16].rearrange("p (j two) -> p j two", two=2)
            ln = t.lnt
            e.tensor_scalar(ln[:, 0:8], st[:, :, 0], 1.0 / D, None, OP.mult)
            e.tensor_scalar(ln[:, 8:16], st[:, :, 1], 1.0 / D, None, OP.mult)
            e.drain()
            e.tensor_tensor(ln[:, 16:24], ln[:, 0:8], ln[:, 0:8], OP.mult)
            e.drain()
            e.scalar_tensor_tensor(ln[:, 24:32], ln[:, 8:16], 1e-5, ln[:, 16:24],
                                   OP.add, OP.subtract)
            e.drain()
            e.reciprocal(ln[:, 16:24], ln[:, 24:32]).then_inc(sem["c4"], 1)
        c4v = sch.inc("c4")
        sch.on("vector", v_chain)
        def a_sqrt(e, c4v=c4v):
            e.wait_ge(sem["c4"], c4v)
            e.activation(t.lnt[:, 24:32], t.lnt[:, 16:24], ACT.Sqrt).then_inc(sem["c5"], 1)
        c5v = sch.inc("c5")
        sch.on("scalar", a_sqrt)
        def v_nmr(e, g=g, c5v=c5v, cache_to=cache_to):
            e.wait_ge(sem["c5"], c5v)
            ln = t.lnt
            e.tensor_tensor(ln[:, 16:24], ln[:, 0:8], ln[:, 24:32], OP.mult)
            e.drain()
            e.tensor_scalar(ln[:, 0:8], ln[:, 16:24], -1.0, None, OP.mult)
            e.drain()
            if cache_to is not None:
                cb = cache_to + 8 * g
                e.tensor_copy(t.negct[:, cb:cb + 8], ln[:, 0:8])
                e.tensor_copy(t.negct[:, 32 + cb:32 + cb + 8], ln[:, 24:32])
                e.drain()
        sch.on("vector", v_nmr)
        for ji in range(8):
            j = 8 * g + ji
            c3p = sch.val("c3")
            leng = loader[0]
            lsem = loader[1][j % 2]
            c1p2 = sch.val("c1")
            c9p2 = sch.val("c9")
            def s_load2(e, j=j, c3p=c3p, lsem=lsem, c1p2=c1p2, c9p2=c9p2):
                e.wait_ge(sem["c3"], max(0, c3p - 1))
                e.wait_ge(sem["c1"], c1p2)
                e.wait_ge(sem["c9"], c9p2)
                e.dma_start(out=t.xbuf[:, (j % 2) * D:(j % 2 + 1) * D],
                            in_=x_in_fn(e, j)).then_inc(sem[lsem], 16)
            ldv = sch.inc(lsem, 16)
            sch.on(leng, s_load2)
            trv = sch.val(("tra", "trb")[j % 2])
            def v_apply(e, j=j, ji=ji, ldv=ldv, trv=trv, lsem=lsem, tsem=("tra", "trb")[j % 2]):
                e.wait_ge(sem[lsem], ldv)
                e.wait_ge(sem[tsem], trv)
                e.drain()
                ln = t.lnt
                e.tensor_scalar(t.xnbuf[:, :], t.xbuf[:, (j % 2) * D:(j % 2 + 1) * D],
                                ln[:, 24 + ji:25 + ji], ln[:, ji:ji + 1],
                                OP.mult, OP.add).then_inc(sem["c3"], 1)
                e.drain()
                if not qdq:
                    e.tensor_copy(t.xnbf[:, (j % 2) * D:(j % 2 + 1) * D],
                                  t.xnbuf[:, :]).then_inc(sem["c2"], 1)
                else:
                    e.tensor_scalar(t.qdt[:, :], t.xnbuf[:, :], t.scl[:, 1:2], CMAGIC,
                                    OP.mult, OP.add)
                    e.drain()
                    e.tensor_scalar(t.xnbf[:, (j % 2) * D:(j % 2 + 1) * D], t.qdt[:, :],
                                    CMAGIC, t.scl[:, 0:1], OP.subtract, OP.mult).then_inc(sem["c2"], 1)
                if absmax_base is not None:
                    e.drain()
                    e.tensor_reduce(t.mxcol[:, absmax_base + j:absmax_base + j + 1],
                                    t.xnbuf[:, :], AX.X, OP.max, apply_absolute_value=True)
            sch.inc("c3")
            c2v = sch.inc("c2")
            sch.on("vector", v_apply)
            tsem = ("tra", "trb")[j % 2]
            def a_tr(e, j=j, c2v=c2v, tsem=tsem):
                e.wait_ge(sem["c2"], c2v)
                for c in range(8):
                    e.dma_start(out=fm_dst_fn(j, c),
                                in_=t.xnbf[:, (j % 2) * D + 128 * c:(j % 2) * D + 128 * (c + 1)],
                                transpose=True).then_inc(sem[tsem], 16)
            sch.inc(tsem, 16 * 8)
            sch.on("scalar", a_tr)


def gen_p1(nc, t, sch):
    sem = t.sem
    din = t.din
    pA, pB = t.pA, t.pB

    def x_in(e, j):
        return din["x_seq"][128 * j:128 * (j + 1), :]

    def fm_dst(j, c):
        return t.xn_fm[:, TOWN * c + 128 * j: TOWN * c + 128 * (j + 1)]

    def kqv_half(th):
        trva = sch.val("tra")
        trvb = sch.val("trb")
        ldwv = sch.val("ldw")
        for wi in range(2):
            for fc in range(4):
                j = wi * 4 + fc
                evv = sch.val("c6")
                def t_mm(e, wi=wi, fc=fc, th=th, trva=trva, trvb=trvb, ldwv=ldwv, evv=evv):
                    e.wait_ge(sem["ldw"], ldwv)
                    e.wait_ge(sem["tra"], trva)
                    e.wait_ge(sem["trb"], trvb)
                    e.wait_ge(sem["c6"], evv)
                    wbase = (wi * 8 + 0) * 512
                    for cc in range(8):
                        lhs = t.wkqv[:, wi * 4096 + 512 * cc + 128 * fc:
                                        wi * 4096 + 512 * cc + 128 * (fc + 1)]
                        e.ldweights(lhs)
                        for tq in range(4):
                            mm = e.matmul(pA[:, 512 * tq:512 * (tq + 1)], lhs,
                                          t.xn_fm[:, TOWN * cc + 512 * tq:
                                                     TOWN * cc + 512 * (tq + 1)],
                                          start=(cc == 0), stop=(cc == 7))
                    mm.then_inc(sem["c7"], 1)
                c7v = sch.inc("c7")
                sch.on("tensor", t_mm)
                def v_evac(e, wi=wi, fc=fc, th=th, c7v=c7v):
                    e.wait_ge(sem["c7"], c7v)
                    dst = (t.kT if wi == 0 else t.qT)
                    e.tensor_copy(dst[:, S * fc + 2048 * th:S * fc + 2048 * (th + 1)],
                                  pA[:, 0:2048]).then_inc(sem["c6"], 1)
                sch.inc("c6")
                sch.on("vector", v_evac)

    def v_half(th):
        trva = sch.val("tra")
        trvb = sch.val("trb")
        for tc in range(16 * th, 16 * (th + 1)):
            evv = sch.val("c9")
            slot = tc % 2
            def t_vmm(e, tc=tc, slot=slot, trva=trva, trvb=trvb, evv=evv):
                e.wait_ge(sem["tra"], trva)
                e.wait_ge(sem["trb"], trvb)
                e.wait_ge(sem["c9"], max(0, evv - 1))
                for cc in range(8):
                    tl = tc % 16
                    lhs = t.xn_fm[:, TOWN * cc + 128 * tl:TOWN * cc + 128 * (tl + 1)]
                    e.ldweights(lhs)
                    mm = e.matmul(pB[:, 512 * slot:512 * (slot + 1)], lhs, t.wvb[:, 512 * cc:512 * (cc + 1)],
                                  start=(cc == 0), stop=(cc == 7))
                mm.then_inc(sem["c8"], 1)
            c8v = sch.inc("c8")
            sch.on("tensor", t_vmm)
            def v_vevac(e, tc=tc, slot=slot, c8v=c8v):
                e.wait_ge(sem["c8"], c8v)
                src = pB[:, 512 * slot:512 * (slot + 1)].rearrange("p (h d) -> p h d", h=8)
                dst = t.vA[:, 520 * tc:520 * (tc + 1)].rearrange("p (h e) -> p h e", h=8)[:, :, 0:64]
                e.tensor_copy(dst, src).then_inc(sem["c9"], 1)
            sch.inc("c9")
            sch.on("vector", v_vevac)

    gen_ln(nc, t, sch, 16, x_in, fm_dst, absmax_base=0, cache_to=0)
    kqv_half(0)
    v_half(0)
    c7v, c8v = sch.val("c7"), sch.val("c8")
    def a_waitreuse(e, c7v=c7v, c8v=c8v):
        e.wait_ge(sem["c7"], c7v)
        e.wait_ge(sem["c8"], c8v)
    sch.on("scalar", a_waitreuse)
    def x_in2(e, j):
        return din["x_seq"][128 * (j + 16):128 * (j + 17), :]
    gen_ln(nc, t, sch, 16, x_in2, fm_dst, absmax_base=16, cache_to=16)
    kqv_half(1)
    v_half(1)

    vfin = sch.val("c3")
    n2p = sch.val("n2")
    def v_negb(e, vfin=vfin):
        e.wait_ge(sem["c3"], vfin)
        e.drain()
        e.tensor_reduce(t.small[:, 6:7], t.mxcol[:, 0:32], AX.X, OP.max).then_inc(sem["n1"], 1)
    n1v3 = sch.inc("n1")
    sch.on("vector", v_negb)
    def t_ntr3(e, n1v3=n1v3, n2p=n2p):
        e.wait_ge(sem["n1"], n1v3)
        e.wait_ge(sem["n2"], n2p)
        e.transpose(pB[0:1, 8:136], t.small[:, 6:7], t.ident[:, :]).then_inc(sem["n2"], 1)
    n2v3 = sch.inc("n2")
    sch.on("tensor", t_ntr3)
    def v_gmax(e, n2v3=n2v3):
        e.wait_ge(sem["n2"], n2v3)
        e.tensor_reduce(t.small[0:1, 7:8], pB[0:1, 8:136], AX.X, OP.max).then_inc(sem["n1"], 1)
    sch.inc("n1")
    sch.on("vector", v_gmax)


def gen_p2a(nc, t, sch):
    sem = t.sem
    din = t.din
    pB = t.pB

    ccd1 = sch.inc("ccd", 16)
    ccs1 = sch.inc("ccs", 1)
    ccd2 = sch.inc("ccd", 16)
    arv = sch.inc("c1")
    def g_ar(e, ccd1=ccd1, ccs1=ccs1, ccd2=ccd2):
        e.dma_start(out=t.ar_in[0].ap()[0:1, 0:8], in_=t.small[0:1, 0:8]).then_inc(sem["ccd"], 16)
        e.wait_ge(sem["ccd"], ccd1)
        e.collective_compute("AllReduce", OP.max, replica_groups=[list(range(8))],
                             ins=[t.ar_in[0].ap()], outs=[t.ar_out[0].ap()]).then_inc(sem["ccs"], 1)
        e.wait_ge(sem["ccs"], ccs1)
        e.dma_start(out=t.small[0:1, 8:9], in_=t.ar_out[0].ap()[0:1, 7:8]).then_inc(sem["ccd"], 16)
        e.wait_ge(sem["ccd"], ccd2)
        e.nop().then_inc(sem["c1"], 1)
    sch.on("gpsimd", g_ar)

    def v_scales(e, arv=arv):
        e.wait_ge(sem["c1"], arv)
        e.tensor_scalar(t.small[0:1, 9:10], t.small[0:1, 8:9], 1.0 / 32767.0, 1e-12, OP.mult, OP.max)
        e.drain()
        e.reciprocal(t.small[0:1, 10:11], t.small[0:1, 9:10]).then_inc(sem["c2"], 1)
    c2v = sch.inc("c2")
    sch.on("vector", v_scales)
    def t_bcast(e, c2v=c2v):
        e.wait_ge(sem["c2"], c2v)
        e.matmul(pB[:, 0:2], t.onesf[0:1, :], t.small[0:1, 9:11], start=True, stop=True).then_inc(sem["c1"], 1)
    c1v = sch.inc("c1")
    sch.on("tensor", t_bcast)
    def v_sc2(e, c1v=c1v):
        e.wait_ge(sem["c1"], c1v)
        e.tensor_copy(t.scl[:, 0:2], pB[:, 0:2]).then_inc(sem["c2"], 1)
    scv = sch.inc("c2")
    sch.on("vector", v_sc2)

    def x_in(e, j):
        return din["x_own"][128 * j:128 * (j + 1), :]

    def fm_dst(j, c):
        return t.xq_fm[:, TOWN * c + 128 * j: TOWN * c + 128 * (j + 1)]

    def v_wait(e, scv=scv):
        e.wait_ge(sem["c2"], scv)
    sch.on("vector", v_wait)
    gen_ln(nc, t, sch, 16, x_in, fm_dst, absmax_base=None, qdq=True, stats_cached=True)


def _gen_wscales(nc, t, sch, wname, nchunks, blocks_per_chunk, dst_sT, halfload):
    sem = t.sem
    din = t.din
    for i in range(nchunks):
        nb = blocks_per_chunk
        rounds = 2 if halfload else 1
        for h in range(rounds):
            cw = 1536 if halfload else nb * 32
            c6p = sch.val("c6")
            c7p = sch.val("c7")
            c8p = sch.val("c8")
            def s_ld(e, i=i, h=h, cw=cw, c6p=c6p, c7p=c7p, c8p=c8p, wname=wname):
                e.wait_ge(sem["c6"], c6p)
                e.wait_ge(sem["c7"], c7p)
                e.wait_ge(sem["c8"], c8p)
                e.dma_start(out=t.xbuf[:, 0:cw],
                            in_=din[wname][128 * i:128 * (i + 1), 1536 * h:1536 * h + cw]
                            ).then_inc(sem["lda"], 16)
            ldv = sch.inc("lda", 16)
            sch.on("sync", s_ld)
            nbh = nb // rounds
            def v_red(e, i=i, h=h, cw=cw, ldv=ldv, nb=nb, nbh=nbh):
                e.wait_ge(sem["lda"], ldv)
                wv = t.xbuf[:, 0:cw].rearrange("p (b e) -> p b e", e=32)
                e.tensor_reduce(t.qdt[:, 0:nbh], wv, AX.X, OP.max, apply_absolute_value=True)
                e.drain()
                e.tensor_scalar(t.sall[:, nb * i + nbh * h:nb * i + nbh * (h + 1)], t.qdt[:, 0:nbh],
                                1.0 / 7.0, 1e-12, OP.mult, OP.max).then_inc(sem["c6"], 1)
            sch.inc("c6")
            sch.on("vector", v_red)
    for i in range(nchunks):
        c6v = sch.val("c6")
        n2p = sch.val("n2")
        nb = blocks_per_chunk
        def t_tr(e, i=i, c6v=c6v, n2p=n2p, nb=nb):
            e.wait_ge(sem["c6"], c6v)
            e.wait_ge(sem["n2"], n2p)
            e.transpose(t.pB[0:nb, 0:128], t.sall[:, nb * i:nb * (i + 1)],
                        t.ident[:, :]).then_inc(sem["n1"], 1)
        n1v = sch.inc("n1")
        sch.on("tensor", t_tr)
        def v_ev(e, i=i, n1v=n1v, nb=nb, dst_sT=dst_sT):
            e.wait_ge(sem["n1"], n1v)
            e.tensor_copy(dst_sT[0:nb, 128 * i:128 * (i + 1)], t.pB[0:nb, 0:128]).then_inc(sem["n2"], 1)
        sch.inc("n2")
        sch.on("vector", v_ev)


def _gen_updown_mm(nc, t, sch, wTname, stash, maxcol0):
    sem = t.sem
    din = t.din
    pA, pB = t.pA, t.pB
    for isl in range(6):
        for cc in range(8):
            c7p = sch.val("c7")
            c6p = sch.val("c6")
            def s_ld(e, isl=isl, cc=cc, c7p=c7p, c6p=c6p, wTname=wTname):
                e.wait_ge(sem["c7"], max(0, c7p - 1))
                e.wait_ge(sem["c6"], c6p)
                e.dma_start(out=t.qdt[:, 512 * (cc % 2):512 * (cc % 2 + 1)],
                            in_=din[wTname][128 * cc:128 * (cc + 1), 512 * isl:512 * (isl + 1)]
                            ).then_inc(sem[("lda", "ldb")[cc % 2]], 16)
            ldv = sch.inc(("lda", "ldb")[cc % 2], 16)
            sch.on("sync", s_ld)
            n2p = sch.val("n2")
            c6r = sch.val("c6")
            def t_bc(e, isl=isl, cc=cc, n2p=n2p, c6r=c6r):
                e.wait_ge(sem["n2"], n2p)
                e.wait_ge(sem["c6"], c6r)
                e.matmul(pB[:, 1024:1536], t.ind32[:, 128 * cc:128 * (cc + 1)],
                         t.rTu[:, 512 * isl:512 * (isl + 1)],
                         start=True, stop=True)
                e.matmul(pB[:, 1536:2048], t.ind32[:, 128 * cc:128 * (cc + 1)],
                         t.sTu[:, 512 * isl:512 * (isl + 1)],
                         start=True, stop=True).then_inc(sem["n1"], 1)
            n1v = sch.inc("n1")
            sch.on("tensor", t_bc)
            mm8 = sch.val("c8")
            def v_dq(e, isl=isl, cc=cc, ldv=ldv, n1v=n1v, mm8=mm8):
                e.wait_ge(sem[("lda", "ldb")[cc % 2]], ldv)
                e.wait_ge(sem["n1"], n1v)
                e.wait_ge(sem["c8"], mm8)
                src = t.qdt[:, 512 * (cc % 2):512 * (cc % 2 + 1)]
                e.tensor_tensor(t.rnorm[:, :], src, pB[:, 1024:1536], OP.mult)
                e.drain()
                e.tensor_scalar(t.rnorm[:, :], t.rnorm[:, :], CMAGIC, CMAGIC, OP.add, OP.subtract)
                e.drain()
                e.tensor_tensor(t.wqb[:, 512 * cc:512 * (cc + 1)], t.rnorm[:, :], pB[:, 1536:2048],
                                OP.mult).then_inc(sem["n2"], 1)
                e.drain()
                e.nop().then_inc(sem["c7"], 1)
            sch.inc("n2")
            sch.inc("c7")
            sch.on("vector", v_dq)
        dqv = sch.val("n2")
        for il in range(4):
            ic = 4 * isl + il
            evp = sch.val("c5")
            def t_mm(e, il=il, dqv=dqv, evp=evp):
                e.wait_ge(sem["n2"], dqv)
                e.wait_ge(sem["c5"], evp)
                for cc in range(8):
                    lhs = t.wqb[:, 512 * cc + 128 * il:512 * cc + 128 * (il + 1)]
                    e.ldweights(lhs)
                    for tq in range(4):
                        mm = e.matmul(pA[:, 512 * tq:512 * (tq + 1)], lhs,
                                      t.xq_fm[:, TOWN * cc + 512 * tq:TOWN * cc + 512 * (tq + 1)],
                                      start=(cc == 0), stop=(cc == 7))
                mm.then_inc(sem["c8"], 1)
            c8v = sch.inc("c8")
            sch.on("tensor", t_mm)
            stp = sch.val("stw")
            def v_evac(e, ic=ic, c8v=c8v, stp=stp, maxcol0=maxcol0):
                e.wait_ge(sem["c8"], c8v)
                e.wait_ge(sem["stw"], stp)
                e.drain()
                e.tensor_copy(t.ebuf0[:, :], pA[:, 0:1024]).then_inc(sem["c5"], 1)
                e.tensor_copy(t.ebuf1[:, :], pA[:, 1024:2048]).then_inc(sem["c5"], 1)
                e.drain()
                e.tensor_reduce(t.mxcol[:, maxcol0 + 2 * (ic % 12):maxcol0 + 2 * (ic % 12) + 1],
                                t.ebuf0[:, :], AX.X, OP.max, apply_absolute_value=True)
                e.drain()
                e.tensor_reduce(t.mxcol[:, maxcol0 + 2 * (ic % 12) + 1:maxcol0 + 2 * (ic % 12) + 2],
                                t.ebuf1[:, :], AX.X, OP.max, apply_absolute_value=True).then_inc(sem["c6"], 1)
            sch.inc("c5", 2)
            c6v = sch.inc("c6")
            sch.on("vector", v_evac)
            if ic % 12 == 11:
                def v_mrg(e, ic=ic, c6v=c6v, maxcol0=maxcol0):
                    e.wait_ge(sem["c6"], c6v)
                    e.drain()
                    e.tensor_reduce(t.mxcol[:, maxcol0 + 24 + (ic // 12):maxcol0 + 25 + (ic // 12)],
                                    t.mxcol[:, maxcol0:maxcol0 + 24], AX.X, OP.max).then_inc(sem["c6"], 1)
                sch.inc("c6")
                sch.on("vector", v_mrg)
            def g_st(e, ic=ic, c6v=c6v, stash=stash):
                e.wait_ge(sem["c6"], c6v)
                e.dma_start(out=stash.ap()[ic, :, 0:1024], in_=t.ebuf0[:, :]).then_inc(sem["stw"], 16)
                e.dma_start(out=stash.ap()[ic, :, 1024:2048], in_=t.ebuf1[:, :]).then_inc(sem["stw"], 16)
            sch.inc("stw", 32)
            sch.on("gpsimd", g_st)


def gen_p2b(nc, t, sch):
    sem = t.sem
    pB = t.pB
    def gen_recip():
        c6v = sch.val("c6")
        n2v = sch.val("n2")
        def v_rc(e, c6v=c6v, n2v=n2v):
            e.wait_ge(sem["c6"], c6v)
            e.wait_ge(sem["n2"], n2v)
            e.reciprocal(t.rTu[:, :], t.sTu[:, :]).then_inc(sem["c6"], 1)
        sch.inc("c6")
        sch.on("vector", v_rc)

    _gen_wscales(nc, t, sch, "wupN", NIC, 32, t.sTu, False)
    gen_recip()
    _gen_updown_mm(nc, t, sch, "wupT", t.stash_up, 0)
    _gen_finmax(nc, t, sch, 0, 11)
    _gen_wscales(nc, t, sch, "wgateN", NIC, 32, t.sTu, False)
    gen_recip()
    _gen_updown_mm(nc, t, sch, "wgateT", t.stash_gate, 0)
    _gen_finmax(nc, t, sch, 0, 12)
    _gen_wscales(nc, t, sch, "wdownN", 8, 96, t.sTd, True)
    ccd1 = sch.inc("ccd", 16)
    ccs1 = sch.inc("ccs", 1)
    ccd2 = sch.inc("ccd", 16)
    c6f = sch.val("c6")
    arv = sch.inc("c1")
    def g_ar(e, ccd1=ccd1, ccs1=ccs1, ccd2=ccd2, c6f=c6f):
        e.wait_ge(sem["c6"], c6f)
        e.dma_start(out=t.ar_in[1].ap()[0:1, 0:8], in_=t.small[0:1, 8:16]).then_inc(sem["ccd"], 16)
        e.wait_ge(sem["ccd"], ccd1)
        e.collective_compute("AllReduce", OP.max, replica_groups=[list(range(8))],
                             ins=[t.ar_in[1].ap()], outs=[t.ar_out[1].ap()]).then_inc(sem["ccs"], 1)
        e.wait_ge(sem["ccs"], ccs1)
        e.dma_start(out=t.small[0:1, 11:13], in_=t.ar_out[1].ap()[0:1, 3:5]).then_inc(sem["ccd"], 16)
        e.wait_ge(sem["ccd"], ccd2)
        e.nop().then_inc(sem["c1"], 1)
    sch.on("gpsimd", g_ar)
    def v_s2(e, arv=arv):
        e.wait_ge(sem["c1"], arv)
        e.tensor_scalar(t.small[0:1, 0:2], t.small[0:1, 11:13], 1.0 / 32767.0, 1e-12, OP.mult, OP.max)
        e.drain()
        e.reciprocal(t.small[0:1, 2:4], t.small[0:1, 0:2]).then_inc(sem["c2"], 1)
    c2v = sch.inc("c2")
    sch.on("vector", v_s2)
    def t_bc2(e, c2v=c2v):
        e.wait_ge(sem["c2"], c2v)
        e.matmul(pB[:, 0:4], t.onesf[0:1, :], t.small[0:1, 0:4], start=True, stop=True).then_inc(sem["c1"], 1)
    c1v = sch.inc("c1")
    sch.on("tensor", t_bc2)
    def v_bc2(e, c1v=c1v):
        e.wait_ge(sem["c1"], c1v)
        e.tensor_copy(t.scl[:, 4:8], pB[:, 0:4]).then_inc(sem["c2"], 1)
    sch.inc("c2")
    sch.on("vector", v_bc2)


def _gen_finmax(nc, t, sch, maxcol0, dstcol):
    sem = t.sem
    pB = t.pB
    c6v = sch.val("c6")
    def v_f(e, c6v=c6v, maxcol0=maxcol0):
        e.wait_ge(sem["c6"], c6v)
        e.tensor_reduce(t.small[:, 14:15], t.mxcol[:, maxcol0 + 24:maxcol0 + 26],
                        AX.X, OP.max).then_inc(sem["n1"], 1)
    n1v = sch.inc("n1")
    sch.on("vector", v_f)
    n2p = sch.val("n2")
    def t_f(e, n1v=n1v, n2p=n2p):
        e.wait_ge(sem["n1"], n1v)
        e.wait_ge(sem["n2"], n2p)
        e.transpose(pB[0:1, 8:136], t.small[:, 14:15], t.ident[:, :]).then_inc(sem["n2"], 1)
    n2v = sch.inc("n2")
    sch.on("tensor", t_f)
    def v_f2(e, n2v=n2v, dstcol=dstcol):
        e.wait_ge(sem["n2"], n2v)
        e.tensor_reduce(t.small[0:1, dstcol:dstcol + 1], pB[0:1, 8:136], AX.X, OP.max).then_inc(sem["c6"], 1)
    sch.inc("c6")
    sch.on("vector", v_f2)

def gen_p3(nc, t, sch, midhook=None, prehook=None):
    sem = t.sem
    pA, pB = t.pA, t.pB
    for p_ in range(NPAIR):
        for sl in range(8):
            n2p = sch.val("n2")
            def v_ksq(e, p_=p_, sl=sl, n2p=n2p):
                e.wait_ge(sem["n2"], n2p)
                e.drain()
                kslc = t.kT[:, S * p_ + 512 * sl:S * p_ + 512 * (sl + 1)]
                e.tensor_tensor(t.sqs[:, 0:512], kslc, kslc, OP.mult).then_inc(sem["n1"], 1)
            n1v = sch.inc("n1")
            sch.on("vector", v_ksq)
            n2c = sch.val("n2")
            def t_kn(e, n1v=n1v, n2c=n2c):
                e.wait_ge(sem["n1"], n1v)
                e.wait_ge(sem["n2"], n2c)
                e.matmul(pB[0:1, 0:512], t.ones64[:, 0:1], t.sqs[:, 0:512], start=True, stop=True)
                e.matmul(pB[64:65, 0:512], t.ones64[:, 1:2], t.sqs[:, 0:512],
                         start=True, stop=True, tile_position=(0, 64)).then_inc(sem["n2"], 1)
            n2v = sch.inc("n2")
            sch.on("tensor", t_kn)
            def v_kmx(e, p_=p_, sl=sl, n2v=n2v):
                e.wait_ge(sem["n2"], n2v)
                e.drain()
                e.tensor_reduce(t.lnt[0:1, 0:1], pB[0:1, 0:512], AX.X, OP.max)
                e.tensor_reduce(t.lnt[64:65, 0:1], pB[64:65, 0:512], AX.X, OP.max)
                e.drain()
                if sl == 0:
                    e.tensor_copy(t.lnt[0:1, 4 + p_:5 + p_], t.lnt[0:1, 0:1])
                    e.tensor_copy(t.lnt[64:65, 4 + p_:5 + p_], t.lnt[64:65, 0:1])
                else:
                    e.tensor_tensor(t.lnt[0:1, 8:9], t.lnt[0:1, 0:1], t.lnt[0:1, 4 + p_:5 + p_], OP.max)
                    e.tensor_tensor(t.lnt[64:65, 8:9], t.lnt[64:65, 0:1], t.lnt[64:65, 4 + p_:5 + p_], OP.max)
                    e.drain()
                    e.tensor_copy(t.lnt[0:1, 4 + p_:5 + p_], t.lnt[0:1, 8:9])
                    e.tensor_copy(t.lnt[64:65, 4 + p_:5 + p_], t.lnt[64:65, 8:9])
                e.drain()
                e.nop().then_inc(sem["n2"], 1)
            sch.inc("n2")
            sch.on("vector", v_kmx)
    n2f = sch.val("n2")
    def a_ksqrt(e, n2f=n2f):
        e.wait_ge(sem["n2"], n2f)
        e.activation(t.lnt[0:1, 12:16], t.lnt[0:1, 4:8], ACT.Sqrt)
        e.activation(t.lnt[64:65, 12:16], t.lnt[64:65, 4:8], ACT.Sqrt).then_inc(sem["a2"], 1)
    a2k = sch.inc("a2")
    sch.on("scalar", a_ksqrt)
    def v_kneg(e, a2k=a2k):
        e.wait_ge(sem["a2"], a2k)
        e.tensor_scalar(t.lnt[0:1, 12:16], t.lnt[0:1, 12:16], -1.0, None, OP.mult)
        e.tensor_scalar(t.lnt[64:65, 12:16], t.lnt[64:65, 12:16], -1.0, None, OP.mult)
        e.drain()
        e.nop().then_inc(sem["n2"], 1)
    sch.inc("n2")
    sch.on("vector", v_kneg)
    prol_n2 = sch.val("n2")

    pend = {}

    def emit_qsq_qn(idx, a2cps, n2sqs):
        p_, qt = idx // NQT, idx % NQT
        def v_qsq(e, p_=p_, qt=qt, n2sqs=n2sqs, prol_n2=prol_n2):
            e.wait_ge(sem["n2"], max(prol_n2, n2sqs))
            e.drain()
            qslc = t.qT[:, S * p_ + 512 * qt:S * p_ + 512 * (qt + 1)]
            e.tensor_tensor(t.sqs[:, 0:512], qslc, qslc, OP.mult).then_inc(sem["n1"], 1)
        n1q = sch.inc("n1")
        sch.on("vector", v_qsq)
        def t_qn(e, n1q=n1q, a2cps=a2cps):
            e.wait_ge(sem["n1"], n1q)
            e.wait_ge(sem["a2"], a2cps)
            e.matmul(pB[0:1, 1024:1536], t.ones64[:, 0:1], t.sqs[:, 0:512],
                     start=True, stop=True)
            e.matmul(pB[64:65, 1024:1536], t.ones64[:, 1:2], t.sqs[:, 0:512],
                     start=True, stop=True, tile_position=(0, 64)).then_inc(sem["n2"], 1)
        n2qv = sch.inc("n2")
        sch.on("tensor", t_qn)
        pend[idx] = dict(n2sqs=n2qv)

    def emit_ncb_shift(idx, a2cps, shiftprev):
        p_ = idx // NQT
        n2qv = pend[idx]["n2sqs"]
        def a_qn(e, n2qv=n2qv, shiftprev=shiftprev):
            e.wait_ge(sem["n2"], max(n2qv, shiftprev))
            e.activation(t.rnorm[0:1, :], pB[0:1, 1024:1536], ACT.Sqrt)
            e.activation(t.rnorm[64:65, :], pB[64:65, 1024:1536], ACT.Sqrt).then_inc(sem["a2"], 1)
        a2qv = sch.inc("a2")
        sch.on("scalar", a_qn)
        def v_ncb(e, p_=p_, a2qv=a2qv):
            e.wait_ge(sem["a2"], a2qv)
            e.drain()
            e.tensor_scalar(t.rnorm[0:1, :], t.rnorm[0:1, :],
                            t.lnt[0:1, 12 + p_:13 + p_], 200.0, OP.mult, OP.add)
            e.tensor_scalar(t.rnorm[64:65, :], t.rnorm[64:65, :],
                            t.lnt[64:65, 12 + p_:13 + p_], 200.0, OP.mult, OP.add)
            e.drain()
            e.nop().then_inc(sem["n2"], 1)
        ncbv = sch.inc("n2")
        sch.on("vector", v_ncb)
        def t_shift(e, ncbv=ncbv, a2cps=a2cps):
            e.wait_ge(sem["n2"], ncbv)
            e.wait_ge(sem["a2"], a2cps)
            e.matmul(pB[:, 1024:1536], t.onesf[0:1, 0:128], t.rnorm[0:1, :],
                     start=True, stop=True, tile_position=(0, 0))
            e.matmul(pB[:, 1536:2048], t.onesf[64:65, 0:128], t.rnorm[64:65, :],
                     start=True, stop=True, tile_position=(64, 0)).then_inc(sem["n2"], 1)
        shiftv = sch.inc("n2")
        sch.on("tensor", t_shift)
        pend[idx]["shiftv"] = shiftv

    def emit_acps(idx):
        shiftv = pend[idx]["shiftv"]
        a5l = sch.val("a5")
        def a_cps(e, shiftv=shiftv, a5l=a5l):
            e.wait_ge(sem["n2"], shiftv)
            e.wait_ge(sem["a5"], a5l)
            e.activation(t.shift_sb[:, :], pB[:, 1024:2048], ACT.Identity).then_inc(sem["a2"], 1)
        pend[idx]["cpsv"] = sch.inc("a2")
        sch.on("scalar", a_cps)

    emit_qsq_qn(0, sch.val("a2"), 0)
    emit_ncb_shift(0, 0, 0)
    emit_acps(0)

    for p_ in range(NPAIR):
        for qt in range(NQT):
            KC = 4 * (qt + 1)
            idx3 = 8 * p_ + qt
            extra_sc = prehook(idx3) if prehook else []
            cpsv = pend[idx3]["cpsv"]
            n2sqs = pend[idx3]["n2sqs"]
            shiftv = pend[idx3]["shiftv"]

            a4st = sch.val("a4")
            a5st = sch.val("a5")
            a2ev = sch.val("a2")
            cxv = {}
            pend_cx = []

            def emit_cx(kc, wv, wsem, p_=p_, KC=KC, a2ev=a2ev):
                def t_cx(e, p_=p_, kc=kc, wv=wv, wsem=wsem, KC=KC, a2ev=a2ev):
                    e.wait_ge(sem[wsem], wv)
                    if kc == 0:
                        e.wait_ge(sem["a2"], a2ev)
                    first, last = (kc == 0), (kc == KC - 1)
                    src = t.pfin2[:, 1024 * (kc % 2):1024 * (kc % 2) + 1024]
                    e.matmul(pA[0:65, 1024:1536],
                             t.vA[:, 520 * kc + 130 * p_:520 * kc + 130 * p_ + 65],
                             src[:, 0:512], start=first, stop=last)
                    e.matmul(pA[0:65, 1536:2048],
                             t.vA[:, 520 * kc + 130 * p_ + 65:520 * kc + 130 * (p_ + 1)],
                             src[:, 512:1024], start=first, stop=last).then_inc(sem["a1"], 1)
                cxv[kc] = sch.inc("a1")
                sch.on("tensor", t_cx)

            for kc in range(KC):
                ps_sc = pA if kc % 2 == 0 else pB
                def t_sc(e, p_=p_, qt=qt, kc=kc, ps_sc=ps_sc,
                         bw=(sch.val("a5") - 1 if kc >= 2 else a5st),
                         ex=(extra_sc if kc == 1 else [])):
                    for smn, vv in ex:
                        e.wait_ge(sem[smn], vv)
                    e.wait_ge(sem["a5"], bw)
                    e.matmul(ps_sc[:, 0:512],
                             t.kT[0:64, S * p_ + 128 * kc:S * p_ + 128 * (kc + 1)],
                             t.qT[0:64, S * p_ + 512 * qt:S * p_ + 512 * (qt + 1)],
                             start=True, stop=True, tile_position=(0, 0))
                    e.matmul(ps_sc[:, 512:1024],
                             t.kT[64:128, S * p_ + 128 * kc:S * p_ + 128 * (kc + 1)],
                             t.qT[64:128, S * p_ + 512 * qt:S * p_ + 512 * (qt + 1)],
                             start=True, stop=True, tile_position=(64, 0)).then_inc(sem["a1"], 1)
                a1v = sch.inc("a1")
                sch.on("tensor", t_sc)
                if len(pend_cx) == 2:
                    emit_cx(*pend_cx.pop(0))
                a2p = sch.val("a2")
                prw = t.praw32 if kc % 2 == 0 else t.praw32b
                def v_add(e, kc=kc, a1v=a1v, a2p=a2p, ps_sc=ps_sc, prw=prw):
                    e.wait_ge(sem["a1"], a1v)
                    e.wait_ge(sem["a2"], a2p - 1 if kc >= 2 else a2p)
                    e.scalar_tensor_tensor(prw[:, :], ps_sc[:, 0:1024], 1.0,
                                           t.shift_sb[:, :], OP.mult,
                                           OP.add).then_inc(sem["a5"], 1)
                a5v = sch.inc("a5")
                sch.on("vector", v_add)
                pfw = cxv.get(kc - 2, None)
                def a_exp(e, kc=kc, a5v=a5v, pfw=pfw, a4st=a4st, prw=prw):
                    e.wait_ge(sem["a5"], a5v)
                    if pfw is not None:
                        e.wait_ge(sem["a1"], pfw)
                    else:
                        e.wait_ge(sem["a4"], a4st)
                    e.activation(t.pfin2[:, 1024 * (kc % 2):1024 * (kc % 2) + 1024],
                                 prw[:, :], ACT.Exp, scale=0.125).then_inc(sem["a2"], 1)
                a2v = sch.inc("a2")
                sch.on("scalar", a_exp)
                dg = kc - 4 * qt
                if dg >= 0:
                    def v_msk(e, qt=qt, kc=kc, dg=dg, a2v=a2v):
                        e.wait_ge(sem["a2"], a2v)
                        e.drain()
                        mv = t.masks[:, 512 * dg:512 * (dg + 1)]
                        dst = t.pfin2[:, 1024 * (kc % 2):1024 * (kc % 2) + 1024]
                        e.tensor_tensor(dst[:, 0:512], dst[:, 0:512], mv, OP.mult)
                        e.tensor_tensor(dst[:, 512:1024], dst[:, 512:1024], mv,
                                        OP.mult).then_inc(sem["a3"], 1)
                    a3v = sch.inc("a3")
                    sch.on("vector", v_msk)
                    pend_cx.append((kc, a3v, "a3"))
                else:
                    pend_cx.append((kc, a2v, "a2"))
                if kc == 1 and idx3 + 1 < 32:
                    emit_qsq_qn(idx3 + 1, cpsv, n2sqs)
                if kc == 2:
                    if idx3 + 1 < 32:
                        emit_ncb_shift(idx3 + 1, cpsv, shiftv)
                    if midhook:
                        midhook(idx3, 0)
                if midhook and kc == min(KC - 1, 14):
                    midhook(idx3, 1)
            for pc in pend_cx:
                emit_cx(*pc)
            if idx3 + 1 < 32:
                emit_acps(idx3 + 1)
            a1e = sch.val("a1")
            a4p = sch.val("a4")
            def a_st(e, a1e=a1e, a4p=a4p):
                e.wait_ge(sem["a1"], a1e)
                e.wait_ge(sem["a4"], a4p)
                e.copy(t.ctxstage[0:65, :], pA[0:65, 1024:2048]).then_inc(sem["a2"], 1)
            astv = sch.inc("a2")
            sch.on("scalar", a_st)
            def v_rc(e, astv=astv):
                e.wait_ge(sem["a2"], astv)
                e.drain()
                e.reciprocal(t.rnorm_e[64:65, 0:1024], t.ctxstage[64:65, :]).then_inc(sem["a4"], 1)
            vrcv = sch.inc("a4")
            sch.on("vector", v_rc)
            def t_rb(e, vrcv=vrcv):
                e.wait_ge(sem["a4"], vrcv)
                e.matmul(pA[0:64, 1024:1536], t.onesf[64:65, 0:64], t.rnorm_e[64:65, 0:512],
                         start=True, stop=True, tile_position=(64, 0))
                e.matmul(pA[0:64, 1536:2048], t.onesf[64:65, 0:64], t.rnorm_e[64:65, 512:1024],
                         start=True, stop=True, tile_position=(64, 0)).then_inc(sem["a1"], 1)
            a1r = sch.inc("a1")
            sch.on("tensor", t_rb)
            def a_ev(e, a1r=a1r, a4p=a4p):
                e.wait_ge(sem["a1"], a1r)
                e.wait_ge(sem["a4"], a4p)
                e.copy(t.rnorm_e[0:64, 0:1024], pA[0:64, 1024:2048]).then_inc(sem["a2"], 1)
            a2r = sch.inc("a2")
            sch.on("scalar", a_ev)
            def v_nm0(e, p_=p_, qt=qt, a2r=a2r):
                e.wait_ge(sem["a2"], a2r)
                e.drain()
                e.tensor_tensor(t.qT[0:64, S * p_ + 512 * qt:S * p_ + 512 * (qt + 1)],
                                t.ctxstage[0:64, 0:512], t.rnorm_e[0:64, 0:512],
                                OP.mult).then_inc(sem["a4"], 1)
            sch.inc("a4")
            sch.on("vector", v_nm0)
            def v_nm1(e, p_=p_, qt=qt, a2r=a2r):
                e.wait_ge(sem["a2"], a2r)
                e.tensor_tensor(t.pfin2[0:64, 0:512], t.ctxstage[0:64, 512:1024],
                                t.rnorm_e[0:64, 512:1024], OP.mult)
                e.drain()
                e.stream_shuffle(t.qT[64:128, S * p_ + 512 * qt:S * p_ + 512 * (qt + 1)],
                                 t.pfin2[0:64, 0:512], list(range(32))).then_inc(sem["a4"], 1)
            sch.inc("a4")
            sch.on("vector", v_nm1)

def _p4_ag_start(nc, t, sch):
    sem = t.sem
    ccd0 = sch.val("ccd")
    def g_ag1(e, ccd0=ccd0):
        for p_ in range(4):
            e.dma_start(out=t.ag_in.ap()[p_, :, :], in_=t.qT[:, S * p_:S * (p_ + 1)]).then_inc(sem["ccd"], 16)
        e.wait_ge(sem["ccd"], ccd0 + 64)
        e.collective_compute("AllGather", OP.bypass, replica_groups=[list(range(8))],
                             ins=[t.ag_in.ap()], outs=[t.ag_out.ap()]).then_inc(sem["ccs"], 1)
    sch.inc("ccd", 64)
    ccs_ag = sch.inc("ccs", 1)
    sch.on("gpsimd", g_ag1)
    return ccd0 + 64, ccs_ag


def _p4_extract(nc, t, sch, ccs_ag):
    sem = t.sem
    din = t.din
    c7g = sch.val("c7")
    stwg = sch.val("stw")
    ccdx = sch.val("ccd")
    def g_ag2(e, ccs_ag=ccs_ag, c7g=c7g, stwg=stwg, ccdx=ccdx):
        e.wait_ge(sem["ccs"], ccs_ag)
        e.wait_ge(sem["c7"], c7g)
        e.wait_ge(sem["stw"], stwg)
        my = e.value_load(t.offs[0:1, 1:2])
        pr = e.value_load(t.offs[0:1, 2:3])
        gv = e.value_load(t.offs[0:1, 3:4])
        for p_ in range(4):
            e.dma_start(out=t.ctx_own[:, TOWN * p_:TOWN * (p_ + 1)],
                        in_=t.ag_out.ap()[bass.ts(my, 1), p_, :, bass.ts(gv, TOWN)]).then_inc(sem["ccd"], 16)
        for p_ in range(4):
            e.dma_start(out=t.ctx_own[:, TOWN * (4 + p_):TOWN * (5 + p_)],
                        in_=t.ag_out.ap()[bass.ts(pr, 1), p_, :, bass.ts(gv, TOWN)]).then_inc(sem["ccd"], 16)
        e.dma_start(out=t.wprojT[:, :].rearrange("p (c f) -> p c f", c=8),
                    in_=din["wprojT"].rearrange("(c p) f -> p c f", p=128)).then_inc(sem["ldw"], 16)
        e.wait_ge(sem["ccd"], ccdx + 128)
        e.nop().then_inc(sem["c4"], 1)
    sch.inc("ccd", 128)
    sch.inc("ldw", 16)
    agv = sch.inc("c4")
    sch.on("gpsimd", g_ag2)
    return agv


def _p4_proj(nc, t, sch, agv):
    sem = t.sem
    pA, pB = t.pA, t.pB
    ldwv = sch.val("ldw")
    for tt in range(16):
        par = tt % 2
        evp = sch.val("c5")
        def t_pj(e, tt=tt, par=par, evp=evp, agv=agv, ldwv=ldwv):
            e.wait_ge(sem["c4"], agv)
            e.wait_ge(sem["ldw"], ldwv)
            e.wait_ge(sem["c5"], evp)
            for fc in range(8):
                lhs = t.ctx_own[:, TOWN * fc + 128 * tt:TOWN * fc + 128 * (tt + 1)]
                e.ldweights(lhs)
                for hf in range(2):
                    mm = e.matmul(pA[:, 1024 * par + 512 * hf:1024 * par + 512 * (hf + 1)], lhs,
                                  t.wprojT[:, 1024 * fc + 512 * hf:1024 * fc + 512 * (hf + 1)],
                                  start=(fc == 0), stop=(fc == 7))
            mm.then_inc(sem["c8"], 1)
        c8v = sch.inc("c8")
        sch.on("tensor", t_pj)
        def v_pj(e, tt=tt, par=par, c8v=c8v):
            e.wait_ge(sem["c8"], c8v)
            dst = (t.aout_a if tt < 8 else t.aout_b)[:, D * (tt % 8):D * (tt % 8 + 1)]
            e.scalar_tensor_tensor(dst, pA[:, 1024 * par:1024 * (par + 1)], 1.0,
                                   t.btile[:, :], OP.mult, OP.add).then_inc(sem["c5"], 1)
        sch.inc("c5")
        sch.on("vector", v_pj)

def _p5_finmax(nc, t, sch, dstcol, a2g=None):
    sem = t.sem
    pB = t.pB
    c6v = sch.val("c6")
    def v_f(e, c6v=c6v):
        e.wait_ge(sem["c6"], c6v)
        e.drain()
        e.tensor_reduce(t.small[:, 14:15], t.mxcol[:, 0:24], AX.X, OP.max).then_inc(sem["n1"], 1)
    n1v = sch.inc("n1")
    sch.on("vector", v_f)
    n2p = sch.val("n2")
    def t_f(e, n1v=n1v, n2p=n2p, a2g=a2g):
        e.wait_ge(sem["n1"], n1v)
        e.wait_ge(sem["n2"], n2p)
        if a2g is not None:
            e.wait_ge(sem["a2"], a2g)
        e.transpose(pB[0:1, 8:136], t.small[:, 14:15], t.ident[:, :]).then_inc(sem["n2"], 1)
    n2v = sch.inc("n2")
    sch.on("tensor", t_f)
    def v_f2(e, n2v=n2v, dstcol=dstcol):
        e.wait_ge(sem["n2"], n2v)
        e.tensor_reduce(t.small[0:1, dstcol:dstcol + 1], pB[0:1, 8:136], AX.X, OP.max).then_inc(sem["c6"], 1)
    sch.inc("c6")
    sch.on("vector", v_f2)


def _p5_ar(nc, t, sch, idx, srccols, scale_fn, bccols):
    sem = t.sem
    pB = t.pB
    ccd1 = sch.inc("ccd", 16)
    ccs1 = sch.inc("ccs", 1)
    ccd2 = sch.inc("ccd", 16)
    c6f = sch.val("c6")
    arv = sch.inc("c1")
    def g_ar(e, idx=idx, ccd1=ccd1, ccs1=ccs1, ccd2=ccd2, c6f=c6f):
        e.wait_ge(sem["c6"], c6f)
        e.dma_start(out=t.ar_in[idx].ap()[0:1, 0:8], in_=t.small[0:1, 8:16]).then_inc(sem["ccd"], 16)
        e.wait_ge(sem["ccd"], ccd1)
        e.collective_compute("AllReduce", OP.max, replica_groups=[list(range(8))],
                             ins=[t.ar_in[idx].ap()], outs=[t.ar_out[idx].ap()]).then_inc(sem["ccs"], 1)
        e.wait_ge(sem["ccs"], ccs1)
        e.dma_start(out=t.small[0:1, 8:16], in_=t.ar_out[idx].ap()[0:1, 0:8]).then_inc(sem["ccd"], 16)
        e.wait_ge(sem["ccd"], ccd2)
        e.nop().then_inc(sem["c1"], 1)
    sch.on("gpsimd", g_ar)
    def v_s(e, arv=arv, scale_fn=scale_fn):
        e.wait_ge(sem["c1"], arv)
        e.drain()
        scale_fn(e)
    c2v = sch.inc("c2")
    sch.on("vector", v_s)
    n, d0 = bccols
    def t_bc(e, c2v=c2v, n=n):
        e.wait_ge(sem["c2"], c2v)
        e.matmul(pB[:, 0:n], t.onesf[0:1, :], t.small[0:1, 0:n], start=True, stop=True).then_inc(sem["c1"], 1)
    c1v = sch.inc("c1")
    sch.on("tensor", t_bc)
    def v_bc(e, c1v=c1v, n=n, d0=d0):
        e.wait_ge(sem["c1"], c1v)
        e.tensor_copy(t.scl[:, d0:d0 + n], pB[:, 0:n]).then_inc(sem["c2"], 1)
    sch.inc("c2")
    sch.on("vector", v_bc)


def _p5_sc3(t):
    CM = CMAGIC
    def sc3(e):
        e.tensor_scalar(t.small[0:1, 0:1], t.small[0:1, 13:14], float(2.0 ** -16 / 32767.0),
                        1e-12, OP.mult, OP.max)
        e.drain()
        e.reciprocal(t.small[0:1, 2:3], t.small[0:1, 0:1])
        e.drain()
        e.tensor_scalar(t.small[0:1, 1:2], t.small[0:1, 2:3], float(2.0 ** -16), None,
                        OP.mult).then_inc(t.sem["c2"], 1)
    return sc3


def _p5_sc4(t):
    def sc4(e):
        e.tensor_scalar(t.small[0:1, 0:1], t.small[0:1, 13:14], 1.0 / 32767.0, 1e-12,
                        OP.mult, OP.max)
        e.drain()
        e.reciprocal(t.small[0:1, 1:2], t.small[0:1, 0:1]).then_inc(t.sem["c2"], 1)
    return sc4


def _p5_sclderiv(nc, t, sch):
    sem = t.sem
    def v_c(e):
        e.drain()
        e.tensor_scalar(t.scl[:, 13:14], t.scl[:, 5:6], -CMAGIC, None, OP.mult)
        e.drain()
        e.tensor_scalar(t.scl[:, 14:15], t.scl[:, 4:5], -CMAGIC, None, OP.mult)
        e.drain()
        e.nop().then_inc(sem["c2"], 1)
    v = sch.inc("c2")
    sch.on("vector", v_c)
    return v


def _p5a_chunk(nc, t, sch, ic, c2c):
    sem = t.sem
    s5 = t.s5
    lsem = ("str", "trs")[ic % 2]
    c7p = sch.val("c7")
    stv = sch.val("stw")
    def s_ld(e, ic=ic, c7p=c7p, stv=stv, lsem=lsem):
        e.wait_ge(sem["c7"], max(0, c7p - 1))
        if ic >= 2:
            e.wait_ge(sem["stw"], stv - 16)
        e.dma_start(out=s5[4 + ic % 2][:, :], in_=t.stash_gate.ap()[ic, :, :]).then_inc(sem[lsem], 16)
    ldv = sch.inc(lsem, 16)
    sch.on("sync", s_ld)
    c7w = sch.val("c7")
    c9w = sch.val("c9")
    def a_blk(e, ic=ic, ldv=ldv, lsem=lsem, c7w=c7w, c9w=c9w, c2c=c2c):
        e.wait_ge(sem[lsem], ldv)
        e.wait_ge(sem["c7"], c7w)
        e.wait_ge(sem["c9"], c9w)
        e.wait_ge(sem["c2"], c2c)
        e.activation(s5[6][:, :], s5[4 + ic % 2][:, :], ACT.Identity,
                     bias=t.scl[:, 12:13], scale=t.scl[:, 7:8])
        e.drain()
        e.activation(s5[7][:, :], s5[6][:, :], ACT.Identity,
                     bias=t.scl[:, 13:14], scale=t.scl[:, 5:6])
        e.drain()
        e.activation(t.slotA[:, :], s5[7][:, :], ACT.Sigmoid)
        e.drain()
        e.activation(t.slotB[:, :], t.slotA[:, :], ACT.Identity,
                     bias=t.scl[:, 12:13], scale=65536.0).then_inc(sem["c1"], 1)
    c1v = sch.inc("c1")
    sch.on("scalar", a_blk)
    def v_blk(e, ic=ic, c1v=c1v):
        e.wait_ge(sem["c1"], c1v)
        e.drain()
        e.scalar_tensor_tensor(t.slotA[:, :], t.slotB[:, :], CMAGIC + 65535.0,
                               t.scl[:, 12:13].broadcast_to((128, 2048)),
                               OP.min, OP.subtract).then_inc(sem["c9"], 1)
        e.drain()
        e.tensor_tensor(s5[4 + ic % 2][:, :], s5[7][:, :], t.slotA[:, :],
                        OP.mult).then_inc(sem["c7"], 1)
    sch.inc("c9")
    c7v = sch.inc("c7")
    sch.on("vector", v_blk)
    def v_2r(e, ic=ic, c7v=c7v):
        e.wait_ge(sem["c7"], c7v)
        e.drain()
        e.tensor_reduce(t.mxcol[:, ic:ic + 1], s5[4 + ic % 2][:, :], AX.X, OP.max,
                        apply_absolute_value=True).then_inc(sem["c6"], 1)
    c6v = sch.inc("c6")
    sch.on("vector", v_2r)
    stq = sch.val("stw")
    def g_st(e, ic=ic, c6v=c6v, stq=stq):
        e.wait_ge(sem["c6"], c6v)
        e.wait_ge(sem["stw"], stq)
        e.dma_start(out=t.stash_g2.ap()[ic, :, :], in_=s5[4 + ic % 2][:, :]).then_inc(sem["stw"], 16)
    sch.inc("stw", 16)
    sch.on("gpsimd", g_st)


def _p5b_chunk(nc, t, sch, ic, c2c):
    sem = t.sem
    s5 = t.s5
    upb = (t.slotA, t.slotB)[ic % 2]
    lsem = ("str", "trs")[ic % 2]
    c7p = sch.val("c7")
    stv = sch.val("stw")
    lag = 1 if ic >= 2 else 0
    def s_ld(e, ic=ic, c7p=c7p, stv=stv, lsem=lsem, lag=lag, upb=upb):
        e.wait_ge(sem["c7"], max(0, c7p - lag))
        e.wait_ge(sem["stw"], max(0, stv - 16))
        e.dma_start(out=s5[4 + ic % 2][:, :], in_=t.stash_g2.ap()[ic, :, :]).then_inc(sem[lsem], 16)
        e.dma_start(out=upb[:, :], in_=t.stash_up.ap()[ic, :, :]).then_inc(sem[lsem], 16)
    ldv = sch.inc(lsem, 32)
    sch.on("sync", s_ld)
    c7w = sch.val("c7")
    def a_blk(e, ic=ic, ldv=ldv, lsem=lsem, c7w=c7w, c2c=c2c, upb=upb):
        e.wait_ge(sem[lsem], ldv)
        e.wait_ge(sem["c7"], c7w)
        e.wait_ge(sem["c2"], c2c)
        e.activation(s5[6][:, :], s5[4 + ic % 2][:, :], ACT.Identity,
                     bias=t.scl[:, 12:13], scale=t.scl[:, 9:10])
        e.drain()
        e.activation(s5[7][:, :], s5[6][:, :], ACT.Identity,
                     bias=t.scl[:, 15:16], scale=t.scl[:, 8:9])
        e.drain()
        e.activation(s5[6][:, :], upb[:, :], ACT.Identity,
                     bias=t.scl[:, 12:13], scale=t.scl[:, 6:7])
        e.drain()
        e.activation(upb[:, :], s5[6][:, :], ACT.Identity,
                     bias=t.scl[:, 14:15], scale=t.scl[:, 4:5]).then_inc(sem["c1"], 1)
    c1v = sch.inc("c1")
    sch.on("scalar", a_blk)
    def v_blk(e, ic=ic, c1v=c1v, upb=upb):
        e.wait_ge(sem["c1"], c1v)
        e.drain()
        e.tensor_tensor(s5[4 + ic % 2][:, :], s5[7][:, :], upb[:, :],
                        OP.mult).then_inc(sem["c7"], 1)
    c7v = sch.inc("c7")
    sch.on("vector", v_blk)
    def v_2r(e, ic=ic, c7v=c7v):
        e.wait_ge(sem["c7"], c7v)
        e.drain()
        e.tensor_reduce(t.mxcol[:, ic:ic + 1], s5[4 + ic % 2][:, :], AX.X, OP.max,
                        apply_absolute_value=True).then_inc(sem["c6"], 1)
    c6v = sch.inc("c6")
    sch.on("vector", v_2r)
    stq = sch.val("stw")
    def g_st(e, ic=ic, c6v=c6v, stq=stq):
        e.wait_ge(sem["c6"], c6v)
        e.wait_ge(sem["stw"], stq)
        e.dma_start(out=t.stash_o.ap()[ic, :, :], in_=s5[4 + ic % 2][:, :]).then_inc(sem["stw"], 16)
    sch.inc("stw", 16)
    sch.on("gpsimd", g_st)


def _p5_arwin(nc, t, sch, idx, scale_fn, bccols, post_fn=None):
    sem = t.sem
    a2g = sch.val("a2")
    _p5_finmax(nc, t, sch, 13, a2g)
    _p5_ar(nc, t, sch, idx, None, scale_fn, bccols)
    if post_fn is not None:
        sch.on("vector", post_fn)
    return [("c2", sch.val("c2"))]


def _p5_dequant(nc, t, sch, agccd, c8proj):
    sem = t.sem
    din = t.din
    pA, pB = t.pA, t.pB
    s5 = t.s5
    ldp = sch.val("ldx")
    stwp = sch.val("stw")
    c7b = sch.val("c7")
    def s_ind(e, ldp=ldp, stwp=stwp, c7b=c7b, agccd=agccd):
        e.wait_ge(sem["c7"], c7b)
        e.wait_ge(sem["stw"], stwp)
        e.wait_ge(sem["ccd"], agccd)
        e.dma_start(out=t.ind96[:, :], in_=din["ind96"]).then_inc(sem["ldx"], 16)
    ldxv = sch.inc("ldx", 16)
    sch.on("sync", s_ind)
    n2p0 = sch.val("n2")
    def v_rtd(e, n2p0=n2p0):
        e.drain()
        e.reciprocal(t.rTd[:, :], t.sTd[:, :]).then_inc(sem["c6"], 1)
    c6rtd = sch.inc("c6")
    sch.on("vector", v_rtd)
    for ic in range(NIC):
        c7p = sch.val("c7")
        def s_ld(e, ic=ic, c7p=c7p, c8proj=c8proj):
            e.wait_ge(sem["c7"], max(0, c7p - 1))
            if ic < 2:
                e.wait_ge(sem["c8"], c8proj)
            e.dma_start(out=s5[4 + ic % 2][:, 0:1024], in_=din["wdownT"][128 * ic:128 * (ic + 1), :]
                        ).then_inc(sem[("str", "trs")[ic % 2]], 16)
        ldv = sch.inc(("str", "trs")[ic % 2], 16)
        sch.on("sync", s_ld)
        n2p = sch.val("n2")
        def t_bc(e, ic=ic, n2p=n2p, ldxv=ldxv, c6rtd=c6rtd):
            e.wait_ge(sem["n2"], n2p)
            e.wait_ge(sem["ldx"], ldxv)
            e.wait_ge(sem["c6"], c6rtd)
            e.matmul(pB[:, 0:512], t.ind96[:, 128 * ic:128 * (ic + 1)], t.rTd[:, 0:512],
                     start=True, stop=True)
            e.matmul(pB[:, 512:1024], t.ind96[:, 128 * ic:128 * (ic + 1)], t.rTd[:, 512:1024],
                     start=True, stop=True)
            e.matmul(pB[:, 1024:1536], t.ind96[:, 128 * ic:128 * (ic + 1)], t.sTd[:, 0:512],
                     start=True, stop=True)
            e.matmul(pB[:, 1536:2048], t.ind96[:, 128 * ic:128 * (ic + 1)], t.sTd[:, 512:1024],
                     start=True, stop=True).then_inc(sem["n1"], 1)
        n1v = sch.inc("n1")
        sch.on("tensor", t_bc)
        stp = sch.val("stw")
        def v_dq(e, ic=ic, ldv=ldv, n1v=n1v, stp=stp):
            e.wait_ge(sem[("str", "trs")[ic % 2]], ldv)
            e.wait_ge(sem["n1"], n1v)
            e.wait_ge(sem["stw"], stp)
            e.drain()
            e.tensor_tensor(s5[6][:, 0:1024], s5[4 + ic % 2][:, 0:1024], pB[:, 0:1024], OP.mult)
            e.drain()
            e.tensor_scalar(s5[6][:, 0:1024], s5[6][:, 0:1024], CMAGIC, CMAGIC, OP.add, OP.subtract)
            e.drain()
            e.tensor_tensor(t.wdq_s[:, 1024 * (ic % 2):1024 * (ic % 2 + 1)], s5[6][:, 0:1024],
                            pB[:, 1024:2048], OP.mult).then_inc(sem["n2"], 1)
            e.drain()
            e.nop().then_inc(sem["c7"], 1)
        sch.inc("n2")
        sch.inc("c7")
        sch.on("vector", v_dq)
        def g_st(e, ic=ic):
            pass
        n2v2 = sch.val("n2")
        def g_st2(e, ic=ic, n2v2=n2v2):
            e.wait_ge(sem["n2"], n2v2)
            e.dma_start(out=t.stash_wdq.ap()[ic, :, :],
                        in_=t.wdq_s[:, 1024 * (ic % 2):1024 * (ic % 2 + 1)]).then_inc(sem["stw"], 16)
        sch.inc("stw", 16)
        sch.on("gpsimd", g_st2)


def _p5_main(nc, t, sch, agccd, c7p5b):
    sem = t.sem
    din = t.din
    pA, pB = t.pA, t.pB
    s5 = t.s5
    stwdq = sch.val("stw")
    oqv_map = {}
    aoq_map = {}
    dn_map = {}

    def emit_oq(tqv, ic):
        lsem = ("lda", "ldb")[ic % 2]
        a1p = aoq_map.get((tqv, ic - 2) if ic >= 2 else (tqv - 1, ic + 22), 0)
        def s_lo(e, tqv=tqv, ic=ic, lsem=lsem, a1p=a1p):
            e.wait_ge(sem["c1"], a1p)
            if tqv == 0 and ic < 2:
                e.wait_ge(sem["c7"], c7p5b)
            e.dma_start(out=t.slotA[:, 512 * (ic % 2):512 * (ic % 2 + 1)],
                        in_=t.stash_o.ap()[ic, :, 512 * tqv:512 * (tqv + 1)]
                        ).then_inc(sem[lsem], 16)
        ldv = sch.inc(lsem, 16)
        sch.on("sync", s_lo)
        c2q = oqv_map.get((tqv, ic - 2) if ic >= 2 else (tqv - 1, ic + 22), 0)
        def a_oq(e, tqv=tqv, ic=ic, ldv=ldv, lsem=lsem, c2q=c2q):
            e.wait_ge(sem[lsem], ldv)
            e.wait_ge(sem["c2"], c2q)
            e.activation(t.slotB[:, 512 * (ic % 2):512 * (ic % 2 + 1)],
                         t.slotA[:, 512 * (ic % 2):512 * (ic % 2 + 1)], ACT.Identity,
                         bias=t.scl[:, 12:13], scale=t.scl[:, 11:12]).then_inc(sem["c1"], 1)
        aoq_map[(tqv, ic)] = sch.inc("c1")
        sch.on("scalar", a_oq)
        dnw = dn_map.get((tqv - 1, ic), 0)
        def v_oq(e, tqv=tqv, ic=ic, dnw=dnw, c1v=aoq_map[(tqv, ic)], agccd=agccd):
            e.wait_ge(sem["c1"], c1v)
            e.wait_ge(sem["c8"], dnw)
            if tqv == 0 and ic == 0:
                e.wait_ge(sem["ccd"], agccd)
            e.drain()
            e.tensor_scalar(t.oqbuf[:, 512 * ic:512 * (ic + 1)],
                            t.slotB[:, 512 * (ic % 2):512 * (ic % 2 + 1)], CMAGIC,
                            t.scl[:, 10:11], OP.subtract, OP.mult).then_inc(sem["c2"], 1)
        oqv_map[(tqv, ic)] = sch.inc("c2")
        sch.on("vector", v_oq)

    for ic in range(NIC):
        emit_oq(0, ic)

    for tqv in range(4):
        evp = sch.val("c5")
        c5q = sch.val("c5")
        for ic in range(NIC):
            c8p = sch.val("c8")
            def s_ldw(e, ic=ic, tqv=tqv, c8p=c8p, stwdq=stwdq, c5q=c5q):
                e.wait_ge(sem["c8"], max(0, c8p - 1))
                if ic < 2:
                    if tqv == 0:
                        e.wait_ge(sem["stw"], stwdq)
                    else:
                        e.wait_ge(sem["c5"], c5q)
                e.dma_start(out=t.wdq_s[:, 1024 * (ic % 2):1024 * (ic % 2 + 1)],
                            in_=t.stash_wdq.ap()[ic, :, :]).then_inc(sem[("str", "trs")[ic % 2]], 16)
            ldv = sch.inc(("str", "trs")[ic % 2], 16)
            sch.on("sync", s_ldw)
            def t_dn(e, ic=ic, ldv=ldv, oqw=oqv_map[(tqv, ic)], evp=evp):
                e.wait_ge(sem[("str", "trs")[ic % 2]], ldv)
                e.wait_ge(sem["c2"], oqw)
                if ic == 0:
                    e.wait_ge(sem["c5"], evp)
                mm = None
                for ts_ in range(4):
                    lhs = t.oqbuf[:, 512 * ic + 128 * ts_:512 * ic + 128 * (ts_ + 1)]
                    e.ldweights(lhs)
                    ps = (pA if ts_ < 2 else pB)
                    for hf in range(2):
                        mm = e.matmul(ps[:, 1024 * (ts_ % 2) + 512 * hf:1024 * (ts_ % 2) + 512 * (hf + 1)],
                                      lhs, t.wdq_s[:, 1024 * (ic % 2) + 512 * hf:1024 * (ic % 2) + 512 * (hf + 1)],
                                      start=(ic == 0), stop=(ic == NIC - 1))
                mm.then_inc(sem["c8"], 1)
            dn_map[(tqv, ic)] = sch.inc("c8")
            sch.on("tensor", t_dn)
            if tqv < 3:
                emit_oq(tqv + 1, ic)
        dnv = sch.val("c8")
        for ts_ in range(4):
            tt = 4 * tqv + ts_
            stop_ = sch.val("sto")
            c5g = sch.val("c5")
            def s_lx(e, tt=tt, stop_=stop_, dnv=dnv, c5g=c5g):
                e.wait_ge(sem["sto"], max(0, stop_ - 32))
                e.wait_ge(sem["c8"], dnv)
                e.wait_ge(sem["c5"], c5g)
                e.dma_start(out=s5[7][:, 0:1024], in_=din["x_own"][128 * tt:128 * (tt + 1), :]
                            ).then_inc(sem["str"], 16)
            ldv = sch.inc("str", 16)
            sch.on("sync", s_lx)
            stov = sch.val("sto")
            def v_ev(e, ts_=ts_, tt=tt, ldv=ldv, dnv=dnv, stov=stov):
                e.wait_ge(sem["c8"], dnv)
                e.wait_ge(sem["str"], ldv)
                e.wait_ge(sem["sto"], stov)
                e.drain()
                ps = (pA if ts_ < 2 else pB)
                src = ps[:, 1024 * (ts_ % 2):1024 * (ts_ % 2 + 1)]
                at = (t.aout_a if tt < 8 else t.aout_b)[:, D * (tt % 8):D * (tt % 8 + 1)]
                e.scalar_tensor_tensor(s5[6][:, 0:1024], src, 1.0, at, OP.mult, OP.add)
                e.drain()
                e.tensor_tensor(s5[6][:, 1024:2048], s5[6][:, 0:1024], s5[7][:, 0:1024],
                                OP.add).then_inc(sem["c5"], 1)
            c5v = sch.inc("c5")
            sch.on("vector", v_ev)
            def g_out(e, tt=tt, c5v=c5v):
                e.wait_ge(sem["c5"], c5v)
                e.dma_start(out=t.out_own[128 * tt:128 * (tt + 1), :],
                            in_=s5[6][:, 1024:2048]).then_inc(sem["sto"], 16)
            sch.inc("sto", 16)
            sch.on("gpsimd", g_out)
    stof = sch.val("sto")
    def g_fin(e, stof=stof):
        e.wait_ge(sem["sto"], stof)
    sch.on("gpsimd", g_fin)


def gen_dbg(nc, t, sch):
    sem = t.sem
    items = []
    if "kT" in t.debug: items.append((t.dout["kT"], t.kT[:, :]))
    if "qT" in t.debug: items.append((t.dout["qT"], t.qT[:, :]))
    if "vA" in t.debug: items.append((t.dout["vA"], t.vA[:, :]))
    if "xq" in t.debug: items.append((t.dout["xq"], t.xq_fm[:, :]))
    if "scl" in t.debug: items.append((t.dout["scl"], t.scl[:, :]))
    if "small" in t.debug: items.append((t.dout["small"], t.small[:, :]))
    if "ctxT" in t.debug: items.append((t.dout["ctxT"], t.qT[:, :]))
    if "ctxown" in t.debug: items.append((t.dout["ctxown"], t.ctx_own[:, :]))
    def g_dump(e):
        for dst, src in items:
            e.dma_start(out=dst, in_=src).then_inc(sem["sto"], 16)
        for nm in ("upst", "gatest", "g2st", "ost", "wdqst"):
            if nm in t.debug:
                stash = {"upst": t.stash_up, "gatest": t.stash_gate, "g2st": t.stash_g2,
                         "ost": t.stash_o, "wdqst": t.stash_wdq}[nm]
                e.dma_start(out=t.dout[nm], in_=stash.ap()).then_inc(sem["sto"], 16)
        if "aout" in t.debug:
            for i in range(16):
                src = (t.aout_a if i < 8 else t.aout_b)[:, D * (i % 8):D * (i % 8 + 1)]
                e.dma_start(out=t.dout["aout"][128 * i:128 * (i + 1), :], in_=src).then_inc(sem["sto"], 16)
        e.wait_ge(sem["sto"], stv)
    nst = len(items) + sum(1 for nm in ("upst", "gatest", "g2st", "ost", "wdqst") if nm in t.debug)
    if "aout" in t.debug:
        nst += 16
    stv = sch.inc("sto", 16 * nst)
    sch.on("gpsimd", g_dump)


def prep_inputs(L0_attn_x, L0_attn_w_kqv, L0_attn_w_proj, L0_attn_b_proj,
                L0_ffn_w_gate, L0_ffn_w_up, L0_ffn_w_down):
    x = np.ascontiguousarray(np.asarray(L0_attn_x, dtype=np.float32))
    wkqv = np.asarray(L0_attn_w_kqv, dtype=np.float32).reshape(16, 64, 3, D)
    wproj = np.asarray(L0_attn_w_proj, dtype=np.float32)
    bproj = np.asarray(L0_attn_b_proj, dtype=np.float32)
    wup = np.ascontiguousarray(np.asarray(L0_ffn_w_up, dtype=np.float32))
    wgate = np.ascontiguousarray(np.asarray(L0_ffn_w_gate, dtype=np.float32))
    wdown = np.ascontiguousarray(np.asarray(L0_ffn_w_down, dtype=np.float32))
    masks, ident, ones64, ind32, ind96, onesf = make_consts()

    wupT = np.ascontiguousarray(wup.T)
    wgateT = np.ascontiguousarray(wgate.T)
    wdownT = np.ascontiguousarray(wdown.T)

    in_maps = []
    for c in range(8):
        b, g = c // 2, c % 2
        wk = wkqv[8 * g:8 * (g + 1), :, 0, :].reshape(512, D)
        wq = wkqv[8 * g:8 * (g + 1), :, 1, :].reshape(512, D)
        wv = wkqv[8 * g:8 * (g + 1), :, 2, :].reshape(512, D)
        wprojT_c = np.concatenate([wproj[:, 512 * g:512 * (g + 1)].T,
                                   wproj[:, 512 * (1 - g):512 * (2 - g)].T], axis=0)
        in_maps.append({
            "x_seq": x[b],
            "x_own": np.ascontiguousarray(x[b, TOWN * g:TOWN * (g + 1), :]),
            "wkT": np.ascontiguousarray(wk.T), "wqT": np.ascontiguousarray(wq.T),
            "wvT": np.ascontiguousarray(wv.T),
            "wprojT": np.ascontiguousarray(wprojT_c),
            "b_row": bproj.reshape(1, D),
            "wupN": wup, "wupT": wupT, "wgateN": wgate, "wgateT": wgateT,
            "wdownN": wdown, "wdownT": wdownT,
            "masks": masks, "ident": ident, "ones64": ones64, "ind32": ind32, "ind96": ind96, "onesf": onesf,
            "offs": np.array([[16 * g, c, c ^ 1, g,
                               2 * g, 2 * g + 1, 4 + 2 * g, 5 + 2 * g]], dtype=np.int32),
        })
    return in_maps


def kernel(**inputs):
    in_maps = prep_inputs(**inputs)
    nc, t = build()
    res = run_bass_kernel_spmd(nc, in_maps, core_ids=list(range(8)))
    out = np.empty((4, S, D), dtype=np.float32)
    for c in range(8):
        b, g = c // 2, c % 2
        out[b, TOWN * g:TOWN * (g + 1), :] = res.results[c]["out_own"]
    return out
